# version 06
# baseline (speedup 1.0000x reference)
import numpy as np
from contextlib import ExitStack

import bass_rust as _bass_rust
import concourse.bass as bass
import concourse.mybir as mybir
from concourse.bass import IndirectOffsetOnAxis
from concourse.bass_utils import run_bass_kernel_spmd
from concourse import library_config as _libcfg

F32 = mybir.dt.float32
BF16 = mybir.dt.bfloat16
U32 = mybir.dt.uint32
I16 = mybir.dt.int16
AF = mybir.ActivationFunctionType
ALU = mybir.AluOpType
AXX = mybir.AxisListType.X


def _lower_ext(nc):
    m = {}
    for lib in _libcfg.all_libraries:
        for it in lib.instructions:
            m[it] = m.get(it, 0) | (1 << lib.index)
    _bass_rust.insert_library_loads(nc, m, len(_libcfg.all_libraries), _libcfg.standard.index)
    mybir.codegen_inst_isa_subclasses(nc)
    return nc

B, S, D, NH, DK = 4, 4096, 1024, 16, 64
TOK = 2048
NT = TOK // 128
NC = 8
E = 8
I0 = 2048
C0 = 384
HID1 = 4096
K1 = 2048
LN_EPS = 1e-5


class Sem:
    def __init__(self, es, nc, name):
        self.h = es.enter_context(nc.semaphore(name))
        self.n = 0

    def inc(self, inst, k=1):
        inst.then_inc(self.h, k)
        self.n += k
        return self.n


class P:
    def __init__(self, nc, es, block=None, pfx=""):
        self.nc, self.es, self.block = nc, es, block
        self.pfx = pfx
        self._sems = {}
        for nm in ["PE", "VE", "AC", "PL", "CC"]:
            setattr(self, nm, self.S(nm))

    def bind(self, block):
        self.block = block
        return self

    def S(self, nm):
        if nm not in self._sems:
            self._sems[nm] = Sem(self.es, self.nc, self.pfx + nm)
        return self._sems[nm]

    def sync(self, fn): self.block.sync(fn)
    def gps(self, fn): self.block.gpsimd(fn)
    def vec(self, fn): self.block.vector(fn)
    def sca(self, fn): self.block.scalar(fn)
    def ten(self, fn): self.block.tensor(fn)


def emit_ln(p, xs, xn_out, scr, out_waits=(), eps=LN_EPS):
    VE, AC = p.VE, p.AC
    Dfree = 1024
    ac_entry = AC.n

    def v1(v):
        v.wait_ge(AC.h, ac_entry)
        v.reduce_sum(out=scr["musum"][:], in_=xs, axis=AXX)
        v.tensor_tensor(out=scr["sq"][:], in0=xs, in1=xs, op=ALU.mult)
        v.tensor_scalar(out=scr["negmu"][:], in0=scr["musum"][:], scalar1=-1.0 / Dfree,
                        scalar2=None, op0=ALU.mult)
        v.reduce_sum(out=scr["ssq"][:], in_=scr["sq"][:], axis=AXX)
        v.tensor_scalar(out=scr["d"][:], in0=xs, scalar1=scr["negmu"][:, :],
                        scalar2=None, op0=ALU.add)
        v.tensor_scalar(out=scr["ex2"][:], in0=scr["ssq"][:], scalar1=1.0 / Dfree,
                        scalar2=float(eps), op0=ALU.mult, op1=ALU.add)
        v.tensor_tensor(out=scr["musq"][:], in0=scr["negmu"][:], in1=scr["negmu"][:],
                        op=ALU.mult)
        v.tensor_copy(out=scr["gap"][:], in_=scr["musum"][:])
        VE.inc(v.tensor_tensor(out=scr["veps"][:], in0=scr["ex2"][:], in1=scr["musq"][:],
                               op=ALU.subtract))
    p.vec(v1)
    m1 = VE.n

    def s2(s):
        s.wait_ge(VE.h, m1)
        AC.inc(s.activation(out=scr["sd"][:], in_=scr["veps"][:], func=AF.Sqrt))
    p.sca(s2)
    m4 = AC.n

    def v3(v):
        v.wait_ge(AC.h, m4)
        for sem, cnt in out_waits:
            v.wait_ge(sem.h, cnt)
        v.tensor_copy(out=scr["gap"][:], in_=scr["gap"][:])
        v.reciprocal(out=scr["rstd"][:], in_=scr["sd"][:])
        v.tensor_copy(out=scr["gap"][:], in_=scr["gap"][:])
        VE.inc(v.tensor_scalar(out=xn_out, in0=scr["d"][:], scalar1=scr["rstd"][:, :],
                               scalar2=None, op0=ALU.mult))
    p.vec(v3)
    return VE.n


def emit_wrap_idx(p, src16, slotT, w32, w16, ident_f, ps1, ps2, RW, ve_src):
    PE, VE = p.PE, p.VE

    def t1(t, m=ve_src):
        t.wait_ge(VE.h, m)
        PE.inc(t.transpose(out=ps1[0:16, 0:128], in_=src16, identity=ident_f[:]))
    p.ten(t1)
    pe1 = PE.n

    def v1(v, pe1=pe1):
        v.wait_ge(PE.h, pe1)
        VE.inc(v.tensor_copy(out=slotT, in_=ps1[0:16, 0:128]))
    p.vec(v1)
    ve1 = VE.n

    def t2(t, ve1=ve1):
        t.wait_ge(VE.h, ve1)
        for k in range(8):
            PE.inc(t.transpose(out=ps2[0:16, k * 16:(k + 1) * 16],
                               in_=slotT[:, 16 * k:16 * (k + 1)],
                               identity=ident_f[0:16, 0:16]))
    p.ten(t2)
    pe2 = PE.n

    def v2(v, pe2=pe2):
        v.wait_ge(PE.h, pe2)
        w32v = w32.rearrange("r (j k) -> r j k", k=8)
        for k in range(8):
            v.tensor_copy(out=w32v[:, :, k], in_=ps2[0:16, k * 16:(k + 1) * 16])
        v.tensor_copy(out=slotT, in_=slotT)
        v.tensor_copy(out=slotT, in_=slotT)
        VE.inc(v.tensor_copy(out=w16[0:16, :], in_=w32))
    p.vec(v2)
    ve2 = VE.n

    def s1(sy, ve2=ve2):
        sy.wait_ge(VE.h, ve2)
        for mrep in range(1, 8):
            RW.inc(sy.dma_start(out=w16[16 * mrep:16 * (mrep + 1), :],
                                in_=w16[0:16, :]), 16)
    p.sync(s1)
    return RW.n



def build_A(nc=None, ext=None, entry=()):
    standalone = nc is None
    if standalone:
        nc = bass.Bass("TRN2", target_bir_lowering=False, detect_race_conditions=False)
        ext = {}
        ext["x_own"] = nc.dram_tensor("x_own", [TOK, D], F32, kind="ExternalInput").ap()
        ext["x_other"] = nc.dram_tensor("x_other", [TOK, D], F32, kind="ExternalInput").ap()
        ext["WqT"] = nc.dram_tensor("WqT", [D, D], F32, kind="ExternalInput").ap()
        ext["WkT"] = nc.dram_tensor("WkT", [D, DK], F32, kind="ExternalInput").ap()
        ext["WvT"] = nc.dram_tensor("WvT", [D, DK], F32, kind="ExternalInput").ap()
        ext["ln_w"] = nc.dram_tensor("ln_w", [D], F32, kind="ExternalInput").ap()
        ext["ln_b"] = nc.dram_tensor("ln_b", [D], F32, kind="ExternalInput").ap()
        ext["xn_own"] = nc.dram_tensor("xn_own", [TOK, D], F32, kind="ExternalOutput").ap()
        ext["nm_own"] = nc.dram_tensor("nm_own", [TOK, D], F32, kind="ExternalOutput").ap()
        ext["att_own"] = nc.dram_tensor("att_own", [TOK, D], F32, kind="ExternalOutput").ap()
    x_own = ext["x_own"]
    x_oth = ext["x_other"]
    wq_d = ext["WqT"]
    wk_d = ext["WkT"]
    wv_d = ext["WvT"]
    lnw_d = ext["ln_w"]
    lnb_d = ext["ln_b"]
    xn_o = ext["xn_own"]
    nm_o = ext["nm_own"]
    att_o = ext["att_own"]

    es = ExitStack()
    sb = lambda n, s, d: es.enter_context(nc.sbuf_tensor("A" + n, s, d))
    psa = lambda n, s, d: es.enter_context(nc.psum_tensor("A" + n, s, d))
    with es:
        pid = sb("pid", [128, 1], F32)
        cid = sb("cid", [128, 128], F32)
        ident_bf = sb("ident_bf", [128, 128], BF16)
        ones_row = sb("ones_row", [1, 128], F32)
        one11 = sb("one11", [1, 1], F32)
        onescol = sb("onescol", [128, 1], F32)
        onescol_bf = sb("onescol_bf", [128, 1], BF16)
        wqT = sb("wqT", [128, 8, D], BF16)
        wkT = sb("wkT", [128, 8, DK], BF16)
        wvT = sb("wvT", [128, 8, DK], BF16)
        lnw_row = sb("lnw_row", [1, D], F32)
        lnb_row = sb("lnb_row", [1, D], F32)
        lnw_rep = sb("lnw_rep", [128, D], F32)
        lnb_rep = sb("lnb_rep", [128, D], F32)
        xs = sb("xs", [128, 2, D], F32)
        xnf = sb("xnf", [128, 2, D], F32)
        nmf = sb("nmf", [128, D], F32)
        nmf2 = sb("nmf2", [128, 2, D], F32)
        xnb = sb("xnb", [128, 2, D], BF16)
        xnTg = sb("xnTg", [128, 2, 8, 128], BF16)
        qT = sb("qT", [128, 8, TOK], BF16)
        kT = sb("kT", [128, 32, 128], BF16)
        vsb = sb("vsb", [128, 32, 65], BF16)
        q2 = sb("q2", [64, TOK], BF16)
        bq = sb("bq", [1, 16], F32)
        kq = sb("kq", [1, 1], F32)
        mscr = sb("mscr", [1, 1], F32)
        negB = sb("negB", [128, 16], F32)
        pbuf = sb("pbuf", [128, 2, 2, 512], BF16)
        attscr = sb("attscr", [64, 512], BF16)
        srow = sb("srow", [1, 512], F32)
        r4 = sb("r4", [128, 4], F32)
        att_sb = sb("att_sb", [128, 16, D], F32)
        lns = dict(musum=sb("musum", [128, 1], F32), negmu=sb("negmu", [128, 1], F32),
                   ssq=sb("ssq", [128, 1], F32), veps=sb("veps", [128, 1], F32),
                   sd=sb("sd", [128, 1], F32), rstd=sb("rstd", [128, 1], F32),
                   d=sb("lnd", [128, D], F32), sq=sb("sq", [128, D], F32),
                   ex2=sb("lnex2", [128, 1], F32), musq=sb("lnmusq", [128, 1], F32),
                   gap=sb("lngap", [128, 1], F32))
        psA = psa("psA", [128, 512], F32)
        psB = psa("psB", [128, 512], BF16)
        psC = psa("psC", [128, 512], F32)
        psD = psa("psD", [128, 512], F32)
        psE = psa("psE", [128, 512], F32)
        psF = psa("psF", [128, 512], F32)
        psG = psa("psG", [128, 512], F32)
        psH = psa("psH", [128, 512], F32)
        SCB = [psC, psD, psE, psF]

        with nc.Block() as block:
            p = ext["_P"].bind(block) if "_P" in ext else P(nc, es, block, pfx="A_")
            DONE = p.S("DONE")
            for _eng in [block.sync, block.gpsimd, block.vector, block.scalar, block.tensor]:
                def _bar(e, entry=entry):
                    for _sh, _cnt in entry:
                        e.wait_ge(_sh, _cnt)
                _eng(_bar)
            PE, VE, AC, PL = p.PE, p.VE, p.AC, p.PL
            LG, LW = p.S("LG"), p.S("LW")
            LDp = [p.S("LD0"), p.S("LD1")]
            SXp = [p.S("SX0"), p.S("SX1")]
            SNp = [p.S("SN0"), p.S("SN1")]
            SA = p.S("SA")

            def gload(g):
                LG.inc(g.dma_start(out=wqT[:], in_=wq_d.rearrange("(c p) q -> p c q", p=128)), 16)
                LG.inc(g.dma_start(out=wkT[:], in_=wk_d.rearrange("(c p) q -> p c q", p=128)), 16)
                LG.inc(g.dma_start(out=wvT[:], in_=wv_d.rearrange("(c p) q -> p c q", p=128)), 16)
                g.iota(out=pid[:], pattern=[[0, 1]], base=0, channel_multiplier=1,
                       allow_small_or_imprecise_dtypes=True)
                PL.inc(g.iota(out=cid[:], pattern=[[1, 128]], base=0, channel_multiplier=0,
                              allow_small_or_imprecise_dtypes=True))
            p.gps(gload)
            lg_w = LG.n
            pl_c = PL.n

            def sload(sy):
                LW.inc(sy.dma_start(out=lnw_row[:], in_=lnw_d[None, :]), 16)
                LW.inc(sy.dma_start(out=lnb_row[:], in_=lnb_d[None, :]), 16)
            p.sync(sload)
            ld_ln = LW.n

            def vcst(v):
                v.wait_ge(PL.h, pl_c)
                v.memset(ones_row[:], 1.0)
                v.memset(one11[:], 1.0)
                v.memset(onescol[:], 1.0)
                v.memset(onescol_bf[:], 1.0)
                v.memset(bq[:], 0.0)
                v.memset(kq[:], 0.0)
                v.memset(lns["gap"][:], 0.0)
                v.memset(mscr[:], 0.0)
                v.memset(srow[:], 0.0)
                v.memset(vsb[:, :, 64:65], 1.0)
                VE.inc(v.tensor_scalar(out=ident_bf[:], in0=cid[:], scalar1=pid[:, :],
                                       scalar2=None, op0=ALU.is_equal))
            p.vec(vcst)
            ve_cst = VE.n

            prev_rep = 0
            for (row, rep) in [(lnw_row, lnw_rep), (lnb_row, lnb_rep)]:
                for i in range(2):
                    def tr(t, row=row, i=i, prev_rep=prev_rep):
                        t.wait_ge(VE.h, ve_cst)
                        t.wait_ge(VE.h, prev_rep)
                        t.wait_ge(LW.h, ld_ln)
                        PE.inc(t.matmul(out=psA[:, 0:512], lhsT=ones_row[:],
                                        rhs=row[0:1, i * 512:(i + 1) * 512], start=True, stop=True))
                    p.ten(tr)
                    mpe = PE.n

                    def vr(v, rep=rep, i=i, mpe=mpe):
                        v.wait_ge(PE.h, mpe)
                        VE.inc(v.tensor_copy(out=rep[:, i * 512:(i + 1) * 512], in_=psA[:, 0:512]))
                    p.vec(vr)
                    prev_rep = VE.n
            rep_done = VE.n

            xs_free = [0, 0]
            xnf_free = [0, 0]
            xnf_st = [0, 0]
            xnb_free = [0, 0]
            xnTg_free = [0, 0]
            nmf2_st = [0, 0]
            for gt in range(32):
                own = gt < 16
                g = gt if own else gt - 16
                src = x_own if own else x_oth
                pp = gt % 2

                mfree = xs_free[pp]

                LD = LDp[pp]

                def syl(sy, g=g, src=src, pp=pp, mfree=mfree, LD=LD):
                    sy.wait_ge(VE.h, mfree)
                    LD.inc(sy.dma_start(out=xs[:, pp, :], in_=src[g * 128:(g + 1) * 128, :]), 16)
                p.sync(syl)
                ld_m = LD.n

                def vwt(v, ld_m=ld_m, LD=LD):
                    v.wait_ge(LD.h, ld_m)
                p.vec(vwt)
                ve_xn = emit_ln(p, xs[:, pp, :], xnf[:, pp, :], lns,
                                out_waits=[(SXp[pp], xnf_st[pp])])

                if own:
                    def syo(sy, g=g, pp=pp, ve_xn=ve_xn):
                        sy.wait_ge(VE.h, ve_xn)
                        SXp[pp].inc(sy.dma_start(out=xn_o[g * 128:(g + 1) * 128, :],
                                                 in_=xnf[:, pp, :]), 16)
                    p.sync(syo)
                    xnf_st[pp] = SXp[pp].n

                def vb(v, pp=pp, pm=xnb_free[pp]):
                    v.wait_ge(PE.h, pm)
                    VE.inc(v.tensor_copy(out=xnb[:, pp, :], in_=xnf[:, pp, :]))
                p.vec(vb)
                ve_xnb = VE.n
                xs_free[pp] = ve_xnb

                evac_marks = []
                for half in range(2):
                    wv = evac_marks[-1] if evac_marks else None

                    def tt(t, half=half, pp=pp, ve_xnb=ve_xnb, wv=wv):
                        t.wait_ge(VE.h, ve_xnb if wv is None else wv)
                        for i in range(4):
                            blk = half * 4 + i
                            PE.inc(t.transpose(out=psB[:, i * 128:(i + 1) * 128],
                                               in_=xnb[:, pp, blk * 128:(blk + 1) * 128],
                                               identity=ident_bf[:]))
                    p.ten(tt)
                    pe_t = PE.n

                    def vt(v, half=half, pp=pp, pe_t=pe_t):
                        v.wait_ge(PE.h, pe_t)
                        VE.inc(v.tensor_copy(
                            out=xnTg[:, pp, half * 4:(half + 1) * 4, :].rearrange("p a b -> p (a b)"),
                            in_=psB[:, 0:512]))
                    p.vec(vt)
                    evac_marks.append(VE.n)
                ve_xt = VE.n
                xnb_free[pp] = pe_t

                xnb_free[pp] = PE.n

                def tk(t, pp=pp, ve_xt=ve_xt, lg_w=lg_w):
                    t.wait_ge(VE.h, ve_xt)
                    t.wait_ge(LG.h, lg_w)
                    for i in range(8):
                        PE.inc(t.matmul(out=psA[0:64, 0:128], lhsT=wkT[:, i, :],
                                        rhs=xnTg[:, pp, i, :], start=(i == 0), stop=(i == 7)))
                p.ten(tk)
                pe_k = PE.n

                def vk(v, gt=gt, pe_k=pe_k):
                    v.wait_ge(PE.h, pe_k)
                    VE.inc(v.tensor_copy(out=kT[0:64, gt, :], in_=psA[0:64, 0:128]))
                    VE.inc(v.tensor_copy(out=kT[64:128, gt, :], in_=psA[0:64, 0:128]))
                p.vec(vk)
                ve_k = VE.n

                def tv(t, pp=pp, ve_k=ve_k):
                    t.wait_ge(VE.h, ve_k)
                    for i in range(8):
                        PE.inc(t.matmul(out=psG[:, 0:64], lhsT=xnTg[:, pp, i, :],
                                        rhs=wvT[:, i, :], start=(i == 0), stop=(i == 7)))
                p.ten(tv)
                pe_v = PE.n

                def vv(v, gt=gt, pe_v=pe_v):
                    v.wait_ge(PE.h, pe_v)
                    VE.inc(v.tensor_copy(out=vsb[:, gt, 0:64], in_=psG[:, 0:64]))
                p.vec(vv)
                ve_v = VE.n

                if own:
                    for qd in range(2):
                        def tq(t, qd=qd, pp=pp, ve_v=ve_v):
                            t.wait_ge(VE.h, ve_v)
                            for sub in range(4):
                                i = qd * 4 + sub
                                for k in range(8):
                                    PE.inc(t.matmul(out=SCB[sub][:, 0:128],
                                                    lhsT=wqT[:, k, i * 128:(i + 1) * 128],
                                                    rhs=xnTg[:, pp, k, :],
                                                    start=(k == 0), stop=(k == 7)))
                        p.ten(tq)
                        pe_q = PE.n

                        def vq(v, qd=qd, g=g, pe_q=pe_q):
                            v.wait_ge(PE.h, pe_q)
                            for sub in range(4):
                                i = qd * 4 + sub
                                VE.inc(v.tensor_copy(out=qT[:, i, g * 128:(g + 1) * 128],
                                                     in_=SCB[sub][:, 0:128]))
                        p.vec(vq)
                        ve_v = VE.n
                    xnTg_free[pp] = PE.n

                    emit_ln(p, xnf[:, pp, :], nmf[:], lns)

                    def vaff(v, pp=pp, rep_done=rep_done, stm=nmf2_st[pp]):
                        v.wait_ge(VE.h, rep_done)
                        v.wait_ge(SNp[pp].h, stm)
                        v.tensor_tensor(out=nmf2[:, pp, :], in0=nmf[:], in1=lnw_rep[:], op=ALU.mult)
                        VE.inc(v.tensor_tensor(out=nmf2[:, pp, :], in0=nmf2[:, pp, :],
                                               in1=lnb_rep[:], op=ALU.add))
                    p.vec(vaff)
                    ve_nm = VE.n

                    def syn(sy, g=g, pp=pp, ve_nm=ve_nm):
                        sy.wait_ge(VE.h, ve_nm)
                        SNp[pp].inc(sy.dma_start(out=nm_o[g * 128:(g + 1) * 128, :],
                                                 in_=nmf2[:, pp, :]), 16)
                    p.sync(syn)
                    nmf2_st[pp] = SNp[pp].n
                else:
                    xnTg_free[pp] = PE.n

            prev_vbq = VE.n
            for h in range(16):
                def sqi(s, h=h, pw=prev_vbq):
                    s.wait_ge(VE.h, pw)
                    r0 = (h % 2) * 64
                    AC.inc(s.activation(out=q2[:, :], in_=qT[r0:r0 + 64, h // 2, :],
                                        func=AF.Square))
                p.sca(sqi)
                ac_q2 = AC.n
                for qc in range(4):
                    def tb(t, qc=qc, ac_q2=ac_q2, pw=prev_vbq):
                        t.wait_ge(AC.h, ac_q2)
                        t.wait_ge(VE.h, pw)
                        PE.inc(t.matmul(out=psA[0:1, 0:512], lhsT=onescol_bf[0:64, :],
                                        rhs=q2[:, qc * 512:(qc + 1) * 512], start=True, stop=True))
                    p.ten(tb)
                    pe_b = PE.n

                    def vbq(v, h=h, pe_b=pe_b):
                        v.wait_ge(PE.h, pe_b)
                        v.reduce_max(out=mscr[:], in_=psA[0:1, 0:512], axis=AXX)
                        VE.inc(v.tensor_tensor(out=bq[0:1, h:h + 1], in0=bq[0:1, h:h + 1],
                                               in1=mscr[:], op=ALU.max))
                    p.vec(vbq)
                    prev_vbq = VE.n
            for kk in range(4):
                def sqk(s, kk=kk, pw=prev_vbq):
                    s.wait_ge(VE.h, pw)
                    AC.inc(s.activation(out=q2[:, 0:1024],
                                        in_=kT[0:64, kk * 8:(kk + 1) * 8, :].rearrange("p a b -> p (a b)"),
                                        func=AF.Square))
                p.sca(sqk)
                ac_k2 = AC.n
                for qc in range(2):
                    def tkb(t, qc=qc, ac_k2=ac_k2, pw=prev_vbq):
                        t.wait_ge(AC.h, ac_k2)
                        t.wait_ge(VE.h, pw)
                        PE.inc(t.matmul(out=psA[0:1, 0:512], lhsT=onescol_bf[0:64, :],
                                        rhs=q2[:, qc * 512:(qc + 1) * 512], start=True, stop=True))
                    p.ten(tkb)
                    pe_kb = PE.n

                    def vkq(v, pe_kb=pe_kb):
                        v.wait_ge(PE.h, pe_kb)
                        v.tensor_copy(out=srow[0:1, 0:1], in_=kq[:])
                        v.reduce_max(out=mscr[:], in_=psA[0:1, 0:512], axis=AXX)
                        v.tensor_copy(out=srow[0:1, 0:1], in_=kq[:])
                        VE.inc(v.tensor_tensor(out=kq[:], in0=kq[:], in1=mscr[:], op=ALU.max))
                    p.vec(vkq)
                    prev_vbq = VE.n

            def vfin(v, pw=prev_vbq):
                v.wait_ge(VE.h, pw)
                v.tensor_copy(out=mscr[:], in_=kq[:])
                VE.inc(v.tensor_scalar(out=bq[:], in0=bq[:], scalar1=kq[0:1, :], scalar2=None,
                                       op0=ALU.mult))
            p.vec(vfin)
            ve_b1 = VE.n

            def sfin(s, ve_b1=ve_b1):
                s.wait_ge(VE.h, ve_b1)
                AC.inc(s.activation(out=bq[:], in_=bq[:], func=AF.Sqrt))
            p.sca(sfin)
            ac_b = AC.n

            def vfin2(v, ac_b=ac_b):
                v.wait_ge(AC.h, ac_b)
                v.tensor_copy(out=mscr[:], in_=mscr[:])
                VE.inc(v.tensor_scalar(out=bq[:], in0=bq[:], scalar1=-0.125, scalar2=None,
                                       op0=ALU.mult))
            p.vec(vfin2)
            ve_b2 = VE.n

            def trep(t, ve_b2=ve_b2):
                t.wait_ge(VE.h, ve_b2)
                PE.inc(t.matmul(out=psA[:, 0:16], lhsT=ones_row[:], rhs=bq[0:1, :],
                                start=True, stop=True))
            p.ten(trep)
            pe_nb = PE.n

            def vnb(v, pe_nb=pe_nb):
                v.wait_ge(PE.h, pe_nb)
                VE.inc(v.tensor_copy(out=negB[:], in_=psA[:, 0:16]))
            p.vec(vnb)
            ve_negB = VE.n

            sc_free_ac = {0: 0, 1: 0}
            last_vev = 0
            last_vacc = 0
            last_ver = 0
            for h in range(16):
                for qc in range(4):
                    pending_av = []
                    for kcg in range(16):
                        gp = kcg % 2
                        bank0, bank1 = (psC, psD) if gp == 0 else (psE, psF)
                        pb = pbuf[:, gp, :, :]
                        ac_prev = sc_free_ac[gp]

                        def tqk(t, h=h, qc=qc, kcg=kcg, bank0=bank0, bank1=bank1,
                                ac_prev=ac_prev, ve_negB=ve_negB):
                            t.wait_ge(VE.h, ve_negB)
                            if ac_prev:
                                t.wait_ge(AC.h, ac_prev)
                            r0 = (h % 2) * 64
                            for j in range(2):
                                kc = kcg * 2 + j
                                bank = bank0 if j == 0 else bank1
                                PE.inc(t.matmul(out=bank[:, 0:512], lhsT=kT[r0:r0 + 64, kc, :],
                                                rhs=qT[r0:r0 + 64, h // 2, qc * 512:(qc + 1) * 512],
                                                start=True, stop=True))
                        p.ten(tqk)
                        pe_qk = PE.n

                        def sexp(s, h=h, bank0=bank0, bank1=bank1, pb=pb, pe_qk=pe_qk):
                            s.wait_ge(PE.h, pe_qk)
                            s.activation(out=pb[:, 0, :], in_=bank0[:, 0:512], func=AF.Exp,
                                         scale=0.125, bias=negB[:, h:h + 1])
                            AC.inc(s.activation(out=pb[:, 1, :], in_=bank1[:, 0:512], func=AF.Exp,
                                                scale=0.125, bias=negB[:, h:h + 1]))
                        p.sca(sexp)
                        sc_free_ac[gp] = AC.n
                        ac_exp = AC.n

                        def tav(t, kcg=kcg, pb=pb, ac_exp=ac_exp, lv=last_vev):
                            t.wait_ge(AC.h, ac_exp)
                            if kcg == 0:
                                t.wait_ge(VE.h, lv)
                            for j in range(2):
                                kc = kcg * 2 + j
                                PE.inc(t.matmul(out=psG[0:65, 0:512], lhsT=vsb[:, kc, :],
                                                rhs=pb[:, j, :], start=(kc == 0), stop=(kc == 31)))
                        pending_av.append(tav)
                        if kcg >= 1:
                            p.ten(pending_av.pop(0))
                    p.ten(pending_av.pop(0))
                    pe_av = PE.n

                    def vev(v, pe_av=pe_av, lpa=last_vacc):
                        v.wait_ge(PE.h, pe_av)
                        v.wait_ge(PE.h, lpa)
                        v.tensor_copy(out=attscr[:], in_=psG[0:64, 0:512])
                        VE.inc(v.tensor_copy(out=srow[:], in_=psG[64:65, 0:512]))
                    p.vec(vev)
                    last_vev = VE.n

                    def trt(t, lv=last_vev, lvr=last_ver):
                        t.wait_ge(VE.h, lv)
                        t.wait_ge(VE.h, lvr)
                        for tb in range(4):
                            PE.inc(t.transpose(out=psH[:, tb:tb + 1],
                                               in_=srow[0:1, tb * 128:(tb + 1) * 128],
                                               identity=one11[:]))
                    p.ten(trt)
                    pe_rt = PE.n

                    def vrc(v, pe_rt=pe_rt):
                        v.wait_ge(PE.h, pe_rt)
                        v.tensor_copy(out=mscr[:], in_=srow[0:1, 0:1])
                        v.reciprocal(out=r4[:], in_=psH[:, 0:4])
                        VE.inc(v.tensor_copy(out=mscr[:], in_=srow[0:1, 0:1]))
                    p.vec(vrc)
                    last_ver = VE.n

                    def tat(t, lv=last_ver):
                        t.wait_ge(VE.h, lv)
                        for tb in range(4):
                            PE.inc(t.transpose(out=psB[:, tb * 128:tb * 128 + 64],
                                               in_=attscr[:, tb * 128:(tb + 1) * 128],
                                               identity=ident_bf[0:64, 0:64]))
                    p.ten(tat)
                    pe_at = PE.n
                    last_vacc = pe_at

                    def vacc(v, h=h, qc=qc, pe_at=pe_at):
                        v.wait_ge(PE.h, pe_at)
                        for tb in range(4):
                            g = 4 * qc + tb
                            VE.inc(v.tensor_scalar(out=att_sb[:, g, h * 64:(h + 1) * 64],
                                                   in0=psB[:, tb * 128:tb * 128 + 64],
                                                   scalar1=r4[:, tb:tb + 1], scalar2=None,
                                                   op0=ALU.mult))
                    p.vec(vacc)

            ve_all = VE.n

            def syatt(sy, ve_all=ve_all):
                sy.wait_ge(VE.h, ve_all)
                for g in range(16):
                    SA.inc(sy.dma_start(out=att_o[g * 128:(g + 1) * 128, :], in_=att_sb[:, g, :]), 16)
                sy.wait_ge(SA.h, SA.n)
                sy.wait_ge(SXp[0].h, SXp[0].n)
                sy.wait_ge(SXp[1].h, SXp[1].n)
                sy.wait_ge(SNp[0].h, SNp[0].n)
                sy.wait_ge(SNp[1].h, SNp[1].n)
                DONE.inc(sy.nop())
            p.sync(syatt)

    return _lower_ext(nc) if standalone else (nc, (DONE.h, DONE.n))



def _prep_A_inputs(x, Wq, Wk, Wv, ln_w, ln_b):
    x = np.ascontiguousarray(np.asarray(x, np.float32))
    WqT = np.ascontiguousarray(np.asarray(Wq, np.float32).T)
    WkT = np.ascontiguousarray(np.asarray(Wk, np.float32).T)
    WvT = np.ascontiguousarray(np.asarray(Wv, np.float32).T)
    ln_w = np.ascontiguousarray(np.asarray(ln_w, np.float32))
    ln_b = np.ascontiguousarray(np.asarray(ln_b, np.float32))
    maps = []
    for c in range(NC):
        b, h = c // 2, c % 2
        xo = np.ascontiguousarray(x[b, h * TOK:(h + 1) * TOK, :])
        xt = np.ascontiguousarray(x[b, (1 - h) * TOK:(2 - h) * TOK, :])
        maps.append({"x_own": xo, "x_other": xt, "WqT": WqT, "WkT": WkT, "WvT": WvT,
                     "ln_w": ln_w, "ln_b": ln_b})
    return maps


def kernel(**inputs):
    raise NotImplementedError("under construction")



BIG_OOB = 4.0e6

def build_B(nc=None, ext=None, entry=()):
    standalone = nc is None
    if standalone:
        nc = bass.Bass("TRN2", target_bir_lowering=False, detect_race_conditions=False)
        ext = {}
        ext["nm_own"] = nc.dram_tensor("nm_own", [TOK, D], F32, kind="ExternalInput").ap()
        ext["router_w"] = nc.dram_tensor("router_w", [D, E], F32, kind="ExternalInput").ap()
        ext["gate_w"] = nc.dram_tensor("gate_w", [E, D, I0], F32, kind="ExternalInput").ap()
        ext["up_w"] = nc.dram_tensor("up_w", [E, D, I0], F32, kind="ExternalInput").ap()
        ext["down_w"] = nc.dram_tensor("down_w", [E, I0, D], F32, kind="ExternalInput").ap()
        ext["sh_gate"] = nc.dram_tensor("sh_gate", [D, I0], F32, kind="ExternalInput").ap()
        ext["sh_up"] = nc.dram_tensor("sh_up", [D, I0], F32, kind="ExternalInput").ap()
        ext["sh_down"] = nc.dram_tensor("sh_down", [I0, D], F32, kind="ExternalInput").ap()
        ext["Ustrict"] = nc.dram_tensor("Ustrict", [128, 128], F32, kind="ExternalInput").ap()
        ext["blkstrict"] = nc.dram_tensor("blkstrict", [128, 128], F32, kind="ExternalInput").ap()
        ext["ecrow"] = nc.dram_tensor("ecrow", [1, 128], F32, kind="ExternalInput").ap()
        ext["moe_own"] = nc.dram_tensor("moe_own", [TOK, D], F32, kind="ExternalOutput").ap()
    nm_d = ext["nm_own"]
    rw_d = ext["router_w"]
    gw_d = ext["gate_w"]
    uw_d = ext["up_w"]
    dw_d = ext["down_w"]
    shg_d = ext["sh_gate"]
    shu_d = ext["sh_up"]
    shd_d = ext["sh_down"]
    us_d = ext["Ustrict"]
    bs_d = ext["blkstrict"]
    ec_d = ext["ecrow"]
    moe_o = ext["moe_own"]
    pack_dram = nc.dram_tensor("B_pack", [E * C0 + 128, D], BF16)
    pout_dram = nc.dram_tensor("B_pout", [E * C0 + 128, D], BF16)

    es = ExitStack()
    sb = lambda n, s_, d_: es.enter_context(nc.sbuf_tensor("B" + n, s_, d_))
    psa = lambda n, s_, d_: es.enter_context(nc.psum_tensor("B" + n, s_, d_))
    with es:
        ones_row = sb("ones_row", [1, 128], F32)
        onescol = sb("onescol", [128, 1], F32)
        nmT = sb("nmT", [128, 8, TOK], BF16)
        nmb = sb("nmb", [128, 16, D], BF16)
        us_s = sb("us_s", [128, 128], F32)
        bs_s = sb("bs_s", [128, 128], F32)
        ec_row = sb("ec_row", [1, 128], F32)
        rw32 = sb("rw32", [128, 8, E], F32)
        xT32 = sb("xT32", [128, 8, 128], F32)
        nmf = sb("nmf", [128, 2, D], F32)
        logit = sb("logit", [128, 16, E], F32)
        m8 = sb("m8", [128, 16, 8], F32)
        argi = sb("argi", [128, 16, 8], U32)
        argf = sb("argf", [128, 16], F32)
        wsig = sb("wsig", [128, 16], F32)
        mask_all = sb("mask_all", [128, 128], F32)
        prefx = sb("prefx", [128, 128], F32)
        tots = sb("tots", [128, 1], F32)
        coloff = sb("coloff", [128, 1], F32)
        corow = sb("corow", [1, 128], F32)
        rankf = sb("rankf", [128, 128], F32)
        ecrep = sb("ecrep", [128, 128], F32)
        valid = sb("valid", [128, 128], F32)
        mval = sb("mval", [128, 128], F32)
        t1 = sb("t1", [128, 128], F32)
        slotA = sb("slotA", [128, 16], F32)
        slotB = sb("slotB", [128, 16], F32)
        slotf = sb("slotf", [128, 16], F32)
        widx = sb("widx", [128, 128], I16)
        gapt = sb("gapt", [128, 1], F32)
        zwt = sb("zwt", [128, 2, D], BF16)
        moe_acc = sb("moe_acc", [128, 16, D], F32)
        shgp = sb("shgp", [128, 2, 8, 128], BF16)
        shup = sb("shup", [128, 2, 8, 128], BF16)
        hmid = sb("hmid", [128, 16, 512], BF16)
        hgt = sb("hgt", [128, 512], BF16)
        hge = sb("hge", [128, C0], BF16)
        orow = sb("orow", [128, D], BF16)
        ident_bf = sb("ident_bf", [128, 128], BF16)
        ident_f = sb("ident_f", [128, 128], F32)
        pid = sb("pid", [128, 1], F32)
        cid = sb("cid", [128, 128], F32)
        psA = psa("psA", [128, 512], F32)
        psB = psa("psB", [128, 512], BF16)
        psC = psa("psC", [128, 512], F32)
        psD = psa("psD", [128, 512], F32)
        psE = psa("psE", [128, 512], F32)
        psF = psa("psF", [128, 512], F32)

        with nc.Block() as block:
            p = ext["_P"].bind(block) if "_P" in ext else P(nc, es, block, pfx="B_")
            DONE = p.S("DONE")
            for _eng in [block.sync, block.gpsimd, block.vector, block.scalar, block.tensor]:
                def _bar(e, entry=entry):
                    for _sh, _cnt in entry:
                        e.wait_ge(_sh, _cnt)
                _eng(_bar)
            PE, VE, AC, PL = p.PE, p.VE, p.AC, p.PL
            LG = p.S("LG"); LW = p.S("LW")
            LDp = [p.S("LD0"), p.S("LD1")]
            LSH = p.S("LSH"); LSH2 = p.S("LSH2"); LSH3 = p.S("LSH3")
            LEX = p.S("LEX"); LEX2 = p.S("LEX2")
            LPK = p.S("LPK"); SCT = p.S("SCT"); SMO = p.S("SMO"); GIX = p.S("GIX")
            SPO = p.S("SPO")

            def gc(g):
                g.iota(out=pid[:], pattern=[[0, 1]], base=0, channel_multiplier=1,
                       allow_small_or_imprecise_dtypes=True)
                PL.inc(g.iota(out=cid[:], pattern=[[1, 128]], base=0, channel_multiplier=0,
                              allow_small_or_imprecise_dtypes=True))
            p.gps(gc)
            pl_c = PL.n
            lg_w = LG.n

            def sc(sy):
                LW.inc(sy.dma_start(out=rw32[:], in_=rw_d.rearrange("(c p) e -> p c e", p=128)), 16)
                LW.inc(sy.dma_start(out=us_s[:], in_=us_d[:, :]), 16)
                LW.inc(sy.dma_start(out=bs_s[:], in_=bs_d[:, :]), 16)
                LW.inc(sy.dma_start(out=ec_row[:], in_=ec_d[:, :]), 16)
            p.sync(sc)
            lw_c = LW.n

            def vc(v):
                v.wait_ge(PL.h, pl_c)
                v.memset(ones_row[:], 1.0)
                v.memset(onescol[:], 1.0)
                v.memset(gapt[:], 0.0)
                v.tensor_scalar(out=ident_bf[:], in0=cid[:], scalar1=pid[:, :],
                                scalar2=None, op0=ALU.is_equal)
                VE.inc(v.tensor_scalar(out=ident_f[:], in0=cid[:], scalar1=pid[:, :],
                                       scalar2=None, op0=ALU.is_equal))
            p.vec(vc)

            nmf_free = [0, 0]
            nmf_free_pe = [0, 0]
            for g in range(16):
                pp = g % 2
                LD = LDp[pp]

                def syl(sy, g=g, pp=pp, LD=LD, m=nmf_free[pp], mp=nmf_free_pe[pp]):
                    sy.wait_ge(VE.h, m)
                    sy.wait_ge(PE.h, mp)
                    LD.inc(sy.dma_start(out=nmf[:, pp, :], in_=nm_d[g * 128:(g + 1) * 128, :]), 16)
                p.sync(syl)
                ld_m = LD.n

                def vb(v, g=g, pp=pp, LD=LD, ld_m=ld_m):
                    v.wait_ge(LD.h, ld_m)
                    VE.inc(v.tensor_copy(out=nmb[:, g, :], in_=nmf[:, pp, :]))
                p.vec(vb)
                nmf_free[pp] = VE.n
                ve_b = VE.n

                evm = []
                for half in range(2):
                    wv = evm[-1] if evm else None

                    def tt(t, g=g, half=half, ve_b=ve_b, wv=wv):
                        t.wait_ge(VE.h, ve_b if wv is None else wv)
                        for i in range(4):
                            blk = half * 4 + i
                            PE.inc(t.transpose(out=psB[:, i * 128:(i + 1) * 128],
                                               in_=nmb[:, g, blk * 128:(blk + 1) * 128],
                                               identity=ident_bf[:]))
                    p.ten(tt)
                    pe_t = PE.n

                    def vt(v, g=g, half=half, pe_t=pe_t):
                        v.wait_ge(PE.h, pe_t)
                        for i in range(4):
                            blk = half * 4 + i
                            VE.inc(v.tensor_copy(out=nmT[:, blk, g * 128:(g + 1) * 128],
                                                 in_=psB[:, i * 128:(i + 1) * 128]))
                    p.vec(vt)
                    evm.append(VE.n)
                ve_tp = VE.n

                evm2 = []
                for half in range(2):
                    wv2 = evm2[-1] if evm2 else None

                    def tt32(t, g=g, pp=pp, half=half, wv2=wv2, m=VE.n):
                        t.wait_ge(VE.h, m if wv2 is None else wv2)
                        for i in range(4):
                            blk = half * 4 + i
                            PE.inc(t.transpose(out=psC[:, i * 128:(i + 1) * 128],
                                               in_=nmf[:, pp, blk * 128:(blk + 1) * 128],
                                               identity=ident_f[:]))
                    p.ten(tt32)
                    pe_t32 = PE.n

                    def vt32(v, half=half, pe_t32=pe_t32):
                        v.wait_ge(PE.h, pe_t32)
                        for i in range(4):
                            blk = half * 4 + i
                            VE.inc(v.tensor_copy(out=xT32[:, blk, :],
                                                 in_=psC[:, i * 128:(i + 1) * 128]))
                    p.vec(vt32)
                    evm2.append(VE.n)
                ve_x32 = VE.n
                nmf_free_pe[pp] = PE.n

                def tl(t, g=g, ve_x32=ve_x32, lw_c=lw_c):
                    t.wait_ge(VE.h, ve_x32)
                    t.wait_ge(LW.h, lw_c)
                    for c in range(8):
                        PE.inc(t.matmul(out=psA[:, 0:8], lhsT=xT32[:, c, :],
                                        rhs=rw32[:, c, :], start=(c == 0), stop=(c == 7)))
                p.ten(tl)
                pe_l = PE.n

                def vl(v, g=g, pe_l=pe_l):
                    v.wait_ge(PE.h, pe_l)
                    v.tensor_copy(out=gapt[:], in_=gapt[:])
                    VE.inc(v.tensor_copy(out=logit[:, g, :], in_=psA[:, 0:8]))
                p.vec(vl)

            ve_log = VE.n

            def vmax(v, ve_log=ve_log):
                v.wait_ge(VE.h, ve_log)
                for g in range(16):
                    VE.inc(v.max(out=m8[:, g, :], in_=logit[:, g, :]))
            p.vec(vmax)
            ve_m8 = VE.n

            def shop(s):
                s.wait_ge(VE.h, ve_m8)
                AC.inc(s.nop())
            p.sca(shop)
            ac_hop = AC.n

            def vmi(v, ac_hop=ac_hop):
                v.wait_ge(AC.h, ac_hop)
                for g in range(16):
                    VE.inc(v.max_index(out=argi[:, g, :], in_max=m8[:, g, :],
                                       in_values=logit[:, g, :]))
                v.tensor_copy(out=gapt[:], in_=gapt[:])
                VE.inc(v.tensor_copy(out=argf[:], in_=argi[:, :, 0]))
            p.vec(vmi)
            ve_arg = VE.n

            def ssig(s, ve_arg=ve_arg):
                s.wait_ge(VE.h, ve_arg)
                AC.inc(s.activation(out=wsig[:], in_=m8[:, :, 0], func=AF.Sigmoid))
            p.sca(ssig)
            ac_sig = AC.n

            def vmask(v):
                v.tensor_copy(out=gapt[:], in_=gapt[:])
                for e in range(E):
                    VE.inc(v.tensor_scalar(
                        out=mask_all.ap().rearrange("p (g e) -> p g e", e=8)[:, :, e],
                        in0=argf[:], scalar1=float(e), scalar2=None, op0=ALU.is_equal))
            p.vec(vmask)
            ve_mask = VE.n

            def tpre(t, ve_mask=ve_mask, lw_c=lw_c):
                t.wait_ge(VE.h, ve_mask)
                t.wait_ge(LW.h, lw_c)
                PE.inc(t.matmul(out=psC[:, 0:128], lhsT=us_s[:], rhs=mask_all[:],
                                start=True, stop=True))
                PE.inc(t.matmul(out=psD[0:128, 0:1], lhsT=mask_all[:], rhs=onescol[:],
                                start=True, stop=True))
            p.ten(tpre)
            pe_pre = PE.n

            def vpre(v, pe_pre=pe_pre):
                v.wait_ge(PE.h, pe_pre)
                v.tensor_copy(out=prefx[:], in_=psC[:, 0:128])
                v.tensor_copy(out=gapt[:], in_=gapt[:])
                VE.inc(v.tensor_copy(out=tots[:], in_=psD[:, 0:1]))
            p.vec(vpre)
            ve_tot = VE.n

            def tco(t, ve_tot=ve_tot):
                t.wait_ge(VE.h, ve_tot)
                PE.inc(t.matmul(out=psE[0:128, 0:1], lhsT=bs_s[:], rhs=tots[:],
                                start=True, stop=True))
            p.ten(tco)
            pe_co = PE.n

            def vco(v, pe_co=pe_co):
                v.wait_ge(PE.h, pe_co)
                v.tensor_copy(out=gapt[:], in_=gapt[:])
                VE.inc(v.tensor_copy(out=coloff[:], in_=psE[:, 0:1]))
            p.vec(vco)
            ve_co = VE.n

            def tct(t, ve_co=ve_co):
                t.wait_ge(VE.h, ve_co)
                PE.inc(t.matmul(out=psF[0:1, 0:128], lhsT=coloff[:], rhs=ident_f[:],
                                start=True, stop=True))
            p.ten(tct)
            pe_ct = PE.n

            def vct(v, pe_ct=pe_ct):
                v.wait_ge(PE.h, pe_ct)
                v.tensor_copy(out=gapt[:], in_=gapt[:])
                VE.inc(v.tensor_copy(out=corow[:], in_=psF[0:1, 0:128]))
            p.vec(vct)
            ve_ct = VE.n

            def trep(t, ve_ct=ve_ct):
                t.wait_ge(VE.h, ve_ct)
                PE.inc(t.matmul(out=psC[:, 0:128], lhsT=ones_row[:], rhs=corow[:],
                                start=True, stop=True))
                PE.inc(t.matmul(out=psD[:, 0:128], lhsT=ones_row[:], rhs=ec_row[:],
                                start=True, stop=True))
            p.ten(trep)
            pe_rep = PE.n

            def vrank(v, pe_rep=pe_rep, ac_sig=ac_sig):
                v.wait_ge(PE.h, pe_rep)
                v.wait_ge(AC.h, ac_sig)
                v.tensor_tensor(out=rankf[:], in0=prefx[:], in1=psC[:, 0:128], op=ALU.add)
                v.tensor_copy(out=ecrep[:], in_=psD[:, 0:128])
                v.tensor_scalar(out=valid[:], in0=rankf[:], scalar1=float(C0), scalar2=None,
                                op0=ALU.is_lt)
                v.tensor_tensor(out=mval[:], in0=valid[:], in1=mask_all[:], op=ALU.mult)
                v.tensor_tensor(out=t1[:], in0=rankf[:], in1=ecrep[:], op=ALU.add)
                v.tensor_tensor(out=t1[:], in0=t1[:], in1=mval[:], op=ALU.mult)
                v.tensor_reduce(out=slotA[:], in_=t1.ap().rearrange("p (g e) -> p g e", e=8),
                                axis=AXX, op=ALU.add)
                v.tensor_tensor(out=mval[:], in0=mask_all[:], in1=mval[:], op=ALU.subtract)
                v.tensor_reduce(out=slotB[:], in_=mval.ap().rearrange("p (g e) -> p g e", e=8),
                                axis=AXX, op=ALU.add)
                v.tensor_copy(out=gapt[:], in_=gapt[:])
                v.scalar_tensor_tensor(out=slotf[:], in0=slotB[:], scalar=BIG_OOB,
                                       in1=slotA[:], op0=ALU.mult, op1=ALU.add)
                v.tensor_copy(out=gapt[:], in_=gapt[:])
                VE.inc(v.tensor_scalar(out=slotf[:], in0=slotf[:], scalar1=float(E * C0),
                                       scalar2=None, op0=ALU.min))
            p.vec(vrank)
            ve_slot = VE.n

            BRW = p.S("BRW")
            xT32_flat = xT32.ap().rearrange("p a b -> p (a b)")
            brw_tot = emit_wrap_idx(p, slotf[:, :], xT32_flat[0:16, 0:128],
                                    xT32_flat[0:16, 128:256], widx, ident_f[:],
                                    psC, psD, BRW, ve_slot)

            ZP = p.S("ZP")

            def vzz(v):
                VE.inc(v.memset(zwt[:, 0, :], 0.0))
            p.vec(vzz)
            ve_zz = VE.n

            def szz(sy, ve_zz=ve_zz):
                sy.wait_ge(VE.h, ve_zz)
                for a in range(24):
                    ZP.inc(sy.dma_start(out=pack_dram[a * 128:(a + 1) * 128, :],
                                        in_=zwt[:, 0, :]), 16)
                ZP.inc(sy.dma_start(out=pout_dram[E * C0:E * C0 + 128, :],
                                    in_=zwt[:, 0, :]), 16)
            p.sync(szz)
            zp_all = ZP.n

            for g in range(16):
                def vzw(v, g=g, ve_slot=ve_slot):
                    v.wait_ge(VE.h, ve_slot)
                    if g < 2:
                        v.tensor_copy(out=gapt[:], in_=gapt[:])
                    VE.inc(v.tensor_scalar(out=nmb[:, g, :], in0=nmb[:, g, :],
                                           scalar1=wsig[:, g:g + 1], scalar2=None,
                                           op0=ALU.mult))
                p.vec(vzw)
            ve_zw_all = VE.n

            def gsc(g_, ve_zw_all=ve_zw_all, zp_all=zp_all, brw_tot=brw_tot):
                g_.wait_ge(VE.h, ve_zw_all)
                g_.wait_ge(ZP.h, zp_all)
                g_.wait_ge(BRW.h, brw_tot)
                SCT.inc(g_.dma_scatter_add(pack_dram[:, :], nmb[:, :, :], widx[:, :],
                                           TOK, TOK, D), 16)
            p.gps(gsc)
            sct_all = SCT.n

            shd_res = nmb

            def gshd(g_, sct_all=sct_all):
                g_.wait_ge(SCT.h, sct_all)
                LSH3.inc(g_.dma_start(out=shd_res[:],
                                      in_=shd_d.rearrange("(c p) d -> p c d", p=128)), 16)
            p.gps(gshd)
            lsh3_c = LSH3.n

            shp_free = [PE.n, PE.n]
            dn_evac = VE.n
            for tg in range(4):
                tsl = slice(tg * 512, (tg + 1) * 512)
                hm_marks = []
                for j in range(16):
                    pp = j % 2

                    def gg(g_, j=j, pp=pp, tg=tg, m=shp_free[pp]):
                        g_.wait_ge(PE.h, m)
                        LSH.inc(g_.dma_start(
                            out=shgp[:, pp, :, :],
                            in_=shg_d[:, j * 128:(j + 1) * 128]
                            .rearrange("(c p) h -> p c h", p=128)), 16)
                        LSH2.inc(g_.dma_start(
                            out=shup[:, pp, :, :],
                            in_=shu_d[:, j * 128:(j + 1) * 128]
                            .rearrange("(c p) h -> p c h", p=128)), 16)
                    p.gps(gg)
                    l1, l2 = LSH.n, LSH2.n

                    def tgu(t, j=j, pp=pp, tg=tg, l1=l1, l2=l2, ac_prev=AC.n, ve_prev=VE.n):
                        t.wait_ge(LSH.h, l1)
                        t.wait_ge(LSH2.h, l2)
                        t.wait_ge(AC.h, ac_prev)
                        t.wait_ge(VE.h, ve_prev)
                        for c in range(8):
                            PE.inc(t.matmul(out=psC[:, 0:512], lhsT=shgp[:, pp, c, :],
                                            rhs=nmT[:, c, tsl], start=(c == 0), stop=(c == 7)))
                        for c in range(8):
                            PE.inc(t.matmul(out=psD[:, 0:512], lhsT=shup[:, pp, c, :],
                                            rhs=nmT[:, c, tsl], start=(c == 0), stop=(c == 7)))
                    p.ten(tgu)
                    shp_free[pp] = PE.n
                    pe_gu = PE.n

                    def sgu(s, pe_gu=pe_gu):
                        s.wait_ge(PE.h, pe_gu)
                        AC.inc(s.activation(out=hgt[:], in_=psC[:, 0:512], func=AF.Silu))
                    p.sca(sgu)
                    ac_si = AC.n

                    def vgu(v, j=j, ac_si=ac_si):
                        v.wait_ge(AC.h, ac_si)
                        VE.inc(v.tensor_tensor(out=hmid[:, j, :], in0=hgt[:], in1=psD[:, 0:512],
                                               op=ALU.mult))
                    p.vec(vgu)
                ve_hm = VE.n

                for tc in range(4):
                    for dh in range(2):
                        bank = [psE, psF][dh]

                        def tdn(t, tg=tg, tc=tc, dh=dh, bank=bank, ve_hm=ve_hm,
                                lsh3_c=lsh3_c, vprev=dn_evac):
                            t.wait_ge(VE.h, ve_hm)
                            t.wait_ge(LSH3.h, lsh3_c)
                            t.wait_ge(VE.h, vprev)
                            for j in range(16):
                                PE.inc(t.matmul(out=bank[:, 0:512],
                                                lhsT=hmid[:, j, tc * 128:(tc + 1) * 128],
                                                rhs=shd_res[:, j, dh * 512:(dh + 1) * 512],
                                                start=(j == 0), stop=(j == 15)))
                        p.ten(tdn)
                        pe_dn = PE.n

                        def vdn(v, tg=tg, tc=tc, dh=dh, bank=bank, pe_dn=pe_dn):
                            v.wait_ge(PE.h, pe_dn)
                            VE.inc(v.tensor_copy(
                                out=moe_acc[:, tg * 4 + tc, dh * 512:(dh + 1) * 512],
                                in_=bank[:, 0:512]))
                        p.vec(vdn)
                        dn_evac = VE.n

            nmT_flat = nmT.ap().rearrange("p a b -> p (a b)")
            zpk = nmT_flat[:, 0:3072].rearrange("p (a d) -> p a d", d=D)
            zTe = nmT_flat[:, 3072:6144].rearrange("p (c s) -> p c s", s=C0)
            hmidE = nmT_flat[:, 6144:12288].rearrange("p (j s) -> p j s", s=C0)
            de_res = nmb
            pe_sh_done = PE.n
            pk_free = PE.n
            for e in range(E):
                def gpk(g_, e=e, sct_all=sct_all, m=pk_free):
                    g_.wait_ge(SCT.h, sct_all)
                    g_.wait_ge(PE.h, m)
                    LPK.inc(g_.dma_start(out=zpk[:],
                                         in_=pack_dram[e * C0:(e + 1) * C0, :]
                                         .rearrange("(a p) d -> p a d", p=128)), 16)
                p.gps(gpk)
                lpk_m = LPK.n

                def gdw(g_, e=e, m=pe_sh_done):
                    g_.wait_ge(PE.h, m)
                    LSH3.inc(g_.dma_start(out=de_res[:],
                                          in_=dw_d[e].rearrange("(c p) d -> p c d", p=128)), 16)
                p.gps(gdw)
                ldw_m = LSH3.n

                ztev = VE.n
                for a in range(3):
                    for half in range(2):
                        def tz(t, a=a, half=half, lpk_m=lpk_m, vprev=ztev):
                            t.wait_ge(LPK.h, lpk_m)
                            t.wait_ge(VE.h, vprev)
                            for i in range(4):
                                c = half * 4 + i
                                PE.inc(t.transpose(out=psB[:, i * 128:(i + 1) * 128],
                                                   in_=zpk[:, a, c * 128:(c + 1) * 128],
                                                   identity=ident_bf[:]))
                        p.ten(tz)
                        pe_z = PE.n

                        def vz(v, a=a, half=half, pe_z=pe_z):
                            v.wait_ge(PE.h, pe_z)
                            for i in range(4):
                                c = half * 4 + i
                                VE.inc(v.tensor_copy(out=zTe[:, c, a * 128:(a + 1) * 128],
                                                     in_=psB[:, i * 128:(i + 1) * 128]))
                        p.vec(vz)
                        ztev = VE.n
                ve_zT = VE.n

                ex_free = [PE.n, PE.n]
                for j in range(16):
                    pp = j % 2

                    def ge(g_, e=e, j=j, pp=pp, m=ex_free[pp]):
                        g_.wait_ge(PE.h, m)
                        LEX.inc(g_.dma_start(
                            out=shgp[:, pp, :, :],
                            in_=gw_d[e, :, j * 128:(j + 1) * 128]
                            .rearrange("(c p) h -> p c h", p=128)), 16)
                        LEX2.inc(g_.dma_start(
                            out=shup[:, pp, :, :],
                            in_=uw_d[e, :, j * 128:(j + 1) * 128]
                            .rearrange("(c p) h -> p c h", p=128)), 16)
                    p.gps(ge)
                    le1, le2 = LEX.n, LEX2.n

                    def teg(t, j=j, pp=pp, le1=le1, le2=le2, ve_zT=ve_zT,
                            ac_prev=AC.n, ve_prev=VE.n):
                        t.wait_ge(LEX.h, le1)
                        t.wait_ge(LEX2.h, le2)
                        t.wait_ge(VE.h, ve_zT)
                        t.wait_ge(AC.h, ac_prev)
                        t.wait_ge(VE.h, ve_prev)
                        for c in range(8):
                            PE.inc(t.matmul(out=psC[:, 0:C0], lhsT=shgp[:, pp, c, :],
                                            rhs=zTe[:, c, :], start=(c == 0), stop=(c == 7)))
                        for c in range(8):
                            PE.inc(t.matmul(out=psD[:, 0:C0], lhsT=shup[:, pp, c, :],
                                            rhs=zTe[:, c, :], start=(c == 0), stop=(c == 7)))
                    p.ten(teg)
                    ex_free[pp] = PE.n
                    pe_eg = PE.n

                    def seg(s, pe_eg=pe_eg):
                        s.wait_ge(PE.h, pe_eg)
                        AC.inc(s.activation(out=hge[:], in_=psC[:, 0:C0], func=AF.Silu))
                    p.sca(seg)
                    ac_se = AC.n

                    def veg(v, j=j, ac_se=ac_se):
                        v.wait_ge(AC.h, ac_se)
                        VE.inc(v.tensor_tensor(out=hmidE[:, j, :], in0=hge[:], in1=psD[:, 0:C0],
                                               op=ALU.mult))
                    p.vec(veg)
                ve_hE = VE.n

                for tc in range(3):
                    spo_prev = SPO.n
                    for dh in range(2):
                        bank = [psE, psF][dh]

                        def tdne(t, tc=tc, dh=dh, bank=bank, ve_hE=ve_hE, ldw_m=ldw_m,
                                 vprev=dn_evac):
                            t.wait_ge(VE.h, ve_hE)
                            t.wait_ge(LSH3.h, ldw_m)
                            t.wait_ge(VE.h, vprev)
                            for j in range(16):
                                PE.inc(t.matmul(out=bank[:, 0:512],
                                                lhsT=hmidE[:, j, tc * 128:(tc + 1) * 128],
                                                rhs=de_res[:, j, dh * 512:(dh + 1) * 512],
                                                start=(j == 0), stop=(j == 15)))
                        p.ten(tdne)
                        pe_dne = PE.n

                        def vdne(v, tc=tc, dh=dh, bank=bank, pe_dne=pe_dne, sm=spo_prev):
                            v.wait_ge(PE.h, pe_dne)
                            v.wait_ge(SPO.h, sm)
                            VE.inc(v.tensor_copy(out=orow[:, dh * 512:(dh + 1) * 512],
                                                 in_=bank[:, 0:512]))
                        p.vec(vdne)
                        dn_evac = VE.n
                    ve_or = VE.n

                    def spo(sy, e=e, tc=tc, ve_or=ve_or):
                        sy.wait_ge(VE.h, ve_or)
                        SPO.inc(sy.dma_start(
                            out=pout_dram[e * C0 + tc * 128:e * C0 + (tc + 1) * 128, :],
                            in_=orow[:]), 16)
                    p.sync(spo)
                pe_sh_done = PE.n
                pk_free = PE.n

            spo_all = SPO.n

            rbuf = nmT_flat[:, 0:16384].rearrange("p (c d) -> p c d", d=D)

            def ggx(g_, spo_all=spo_all, zp_all=zp_all, brw_tot=brw_tot, m=PE.n):
                g_.wait_ge(SPO.h, spo_all)
                g_.wait_ge(ZP.h, zp_all)
                g_.wait_ge(BRW.h, brw_tot)
                g_.wait_ge(PE.h, m)
                GIX.inc(g_.dma_gather(rbuf[:, 0:8, :], pout_dram[:, :],
                                      widx[:, 0:64], 1024, 1024, D), 16)
                GIX.inc(g_.dma_gather(rbuf[:, 8:16, :], pout_dram[:, :],
                                      widx[:, 64:128], 1024, 1024, D), 16)
            p.gps(ggx)
            gix_m = GIX.n

            for g in range(16):
                def vadd(v, g=g, gix_m=gix_m):
                    v.wait_ge(GIX.h, gix_m)
                    VE.inc(v.tensor_tensor(out=moe_acc[:, g, :], in0=moe_acc[:, g, :],
                                           in1=rbuf[:, g, :], op=ALU.add))
                p.vec(vadd)
                ve_add = VE.n

                def smo(sy, g=g, ve_add=ve_add):
                    sy.wait_ge(VE.h, ve_add)
                    SMO.inc(sy.dma_start(out=moe_o[g * 128:(g + 1) * 128, :],
                                         in_=moe_acc[:, g, :]), 16)
                p.sync(smo)

            def fin(sy):
                sy.wait_ge(SMO.h, SMO.n)
                DONE.inc(sy.nop())
            p.sync(fin)

    return _lower_ext(nc) if standalone else (nc, (DONE.h, DONE.n))


def _prep_B_inputs(nm_by_core, router_w, gate_w, up_w, down_w, sh_gate, sh_up, sh_down):
    f = lambda a: np.ascontiguousarray(np.asarray(a, np.float32))
    us = np.triu(np.ones((128, 128), np.float32), 1)
    gi, ei = np.divmod(np.arange(128), 8)
    bs = ((ei[:, None] == ei[None, :]) & (gi[:, None] < gi[None, :])).astype(np.float32)
    ec = (ei * C0).astype(np.float32)[None, :]
    maps = []
    for c in range(NC):
        maps.append({"nm_own": f(nm_by_core[c]), "router_w": f(router_w), "gate_w": f(gate_w),
                     "up_w": f(up_w), "down_w": f(down_w), "sh_gate": f(sh_gate),
                     "sh_up": f(sh_up), "sh_down": f(sh_down), "Ustrict": us,
                     "blkstrict": bs, "ecrow": ec})
    return maps



def build_C(nc=None, ext=None, entry=(), fused=False):
    standalone = nc is None
    if standalone:
        nc = bass.Bass("TRN2", target_bir_lowering=False, detect_race_conditions=False)
        ext = {}
        ext["x_own"] = nc.dram_tensor("x_own", [TOK, D], F32, kind="ExternalInput").ap()
        ext["xn_own"] = nc.dram_tensor("xn_own", [TOK, D], F32, kind="ExternalInput").ap()
        ext["att_own"] = nc.dram_tensor("att_own", [TOK, D], F32, kind="ExternalInput").ap()
        ext["moe_own"] = nc.dram_tensor("moe_own", [TOK, D], F32, kind="ExternalInput").ap()
        ext["M_bind"] = nc.dram_tensor("M_bind", [D, D], F32, kind="ExternalInput").ap()
        ext["gw1T"] = nc.dram_tensor("gw1T", [D, E], F32, kind="ExternalInput").ap()
        ext["lgam"] = nc.dram_tensor("lgam", [D], F32, kind="ExternalInput").ap()
        ext["lbeta"] = nc.dram_tensor("lbeta", [D], F32, kind="ExternalInput").ap()
        ext["x1_own"] = nc.dram_tensor("x1_own", [TOK, D], F32, kind="ExternalOutput").ap()
        ext["xn1_own"] = nc.dram_tensor("xn1_own", [TOK, D], F32, kind="ExternalOutput").ap()
        ext["sraw_own"] = nc.dram_tensor("sraw_own", [TOK, E], F32, kind="ExternalOutput").ap()
        ext["boundT_own"] = nc.dram_tensor("boundT_own", [D, TOK], F32, kind="ExternalOutput").ap()
        ext["z1_own"] = nc.dram_tensor("z1_own", [TOK, D], BF16, kind="ExternalOutput").ap()
    x_d = ext["x_own"]
    xn_d = ext["xn_own"]
    att_d = ext["att_own"]
    moe_d = ext["moe_own"]
    mb_d = ext["M_bind"]
    gw_d = ext["gw1T"]
    lg_d = ext["lgam"]
    lb_d = ext["lbeta"]
    x1_o = ext["x1_own"]
    xn1_o = ext["xn1_own"]
    sr_o = ext["sraw_own"]
    bt_o = ext.get("boundT_own")
    z1_o = ext["z1_own"]

    es = ExitStack()
    sb = lambda n, s_, d_: es.enter_context(nc.sbuf_tensor("C" + n, s_, d_))
    psa = lambda n, s_, d_: es.enter_context(nc.psum_tensor("C" + n, s_, d_))
    with es:
        ones_row = sb("ones_row", [1, 128], F32)
        ident_bf = sb("ident_bf", [128, 128], BF16)
        ident_f = sb("ident_f", [128, 128], F32)
        pid = sb("pid", [128, 1], F32)
        cid = sb("cid", [128, 128], F32)
        xs = sb("xs", [128, 2, D], F32)
        ys = sb("ys", [128, 2, D], F32)
        res = sb("res", [128, 2, D], F32)
        xn1f = sb("xn1f", [128, 2, D], F32)
        xn1b = sb("xn1b", [128, D], BF16)
        n2 = sb("n2", [128, D], F32)
        z1t = sb("z1t", [128, 2, D], BF16)
        xn1T = sb("xn1T", [128, 8, TOK], BF16)
        xT32 = sb("xT32", [128, 8, 128], F32)
        gw32 = sb("gw32", [128, 8, E], F32)
        mb_s = sb("mb_s", [128, 8, D], BF16)
        lgr = sb("lgr", [128, D], F32)
        lbr = sb("lbr", [128, D], F32)
        lg_row = sb("lg_row", [1, D], F32)
        lb_row = sb("lb_row", [1, D], F32)
        sr_sb = sb("sr_sb", [128, 16, E], F32)
        btile = sb("btile", [128, 2, 2048], F32)
        prov = sb("prov", [128, 2048], F32)
        hbt = sb("hbt", [128, 2, 2048], BF16)
        dtile = sb("dtile", [128, 2048], F32)
        carry_sb = sb("carry_sb", [128, 8], F32)
        dsg = sb("dsg", [1, 1], F32)
        drep = sb("drep", [128, 1], F32)
        gapt = sb("gapt", [128, 1], F32)
        lns = dict(musum=sb("musum", [128, 1], F32), negmu=sb("negmu", [128, 1], F32),
                   ssq=sb("ssq", [128, 1], F32), veps=sb("veps", [128, 1], F32),
                   sd=sb("sd", [128, 1], F32), rstd=sb("rstd", [128, 1], F32),
                   d=sb("lnd", [128, D], F32), sq=sb("sq", [128, D], F32),
                   ex2=sb("lnex2", [128, 1], F32), musq=sb("lnmusq", [128, 1], F32),
                   gap=sb("lngap", [128, 1], F32))
        psA = psa("psA", [128, 512], F32)
        psB = psa("psB", [128, 512], BF16)
        psC = psa("psC", [128, 512], F32)
        psD = psa("psD", [128, 512], F32)
        psE = psa("psE", [128, 512], F32)
        psF = psa("psF", [128, 512], F32)

        with nc.Block() as block:
            p = ext["_P"].bind(block) if "_P" in ext else P(nc, es, block, pfx="C_")
            DONE = p.S("DONE")
            for _eng in [block.sync, block.gpsimd, block.vector, block.scalar, block.tensor]:
                def _bar(e, entry=entry):
                    for _sh, _cnt in entry:
                        e.wait_ge(_sh, _cnt)
                _eng(_bar)
            PE, VE, AC, PL = p.PE, p.VE, p.AC, p.PL
            LW = p.S("LW"); LG = p.S("LG")
            LD1 = [p.S("LA0"), p.S("LA1")]
            LD2 = [p.S("LB0"), p.S("LB1")]
            LD3 = [p.S("LC0"), p.S("LC1")]
            LD4 = [p.S("LDD0"), p.S("LDD1")]
            SX1 = [p.S("SX10"), p.S("SX11")]
            SXN = [p.S("SXN0"), p.S("SXN1")]
            SZ1 = [p.S("SZ10"), p.S("SZ11")]
            SBT = p.S("SBT")
            SSR = p.S("SSR")

            def gc(g):
                g.iota(out=pid[:], pattern=[[0, 1]], base=0, channel_multiplier=1,
                       allow_small_or_imprecise_dtypes=True)
                PL.inc(g.iota(out=cid[:], pattern=[[1, 128]], base=0, channel_multiplier=0,
                              allow_small_or_imprecise_dtypes=True))
                LG.inc(g.dma_start(out=mb_s[:], in_=mb_d.rearrange("(c p) j -> p c j", p=128)), 16)
            p.gps(gc)
            pl_c = PL.n
            lg_c = LG.n

            def sc(sy):
                LW.inc(sy.dma_start(out=gw32[:], in_=gw_d.rearrange("(c p) e -> p c e", p=128)), 16)
                LW.inc(sy.dma_start(out=lg_row[:], in_=lg_d[None, :]), 16)
                LW.inc(sy.dma_start(out=lb_row[:], in_=lb_d[None, :]), 16)
            p.sync(sc)
            lw_c = LW.n

            def vc(v):
                v.wait_ge(PL.h, pl_c)
                v.memset(ones_row[:], 1.0)
                v.memset(gapt[:], 0.0)
                v.memset(lns["gap"][:], 0.0)
                v.tensor_scalar(out=ident_bf[:], in0=cid[:], scalar1=pid[:, :],
                                scalar2=None, op0=ALU.is_equal)
                VE.inc(v.tensor_scalar(out=ident_f[:], in0=cid[:], scalar1=pid[:, :],
                                       scalar2=None, op0=ALU.is_equal))
            p.vec(vc)
            ve_cst = VE.n

            prev_rep = 0
            for (row, rep) in [(lg_row, lgr), (lb_row, lbr)]:
                for i in range(2):
                    def tr(t, row=row, i=i, prev_rep=prev_rep, ve_cst=ve_cst, lw_c=lw_c):
                        t.wait_ge(VE.h, max(ve_cst, prev_rep))
                        t.wait_ge(LW.h, lw_c)
                        PE.inc(t.matmul(out=psA[:, 0:512], lhsT=ones_row[:],
                                        rhs=row[0:1, i * 512:(i + 1) * 512], start=True, stop=True))
                    p.ten(tr)
                    mpe = PE.n

                    def vr(v, rep=rep, i=i, mpe=mpe):
                        v.wait_ge(PE.h, mpe)
                        VE.inc(v.tensor_copy(out=rep[:, i * 512:(i + 1) * 512], in_=psA[:, 0:512]))
                    p.vec(vr)
                    prev_rep = VE.n
            rep_done = VE.n

            xs_free = [0, 0]
            xs_free_pe = [0, 0]
            xn1b_free = [0]
            st_x1 = [0, 0]
            st_xn = [0, 0]
            st_z1 = [0, 0]
            for g in range(16):
                pp = g % 2

                def syl(sy, g=g, pp=pp, m=xs_free[pp], mp=xs_free_pe[pp],
                        s1=st_x1[pp], s2=st_xn[pp]):
                    sy.wait_ge(VE.h, m)
                    sy.wait_ge(PE.h, mp)
                    sy.wait_ge(SX1[pp].h, s1)
                    sy.wait_ge(SXN[pp].h, s2)
                    LD1[pp].inc(sy.dma_start(out=xs[:, pp, :],
                                             in_=x_d[g * 128:(g + 1) * 128, :]), 16)
                    LD2[pp].inc(sy.dma_start(out=ys[:, pp, :],
                                             in_=xn_d[g * 128:(g + 1) * 128, :]), 16)
                p.sync(syl)
                l1, l2 = LD1[pp].n, LD2[pp].n

                def vr1(v, pp=pp, l1=l1, l2=l2):
                    v.wait_ge(LD1[pp].h, l1)
                    v.wait_ge(LD2[pp].h, l2)
                    VE.inc(v.tensor_tensor(out=res[:, pp, :], in0=xs[:, pp, :], in1=ys[:, pp, :],
                                           op=ALU.add))
                p.vec(vr1)
                ve_r1 = VE.n

                def syl2(sy, g=g, pp=pp, ve_r1=ve_r1):
                    sy.wait_ge(VE.h, ve_r1)
                    LD3[pp].inc(sy.dma_start(out=xs[:, pp, :],
                                             in_=att_d[g * 128:(g + 1) * 128, :]), 16)
                    LD4[pp].inc(sy.dma_start(out=ys[:, pp, :],
                                             in_=moe_d[g * 128:(g + 1) * 128, :]), 16)
                p.sync(syl2)
                l3, l4 = LD3[pp].n, LD4[pp].n

                def vr2(v, pp=pp, l3=l3, l4=l4):
                    v.wait_ge(LD3[pp].h, l3)
                    v.wait_ge(LD4[pp].h, l4)
                    v.tensor_tensor(out=res[:, pp, :], in0=res[:, pp, :], in1=xs[:, pp, :],
                                    op=ALU.add)
                    VE.inc(v.tensor_tensor(out=res[:, pp, :], in0=res[:, pp, :], in1=ys[:, pp, :],
                                           op=ALU.add))
                p.vec(vr2)
                ve_x1 = VE.n
                xs_free[pp] = ve_x1

                def sx1(sy, g=g, pp=pp, ve_x1=ve_x1):
                    sy.wait_ge(VE.h, ve_x1)
                    SX1[pp].inc(sy.dma_start(out=x1_o[g * 128:(g + 1) * 128, :],
                                             in_=res[:, pp, :]), 16)
                p.sync(sx1)
                st_x1[pp] = SX1[pp].n

                emit_ln(p, res[:, pp, :], xn1f[:, pp, :], lns,
                        out_waits=[(SXN[pp], st_xn[pp]), (PE, xs_free_pe[pp])])
                ve_xn1 = VE.n

                def sxn(sy, g=g, pp=pp, ve_xn1=ve_xn1):
                    sy.wait_ge(VE.h, ve_xn1)
                    SXN[pp].inc(sy.dma_start(out=xn1_o[g * 128:(g + 1) * 128, :],
                                             in_=xn1f[:, pp, :]), 16)
                p.sync(sxn)
                st_xn[pp] = SXN[pp].n

                def vb(v, pp=pp, pm=xn1b_free[0]):
                    v.wait_ge(PE.h, pm)
                    VE.inc(v.tensor_copy(out=xn1b[:], in_=xn1f[:, pp, :]))
                p.vec(vb)
                ve_bb = VE.n

                evm = []
                for half in range(2):
                    wv = evm[-1] if evm else None

                    def tt(t, half=half, ve_bb=ve_bb, wv=wv):
                        t.wait_ge(VE.h, ve_bb if wv is None else wv)
                        for i in range(4):
                            blk = half * 4 + i
                            PE.inc(t.transpose(out=psB[:, i * 128:(i + 1) * 128],
                                               in_=xn1b[:, blk * 128:(blk + 1) * 128],
                                               identity=ident_bf[:]))
                    p.ten(tt)
                    pe_t = PE.n

                    def vt(v, g=g, half=half, pe_t=pe_t):
                        v.wait_ge(PE.h, pe_t)
                        for i in range(4):
                            blk = half * 4 + i
                            VE.inc(v.tensor_copy(out=xn1T[:, blk, g * 128:(g + 1) * 128],
                                                 in_=psB[:, i * 128:(i + 1) * 128]))
                    p.vec(vt)
                    evm.append(VE.n)
                xn1b_free[0] = PE.n

                evm2 = []
                for half in range(2):
                    wv2 = evm2[-1] if evm2 else None

                    def tt32(t, pp=pp, half=half, wv2=wv2, m=VE.n):
                        t.wait_ge(VE.h, m if wv2 is None else wv2)
                        for i in range(4):
                            blk = half * 4 + i
                            PE.inc(t.transpose(out=psC[:, i * 128:(i + 1) * 128],
                                               in_=xn1f[:, pp, blk * 128:(blk + 1) * 128],
                                               identity=ident_f[:]))
                    p.ten(tt32)
                    pe_t32 = PE.n

                    def vt32(v, half=half, pe_t32=pe_t32):
                        v.wait_ge(PE.h, pe_t32)
                        for i in range(4):
                            blk = half * 4 + i
                            VE.inc(v.tensor_copy(out=xT32[:, blk, :],
                                                 in_=psC[:, i * 128:(i + 1) * 128]))
                    p.vec(vt32)
                    evm2.append(VE.n)
                ve_x32 = VE.n
                xs_free_pe[pp] = PE.n

                def ts_(t, ve_x32=ve_x32, lw_c=lw_c):
                    t.wait_ge(VE.h, ve_x32)
                    t.wait_ge(LW.h, lw_c)
                    for c in range(8):
                        PE.inc(t.matmul(out=psA[:, 0:8], lhsT=xT32[:, c, :], rhs=gw32[:, c, :],
                                        start=(c == 0), stop=(c == 7)))
                p.ten(ts_)
                pe_s = PE.n

                def vs_(v, g=g, pe_s=pe_s):
                    v.wait_ge(PE.h, pe_s)
                    v.tensor_copy(out=gapt[:], in_=gapt[:])
                    VE.inc(v.tensor_copy(out=sr_sb[:, g, :], in_=psA[:, 0:8]))
                p.vec(vs_)

                emit_ln(p, xn1f[:, pp, :], n2[:], lns)

                def vaf(v, pp=pp, rep_done=rep_done, sz=st_z1[pp]):
                    v.wait_ge(VE.h, rep_done)
                    v.wait_ge(SZ1[pp].h, sz)
                    v.tensor_tensor(out=n2[:], in0=n2[:], in1=lgr[:], op=ALU.mult)
                    VE.inc(v.tensor_tensor(out=z1t[:, pp, :], in0=n2[:], in1=lbr[:], op=ALU.add))
                p.vec(vaf)
                ve_z1 = VE.n

                def sz1(sy, g=g, pp=pp, ve_z1=ve_z1):
                    sy.wait_ge(VE.h, ve_z1)
                    SZ1[pp].inc(sy.dma_start(out=z1_o[g * 128:(g + 1) * 128, :],
                                             in_=z1t[:, pp, :]), 16)
                p.sync(sz1)
                st_z1[pp] = SZ1[pp].n

            ve_sr = VE.n

            def ssr(sy, ve_sr=ve_sr):
                sy.wait_ge(VE.h, ve_sr)
                SSR.inc(sy.dma_start(out=sr_o.rearrange("(g p) e -> p g e", p=128),
                                     in_=sr_sb[:]), 16)
            p.sync(ssr)

            if fused:
                CCe = p.CC

                def gcc1(g, m0=SZ1[0].n, m1=SZ1[1].n, msr=SSR.n):
                    g.wait_ge(SZ1[0].h, m0)
                    g.wait_ge(SZ1[1].h, m1)
                    g.wait_ge(SSR.h, msr)
                    CCe.inc(g.collective_compute(
                        "AllGather", ALU.bypass, replica_groups=[list(range(NC))],
                        ins=[ext["z1_own"].opt()], outs=[ext["z1ag"].opt()]))
                    CCe.inc(g.collective_compute(
                        "AllGather", ALU.bypass, replica_groups=[list(range(NC))],
                        ins=[ext["sraw_own"].opt()], outs=[ext["srawag"].opt()]))
                p.gps(gcc1)

            if fused:
                dec_ap = ext["decay"]
                LDC = p.S("LDC")

                def sdc(sy):
                    LDC.inc(sy.dma_start(out=dsg[:], in_=dec_ap[:, :]), 16)
                p.sync(sdc)
                ldc_c = LDC.n

                def sdc2(sc_, ldc_c=ldc_c):
                    sc_.wait_ge(LDC.h, ldc_c)
                    AC.inc(sc_.activation(out=dsg[:], in_=dsg[:], func=AF.Sigmoid))
                p.sca(sdc2)
                ac_dd = AC.n

                def tdc(t, ac_dd=ac_dd):
                    t.wait_ge(AC.h, ac_dd)
                    PE.inc(t.matmul(out=psA[:, 0:1], lhsT=ones_row[:], rhs=dsg[:],
                                    start=True, stop=True))
                p.ten(tdc)
                pe_dd = PE.n

                def vdc(v, pe_dd=pe_dd):
                    v.wait_ge(PE.h, pe_dd)
                    v.memset(dtile[:], 1.0)
                    v.tensor_copy(out=gapt[:], in_=gapt[:])
                    v.tensor_copy(out=drep[:], in_=psA[:, 0:1])
                    v.tensor_copy(out=gapt[:], in_=gapt[:])
                    VE.inc(v.tensor_scalar(out=dtile[:], in0=dtile[:], scalar1=drep[:, :],
                                           scalar2=None, op0=ALU.mult))
                p.vec(vdc)
                ve_dd = VE.n

            bt_free = [0, 0]
            for j in range(8):
                pp = j % 2

                def tbd(t, j=j, pp=pp, lg_c=lg_c, m=bt_free[pp], ve_all=VE.n):
                    t.wait_ge(LG.h, lg_c)
                    t.wait_ge(VE.h, ve_all)
                    t.wait_ge(SBT.h, m)
                    for qc in range(4):
                        for c in range(8):
                            PE.inc(t.matmul(out=[psC, psD, psE, psF][qc][:, 0:512],
                                            lhsT=mb_s[:, c, j * 128:(j + 1) * 128],
                                            rhs=xn1T[:, c, qc * 512:(qc + 1) * 512],
                                            start=(c == 0), stop=(c == 7)))
                p.ten(tbd)
                pe_b = PE.n

                def vbd(v, j=j, pp=pp, pe_b=pe_b):
                    v.wait_ge(PE.h, pe_b)
                    for qc in range(4):
                        VE.inc(v.tensor_copy(out=btile[:, pp, qc * 512:(qc + 1) * 512],
                                             in_=[psC, psD, psE, psF][qc][:, 0:512]))
                p.vec(vbd)
                ve_bt = VE.n

                if not fused:
                    def sbt(sy, j=j, pp=pp, ve_bt=ve_bt):
                        sy.wait_ge(VE.h, ve_bt)
                        SBT.inc(sy.dma_start(out=bt_o[j * 128:(j + 1) * 128, :],
                                             in_=btile[:, pp, :]), 16)
                    p.sync(sbt)
                    bt_free[pp] = SBT.n
                else:
                    def vscn(v, j=j, pp=pp, ve_dd=ve_dd, m=SBT.n):
                        v.wait_ge(VE.h, ve_dd)
                        v.wait_ge(SBT.h, m)
                        v.tensor_copy(out=gapt[:], in_=gapt[:])
                        v.tensor_copy(out=gapt[:], in_=gapt[:])
                        v.tensor_tensor_scan(out=prov[:], data0=dtile[:], data1=btile[:, pp, :],
                                             initial=0.0, op0=ALU.mult, op1=ALU.add)
                        v.tensor_copy(out=gapt[:], in_=gapt[:])
                        v.tensor_copy(out=carry_sb[:, j:j + 1], in_=prov[:, 2047:2048])
                        VE.inc(v.tensor_copy(out=hbt[:, pp, :], in_=prov[:]))
                    p.vec(vscn)
                    ve_sc = VE.n

                    def sht(sy, j=j, pp=pp, ve_sc=ve_sc):
                        sy.wait_ge(VE.h, ve_sc)
                        SBT.inc(sy.dma_start(out=ext["holoT_bf"][j * 128:(j + 1) * 128, :],
                                             in_=hbt[:, pp, :]), 16)
                    p.sync(sht)
                    bt_free[pp] = SBT.n

            if fused:
                ve_cy = VE.n
                SCY = p.S("SCY")
                CC = p.CC

                def scy(sy, ve_cy=ve_cy):
                    sy.wait_ge(VE.h, ve_cy)
                    SCY.inc(sy.dma_start(
                        out=ext["carr_my"][0, :].rearrange("(pp jj) -> pp jj", pp=128),
                        in_=carry_sb[:]), 16)
                p.sync(scy)
                scy_c = SCY.n

                def gcc(g, scy_c=scy_c):
                    g.wait_ge(SCY.h, scy_c)
                    CC.inc(g.collective_compute(
                        "AllGather", ALU.bypass, replica_groups=[list(range(NC))],
                        ins=[ext["carr_my"].opt()], outs=[ext["carrag"].opt()]))
                p.gps(gcc)

            def fin(sy):
                sy.wait_ge(SBT.h, SBT.n)
                sy.wait_ge(SSR.h, SSR.n)
                for pp in range(2):
                    sy.wait_ge(SX1[pp].h, SX1[pp].n)
                    sy.wait_ge(SXN[pp].h, SXN[pp].n)
                    sy.wait_ge(SZ1[pp].h, SZ1[pp].n)
                if fused:
                    sy.wait_ge(p.CC.h, p.CC.n)
                DONE.inc(sy.nop())
            p.sync(fin)

    return _lower_ext(nc) if standalone else (nc, (DONE.h, DONE.n))



def build_D(nc=None, ext=None, entry=()):
    standalone = nc is None
    if standalone:
        nc = bass.Bass("TRN2", target_bir_lowering=False, detect_race_conditions=False)
        ext = {}
        ext["bound_units"] = nc.dram_tensor("bound_units", [4, 128, S], F32, kind="ExternalInput").ap()
        ext["decay"] = nc.dram_tensor("decay", [1, 1], F32, kind="ExternalInput").ap()
        ext["holo_units"] = nc.dram_tensor("holo_units", [4, 128, S], F32, kind="ExternalOutput").ap()
    bu_d = ext["bound_units"]
    dec_d = ext["decay"]
    ho_o = ext["holo_units"]

    es = ExitStack()
    sb = lambda n, s_, d_: es.enter_context(nc.sbuf_tensor("D" + n, s_, d_))
    with es:
        dval = sb("dval", [1, 1], F32)
        dsg = sb("dsg", [1, 1], F32)
        drep = sb("drep", [128, 1], F32)
        ones_row = sb("ones_row", [1, 128], F32)
        dtile = sb("dtile", [128, S], F32)
        bsb = sb("bsb", [128, 2, S], F32)
        hsb = sb("hsb", [128, 2, S], F32)
        gapt = sb("gapt", [128, 1], F32)
        psA = es.enter_context(nc.psum_tensor("DpsA", [128, 512], F32))

        with nc.Block() as block:
            p = ext["_P"].bind(block) if "_P" in ext else P(nc, es, block, pfx="E_")
            DONE = p.S("DONE")
            for _eng in [block.sync, block.gpsimd, block.vector, block.scalar, block.tensor]:
                def _bar(e, entry=entry):
                    for _sh, _cnt in entry:
                        e.wait_ge(_sh, _cnt)
                _eng(_bar)
            PE, VE, AC = p.PE, p.VE, p.AC
            LW = p.S("LW")
            LDu = [p.S("LD0"), p.S("LD1")]
            STu = [p.S("ST0"), p.S("ST1")]

            def sc(sy):
                LW.inc(sy.dma_start(out=dval[:], in_=dec_d[:, :]), 16)
            p.sync(sc)
            lw_c = LW.n

            def s1(s):
                s.wait_ge(LW.h, lw_c)
                AC.inc(s.activation(out=dsg[:], in_=dval[:], func=AF.Sigmoid))
            p.sca(s1)
            ac_d = AC.n

            def v0(v, ac_d=ac_d):
                v.wait_ge(AC.h, ac_d)
                v.memset(ones_row[:], 1.0)
                v.memset(gapt[:], 0.0)
                VE.inc(v.memset(dtile[:], 1.0))
            p.vec(v0)
            ve_0 = VE.n

            def t0(t, ve_0=ve_0):
                t.wait_ge(VE.h, ve_0)
                PE.inc(t.matmul(out=psA[:, 0:1], lhsT=ones_row[:], rhs=dsg[:], start=True,
                                stop=True))
            p.ten(t0)
            pe_0 = PE.n

            def v1(v, pe_0=pe_0):
                v.wait_ge(PE.h, pe_0)
                v.tensor_copy(out=gapt[:], in_=gapt[:])
                v.tensor_copy(out=drep[:], in_=psA[:, 0:1])
                v.tensor_copy(out=gapt[:], in_=gapt[:])
                VE.inc(v.tensor_scalar(out=dtile[:], in0=dtile[:], scalar1=drep[:, :],
                                       scalar2=None, op0=ALU.mult))
            p.vec(v1)
            ve_d = VE.n

            st_m = [0, 0]
            for u in range(4):
                pp = u % 2

                def syl(sy, u=u, pp=pp, m=st_m[pp]):
                    sy.wait_ge(STu[pp].h, m)
                    LDu[pp].inc(sy.dma_start(out=bsb[:, pp, :], in_=bu_d[u, :, :]), 16)
                p.sync(syl)
                lm = LDu[pp].n

                def vsc(v, u=u, pp=pp, lm=lm, ve_d=ve_d):
                    v.wait_ge(LDu[pp].h, lm)
                    v.wait_ge(VE.h, ve_d)
                    VE.inc(v.tensor_tensor_scan(out=hsb[:, pp, :], data0=dtile[:],
                                                data1=bsb[:, pp, :], initial=0.0,
                                                op0=ALU.mult, op1=ALU.add))
                p.vec(vsc)
                vm = VE.n

                def sst(sy, u=u, pp=pp, vm=vm):
                    sy.wait_ge(VE.h, vm)
                    STu[pp].inc(sy.dma_start(out=ho_o[u, :, :], in_=hsb[:, pp, :]), 16)
                p.sync(sst)
                st_m[pp] = STu[pp].n

            def fin(sy):
                sy.wait_ge(STu[0].h, STu[0].n)
                sy.wait_ge(STu[1].h, STu[1].n)
                DONE.inc(sy.nop())
            p.sync(fin)

    return _lower_ext(nc) if standalone else (nc, (DONE.h, DONE.n))



NTOK = NC * TOK

def build_E(nc=None, ext=None, entry=(), fused=False):
    standalone = nc is None
    if standalone:
        nc = bass.Bass("TRN2", target_bir_lowering=False, detect_race_conditions=False)
        ext = {}
        ext["z1_all"] = nc.dram_tensor("z1_all", [NTOK, D], BF16, kind="ExternalInput").ap()
        ext["scol"] = nc.dram_tensor("scol", [NTOK, 1], F32, kind="ExternalInput").ap()
        ext["fc1T"] = nc.dram_tensor("fc1T", [D, HID1], F32, kind="ExternalInput").ap()
        ext["b1"] = nc.dram_tensor("b1", [128, HID1 // 128], F32, kind="ExternalInput").ap()
        ext["fc2T"] = nc.dram_tensor("fc2T", [HID1, D], F32, kind="ExternalInput").ap()
        ext["b2"] = nc.dram_tensor("b2", [D], F32, kind="ExternalInput").ap()
        ext["Ustrict"] = nc.dram_tensor("Ustrict", [128, 128], F32, kind="ExternalInput").ap()
        ext["opack"] = nc.dram_tensor("opack", [K1, D], BF16, kind="ExternalOutput").ap()
        ext["tsel"] = nc.dram_tensor("tsel", [128, 16], U32, kind="ExternalOutput").ap()
    zall_d = ext["z1_all"]
    scol_d = ext.get("scol")
    f1_d = ext["fc1T"]
    b1_d = ext["b1"]
    f2_d = ext["fc2T"]
    b2_d = ext["b2"]
    us_d = ext["Ustrict"]
    op_o = ext.get("opack")
    ts_o = ext.get("tsel")
    fa_dram = nc.dram_tensor("E_fa_scr", [NTOK], F32)
    f2bf = nc.dram_tensor("E_f2bf", [HID1, D], BF16).ap()

    es = ExitStack()
    sb = lambda n, s_, d_: es.enter_context(nc.sbuf_tensor("E" + n, s_, d_))
    psa = lambda n, s_, d_: es.enter_context(nc.psum_tensor("D" + n, s_, d_))
    with es:
        ones_row = sb("ones_row", [1, 128], F32)
        onescol = sb("onescol", [128, 1], F32)
        onetile = sb("onetile", [128, 128], F32)
        ident_bf = sb("ident_bf", [128, 128], BF16)
        pid = sb("pid", [128, 1], F32)
        cid = sb("cid", [128, 128], F32)
        us_s = sb("us_s", [128, 128], F32)
        s_sb = sb("s_sb", [128, 128], F32)
        sr8 = sb("sr8", [128, 1024], F32)
        wr8 = sb("wr8", [128, 8], F32)
        esel8_sb = sb("esel8_sb", [128, 8], F32)
        eselt_sb = sb("eselt_sb", [128, 1024], F32)
        zbf = sb("zbf", [128, D], BF16)
        maskb = sb("maskb", [128, 128], F32)
        maskhi = sb("maskhi", [128, 128], F32)
        maskeq = sb("maskeq", [128, 128], F32)
        incl = sb("incl", [128, 128], F32)
        rankeq = sb("rankeq", [128, 128], F32)
        mask = sb("mask", [128, 128], F32)
        rank = sb("rank", [128, 128], F32)
        Atile = sb("Atile", [128, 128], F32)
        rowc = sb("rowc", [128, 1], F32)
        rowoff = sb("rowoff", [128, 1], F32)
        threp = sb("threp", [128, 1], F32)
        hirep = sb("hirep", [128, 1], F32)
        lorep = sb("lorep", [128, 1], F32)
        needrep = sb("needrep", [128, 1], F32)
        st = sb("st", [1, 16], F32)
        gapt = sb("gapt", [128, 1], F32)
        rowbuf = sb("rowbuf", [1, 2, 512], F32)
        Arep = sb("Arep", [128, NTOK], F32)
        iot = sb("iot", [128, 2, 8], F32)
        idx2 = sb("idx2", [128, 2, 8], U32)
        tokf = sb("tokf", [128, 16], F32)
        widx = sb("widx", [128, 128], I16)
        ident_f = sb("ident_f", [128, 128], F32)
        zselA = sb("zselA", [128, 2, D], BF16)
        zselB = sb("zselB", [128, 2, D], BF16)
        zT = sb("zT", [128, 8, K1], BF16)
        wraw = sb("wraw", [128, 16], F32)
        wsel = sb("wsel", [128, 16], F32)
        b1_sb = sb("b1_sb", [128, 32], F32)
        b2row = sb("b2row", [1, D], F32)
        b2rep = sb("b2rep", [128, D], F32)
        f1p = sb("f1p", [128, 2, 8, 128], BF16)
        f2p = sb("f2p", [128, 2, 512], BF16)
        hmidQ = sb("hmidQ", [128, 32, 512], BF16)
        xg = sb("xg", [128, 512], F32)
        aa = sb("aa", [128, 512], F32)
        w1 = sb("w1", [128, 512], F32)
        c2 = sb("c2", [128, 512], F32)
        sg = sb("sg", [128, 512], F32)
        tvv = sb("tvv", [128, 1024], F32)
        orow4 = sb("orow4", [128, 4, 1024], BF16)
        pT = psa("pT", [128, 512], BF16)
        p1 = psa("p1", [128, 512], F32)
        p2 = psa("p2", [128, 512], F32)
        p3 = psa("p3", [128, 512], F32)
        p4 = psa("p4", [128, 512], F32)
        PSC = [p1, p2, p3, p4]

        with nc.Block() as block:
            p = ext["_P"].bind(block) if "_P" in ext else P(nc, es, block, pfx="F_")
            DONE = p.S("DONE")
            for _eng in [block.sync, block.gpsimd, block.vector, block.scalar, block.tensor]:
                def _bar(e, entry=entry):
                    for _sh, _cnt in entry:
                        e.wait_ge(_sh, _cnt)
                _eng(_bar)
            PE, VE, AC, PL = p.PE, p.VE, p.AC, p.PL
            LW = p.S("LW"); LG = p.S("LG")
            LZ = [p.S("LZ0"), p.S("LZ1")]
            LS = p.S("LS")
            LF1 = [p.S("LF10"), p.S("LF11")]
            LF2 = [p.S("LF20"), p.S("LF21")]
            SOP = p.S("SOP"); STS = p.S("STS")
            DFA = p.S("DFA"); SSC = p.S("SSC")
            ssc_c = 0

            def gc(g):
                g.iota(out=pid[:], pattern=[[0, 1]], base=0, channel_multiplier=1,
                       allow_small_or_imprecise_dtypes=True)
                g.iota(out=cid[:], pattern=[[1, 128]], base=0, channel_multiplier=0,
                       allow_small_or_imprecise_dtypes=True)
                g.iota(out=iot[:, 0, :], pattern=[[128, 8]], base=0, channel_multiplier=1,
                       allow_small_or_imprecise_dtypes=True)
                PL.inc(g.iota(out=iot[:, 1, :], pattern=[[128, 8]], base=1024,
                              channel_multiplier=1, allow_small_or_imprecise_dtypes=True))
                LG.inc(g.dma_start(out=b1_sb[:], in_=b1_d[:, :]), 16)
            p.gps(gc)
            pl_c = PL.n
            lg_c = LG.n

            def sc(sy):
                LW.inc(sy.dma_start(out=us_s[:], in_=us_d[:, :]), 16)
                LW.inc(sy.dma_start(out=b2row[:], in_=b2_d[None, :]), 16)
                if not fused:
                    LS.inc(sy.dma_start(out=s_sb[:],
                                        in_=scol_d.rearrange("(p f) o -> p (f o)", p=128)), 16)
                else:
                    LS.inc(sy.dma_start(out=sr8[:],
                                        in_=ext["srawag"].rearrange("(p f) e -> p (f e)", p=128)),
                           16)
            p.sync(sc)
            lw_c = LW.n
            ls_c = LS.n
            if fused:
                ESL = p.S("ESL")

                def sesl(sy):
                    ESL.inc(sy.dma_start(out=esel8_sb[:], in_=ext["esel8"][:, :]), 16)
                    ESL.inc(sy.dma_start(out=eselt_sb[:], in_=ext["esel_tiled"][:, :]), 16)
                p.sync(sesl)
                esl_c = ESL.n

                def vex(v, ls_c=ls_c, esl_c=esl_c):
                    v.wait_ge(LS.h, ls_c)
                    v.wait_ge(ESL.h, esl_c)
                    v.tensor_tensor(out=sr8[:], in0=sr8[:], in1=eselt_sb[:], op=ALU.mult)
                    VE.inc(v.tensor_reduce(
                        out=s_sb[:], in_=sr8.ap().rearrange("p (f e) -> p f e", e=8),
                        axis=AXX, op=ALU.add))
                p.vec(vex)
                ls_c = VE.n
                scol_d = ext["scolE"]

                def sscd(sy, m=VE.n):
                    sy.wait_ge(VE.h, m)
                    SSC.inc(sy.dma_start(out=scol_d.rearrange("(pp f) o -> pp (f o)", pp=128),
                                         in_=s_sb[:]), 16)
                p.sync(sscd)
                ssc_c = SSC.n


            def vc(v):
                v.wait_ge(PL.h, pl_c)
                v.memset(ones_row[:], 1.0)
                v.memset(onescol[:], 1.0)
                v.memset(onetile[:], 1.0)
                v.memset(gapt[:], 0.0)
                v.memset(st[:], 0.0)
                v.memset(st[0:1, 0:1], -1000.0)
                v.memset(st[0:1, 1:2], 1000.0)
                if fused:
                    v.memset(zbf[:], 0.0)
                v.tensor_scalar(out=ident_f[:], in0=cid[:], scalar1=pid[:, :],
                                scalar2=None, op0=ALU.is_equal)
                VE.inc(v.tensor_scalar(out=ident_bf[:], in0=cid[:], scalar1=pid[:, :],
                                       scalar2=None, op0=ALU.is_equal))
            p.vec(vc)
            ve_cst = VE.n

            ZC = p.S("ZC")
            zc_all = 0
            if fused:
                def szc(sy, m=VE.n):
                    sy.wait_ge(VE.h, m)
                    for a in range(NTOK // 128):
                        ZC.inc(sy.dma_start(out=ext["contribZ"][a * 128:(a + 1) * 128, :],
                                            in_=zbf[:]), 16)
                p.sync(szc)
                zc_all = ZC.n

            prev = 0
            for i in range(2):
                def tr(t, i=i, prev=prev, ve_cst=ve_cst, lw_c=lw_c):
                    t.wait_ge(VE.h, max(ve_cst, prev))
                    t.wait_ge(LW.h, lw_c)
                    PE.inc(t.matmul(out=p1[:, 0:512], lhsT=ones_row[:],
                                    rhs=b2row[0:1, i * 512:(i + 1) * 512], start=True, stop=True))
                p.ten(tr)
                mpe = PE.n

                def vr(v, i=i, mpe=mpe):
                    v.wait_ge(PE.h, mpe)
                    VE.inc(v.tensor_copy(out=b2rep[:, i * 512:(i + 1) * 512], in_=p1[:, 0:512]))
                p.vec(vr)
                prev = VE.n

            lo, hi, th, cnt, sel = (st[0:1, k:k + 1] for k in range(5))
            ta, tcv, tb, td, ssum, ngt, need = (st[0:1, k:k + 1] for k in range(5, 12))
            ve_mark = VE.n
            for it in range(40):
                def t1_(t, m=ve_mark, lw_c=lw_c):
                    t.wait_ge(VE.h, m)
                    t.wait_ge(LW.h, lw_c)
                    PE.inc(t.matmul(out=p1[:, 0:1], lhsT=ones_row[:], rhs=th, start=True,
                                    stop=True))
                p.ten(t1_)
                pe1 = PE.n

                def v1_(v, pe1=pe1, ls_c=ls_c):
                    v.wait_ge(PE.h, pe1)
                    if fused:
                        v.wait_ge(VE.h, ls_c)
                    else:
                        v.wait_ge(LS.h, ls_c)
                    v.tensor_copy(out=gapt[:], in_=gapt[:])
                    v.tensor_copy(out=threp[:], in_=p1[:, 0:1])
                    v.tensor_copy(out=gapt[:], in_=gapt[:])
                    v.tensor_scalar(out=maskb[:], in0=s_sb[:], scalar1=threp[:, :],
                                    scalar2=None, op0=ALU.is_gt)
                    VE.inc(v.reduce_sum(out=rowc[:], in_=maskb[:], axis=AXX))
                p.vec(v1_)
                ve1 = VE.n

                def t2_(t, ve1=ve1):
                    t.wait_ge(VE.h, ve1)
                    PE.inc(t.matmul(out=p2[0:1, 0:1], lhsT=onescol[:], rhs=rowc[:],
                                    start=True, stop=True))
                p.ten(t2_)
                pe2 = PE.n

                def v2_(v, pe2=pe2):
                    v.wait_ge(PE.h, pe2)
                    v.tensor_copy(out=gapt[:], in_=gapt[:])
                    v.tensor_copy(out=cnt, in_=p2[0:1, 0:1])
                    v.tensor_tensor(out=ta, in0=th, in1=lo, op=ALU.subtract)
                    v.tensor_scalar(out=sel, in0=cnt, scalar1=float(K1), scalar2=None,
                                    op0=ALU.is_gt)
                    v.tensor_tensor(out=tcv, in0=hi, in1=th, op=ALU.subtract)
                    v.tensor_tensor(out=tb, in0=ta, in1=sel, op=ALU.mult)
                    v.tensor_tensor(out=td, in0=tcv, in1=sel, op=ALU.mult)
                    v.tensor_tensor(out=lo, in0=lo, in1=tb, op=ALU.add)
                    v.tensor_tensor(out=hi, in0=th, in1=td, op=ALU.add)
                    v.tensor_copy(out=gapt[:], in_=gapt[:])
                    v.tensor_tensor(out=ssum, in0=lo, in1=hi, op=ALU.add)
                    v.tensor_copy(out=gapt[:], in_=gapt[:])
                    VE.inc(v.tensor_scalar(out=th, in0=ssum, scalar1=0.5, scalar2=None,
                                           op0=ALU.mult))
                p.vec(v2_)
                ve_mark = VE.n

            def t3_(t, m=ve_mark):
                t.wait_ge(VE.h, m)
                PE.inc(t.matmul(out=p1[:, 0:1], lhsT=ones_row[:], rhs=hi, start=True, stop=True))
                PE.inc(t.matmul(out=p2[:, 0:1], lhsT=ones_row[:], rhs=lo, start=True, stop=True))
            p.ten(t3_)
            pe3 = PE.n

            def v3_(v, pe3=pe3):
                v.wait_ge(PE.h, pe3)
                v.tensor_copy(out=gapt[:], in_=gapt[:])
                v.tensor_copy(out=hirep[:], in_=p1[:, 0:1])
                v.tensor_copy(out=lorep[:], in_=p2[:, 0:1])
                v.tensor_copy(out=gapt[:], in_=gapt[:])
                v.tensor_scalar(out=maskhi[:], in0=s_sb[:], scalar1=hirep[:, :], scalar2=None,
                                op0=ALU.is_gt)
                v.tensor_scalar(out=maskb[:], in0=s_sb[:], scalar1=lorep[:, :], scalar2=None,
                                op0=ALU.is_gt)
                v.tensor_tensor(out=maskeq[:], in0=maskb[:], in1=maskhi[:], op=ALU.subtract)
                VE.inc(v.reduce_sum(out=rowc[:], in_=maskhi[:], axis=AXX))
            p.vec(v3_)
            ve3 = VE.n

            def t4_(t, ve3=ve3):
                t.wait_ge(VE.h, ve3)
                PE.inc(t.matmul(out=p3[0:1, 0:1], lhsT=onescol[:], rhs=rowc[:], start=True,
                                stop=True))
            p.ten(t4_)
            pe4 = PE.n

            def v4_(v, pe4=pe4):
                v.wait_ge(PE.h, pe4)
                v.tensor_copy(out=gapt[:], in_=gapt[:])
                v.tensor_copy(out=ngt, in_=p3[0:1, 0:1])
                v.tensor_copy(out=gapt[:], in_=gapt[:])
                v.tensor_scalar(out=need, in0=ngt, scalar1=-1.0, scalar2=float(K1),
                                op0=ALU.mult, op1=ALU.add)
                v.tensor_tensor_scan(out=incl[:], data0=onetile[:], data1=maskeq[:],
                                     initial=0.0, op0=ALU.mult, op1=ALU.add)
                v.tensor_tensor(out=rankeq[:], in0=incl[:], in1=maskeq[:], op=ALU.subtract)
                VE.inc(v.reduce_sum(out=rowc[:], in_=maskeq[:], axis=AXX))
            p.vec(v4_)
            ve4 = VE.n

            def t5_(t, ve4=ve4):
                t.wait_ge(VE.h, ve4)
                PE.inc(t.matmul(out=p4[:, 0:1], lhsT=us_s[:], rhs=rowc[:], start=True, stop=True))
                PE.inc(t.matmul(out=p1[:, 0:1], lhsT=ones_row[:], rhs=need, start=True, stop=True))
            p.ten(t5_)
            pe5 = PE.n

            def v5_(v, pe5=pe5):
                v.wait_ge(PE.h, pe5)
                v.tensor_copy(out=gapt[:], in_=gapt[:])
                v.tensor_copy(out=rowoff[:], in_=p4[:, 0:1])
                v.tensor_copy(out=needrep[:], in_=p1[:, 0:1])
                v.tensor_copy(out=gapt[:], in_=gapt[:])
                v.tensor_scalar(out=rankeq[:], in0=rankeq[:], scalar1=rowoff[:, :],
                                scalar2=None, op0=ALU.add)
                v.tensor_copy(out=gapt[:], in_=gapt[:])
                v.tensor_scalar(out=maskb[:], in0=rankeq[:], scalar1=needrep[:, :],
                                scalar2=None, op0=ALU.is_lt)
                v.tensor_tensor(out=maskb[:], in0=maskb[:], in1=maskeq[:], op=ALU.mult)
                v.tensor_tensor(out=mask[:], in0=maskhi[:], in1=maskb[:], op=ALU.add)
                v.tensor_tensor_scan(out=incl[:], data0=onetile[:], data1=mask[:],
                                     initial=0.0, op0=ALU.mult, op1=ALU.add)
                v.tensor_tensor(out=rank[:], in0=incl[:], in1=mask[:], op=ALU.subtract)
                VE.inc(v.reduce_sum(out=rowc[:], in_=mask[:], axis=AXX))
            p.vec(v5_)
            ve5 = VE.n

            def t6_(t, ve5=ve5):
                t.wait_ge(VE.h, ve5)
                PE.inc(t.matmul(out=p2[:, 0:1], lhsT=us_s[:], rhs=rowc[:], start=True, stop=True))
            p.ten(t6_)
            pe6 = PE.n

            def v6_(v, pe6=pe6):
                v.wait_ge(PE.h, pe6)
                v.tensor_copy(out=gapt[:], in_=gapt[:])
                v.tensor_copy(out=rowoff[:], in_=p2[:, 0:1])
                v.tensor_copy(out=gapt[:], in_=gapt[:])
                v.tensor_scalar(out=rank[:], in0=rank[:], scalar1=rowoff[:, :], scalar2=None,
                                op0=ALU.add)
                v.tensor_scalar(out=Atile[:], in0=rank[:], scalar1=-BIG_OOB, scalar2=None,
                                op0=ALU.add)
                v.tensor_tensor(out=Atile[:], in0=Atile[:], in1=mask[:], op=ALU.mult)
                VE.inc(v.tensor_scalar(out=Atile[:], in0=Atile[:], scalar1=BIG_OOB,
                                       scalar2=None, op0=ALU.add))
            p.vec(v6_)
            ve6 = VE.n

            def sfa(sy, ve6=ve6):
                sy.wait_ge(VE.h, ve6)
                DFA.inc(sy.dma_start(out=fa_dram.rearrange("(p f) -> p f", p=128),
                                     in_=Atile[:]), 16)
            p.sync(sfa)
            dfa1 = DFA.n

            DFR = p.S("DFR0"); DFR2 = p.S("DFR1")
            rb_free = [0, 0]
            prevv = 0
            for i in range(32):
                rp = i % 2
                RB = [DFR, DFR2][rp]

                def sfr(sy, i=i, rp=rp, RB=RB, dfa1=dfa1, m=rb_free[rp]):
                    sy.wait_ge(DFA.h, dfa1)
                    sy.wait_ge(PE.h, m)
                    RB.inc(sy.dma_start(out=rowbuf[:, rp, :],
                                        in_=fa_dram[None, i * 512:(i + 1) * 512]), 16)
                p.sync(sfr)
                rb_m = RB.n

                def trp(t, i=i, rp=rp, RB=RB, rb_m=rb_m, prevv=prevv):
                    t.wait_ge(RB.h, rb_m)
                    t.wait_ge(VE.h, prevv)
                    PE.inc(t.matmul(out=PSC[i % 4][:, 0:512], lhsT=ones_row[:],
                                    rhs=rowbuf[0:1, rp, :], start=True, stop=True))
                p.ten(trp)
                mpe = PE.n
                rb_free[rp] = mpe

                def vrp(v, i=i, mpe=mpe):
                    v.wait_ge(PE.h, mpe)
                    VE.inc(v.tensor_copy(out=Arep[:, i * 512:(i + 1) * 512],
                                         in_=PSC[i % 4][:, 0:512]))
                p.vec(vrp)
                if i % 4 == 3:
                    prevv = VE.n
            ve_arep = VE.n

            def vmi1(v, ve_arep=ve_arep):
                v.wait_ge(VE.h, ve_arep)
                VE.inc(v.tensor_copy(out=wraw[:], in_=iot.ap().rearrange("p a b -> p (a b)")))
            p.vec(vmi1)
            ve_hop = VE.n

            def shp(s, ve_hop=ve_hop):
                s.wait_ge(VE.h, ve_hop)
                AC.inc(s.nop())
            p.sca(shp)
            ac_hop = AC.n

            def vmi2(v, ac_hop=ac_hop):
                v.wait_ge(AC.h, ac_hop)
                VE.inc(v.max_index(out=idx2[:, 0, :], in_max=iot[:, 0, :], in_values=Arep[:]))
                VE.inc(v.max_index(out=idx2[:, 1, :], in_max=iot[:, 1, :], in_values=Arep[:]))
            p.vec(vmi2)
            ve_idx = VE.n

            if not fused:
                def sts_(sy, ve_idx=ve_idx):
                    sy.wait_ge(VE.h, ve_idx)
                    STS.inc(sy.dma_start(out=ts_o[:, :],
                                         in_=idx2.ap().rearrange("p a b -> p (a b)")), 16)
                p.sync(sts_)

            idxf = idx2.ap().rearrange("p a b -> p (a b)")
            ERW = p.S("ERW")

            def vtk(v, ve_idx=ve_idx):
                v.wait_ge(VE.h, ve_idx)
                VE.inc(v.tensor_copy(out=tokf[:], in_=idxf))
            p.vec(vtk)
            ve_tok = VE.n
            erw_tot = emit_wrap_idx(p, tokf[:, :], sr8.ap()[0:16, 0:128],
                                    sr8.ap()[0:16, 128:256], widx, ident_f[:],
                                    p3, p4, ERW, ve_tok)

            PCI = [p.S("PCI0"), p.S("PCI1")]
            PCO = [p.S("PCO0"), p.S("PCO1")]
            pc_inbuf = [xg, aa]
            pcv_mark = [0, 0]
            pco_mark = [0, 0]
            for kch in range(64):
                ppc = kch % 2
                r, cc = kch // 2, kch % 2

                def spci(sy, r=r, cc=cc, ppc=ppc, m=pcv_mark[ppc]):
                    sy.wait_ge(VE.h, m)
                    PCI[ppc].inc(sy.dma_start(
                        out=pc_inbuf[ppc][:],
                        in_=f2_d[r * 128:(r + 1) * 128, cc * 512:(cc + 1) * 512]), 16)
                p.sync(spci)
                pci_m = PCI[ppc].n

                def vpc(v, ppc=ppc, pci_m=pci_m, m=pco_mark[ppc]):
                    v.wait_ge(PCI[ppc].h, pci_m)
                    v.wait_ge(PCO[ppc].h, m)
                    VE.inc(v.tensor_copy(out=f2p[:, ppc, :], in_=pc_inbuf[ppc][:]))
                p.vec(vpc)
                pcv_mark[ppc] = VE.n
                vem = VE.n

                def spco(sy, r=r, cc=cc, ppc=ppc, vem=vem):
                    sy.wait_ge(VE.h, vem)
                    PCO[ppc].inc(sy.dma_start(
                        out=f2bf[r * 128:(r + 1) * 128, cc * 512:(cc + 1) * 512],
                        in_=f2p[:, ppc, :]), 16)
                p.sync(spco)
                pco_mark[ppc] = PCO[ppc].n
            pco_tot = [PCO[0].n, PCO[1].n]

            for g2 in range(16):
                def gw_(g_, g2=g2, ve_idx=ve_idx, sscm=ssc_c):
                    g_.wait_ge(VE.h, ve_idx)
                    if fused:
                        g_.wait_ge(SSC.h, sscm)
                    LZ[0].inc(g_.indirect_dma_start(
                        out=wraw[:, g2:g2 + 1], out_offset=None, in_=scol_d[:, :],
                        in_offset=IndirectOffsetOnAxis(ap=idxf[:, g2:g2 + 1], axis=0),
                        bounds_check=NTOK - 1, oob_is_err=False), 16)
                p.gps(gw_)
            lzw_tot = LZ[0].n

            zbuf = [zselA, zselB]
            ZG = [p.S("ZG0"), p.S("ZG1")]
            zg_free = [0, 0]
            for tch in range(8):
                pp = tch % 2

                def gz(g_, tch=tch, pp=pp, erw_tot=erw_tot, m=zg_free[pp]):
                    g_.wait_ge(ERW.h, erw_tot)
                    g_.wait_ge(PE.h, m)
                    ZG[pp].inc(g_.dma_gather(zbuf[pp][:, :, :], zall_d[:, :],
                                             widx[:, 16 * tch:16 * (tch + 1)],
                                             256, 256, D), 16)
                p.gps(gz)
                zg_m = ZG[pp].n

                for gg in range(2):
                    g = 2 * tch + gg
                    for half in range(2):
                        def tz(t, gg=gg, pp=pp, half=half, zg_m=zg_m, wv=VE.n):
                            t.wait_ge(ZG[pp].h, zg_m)
                            t.wait_ge(VE.h, wv)
                            for i2 in range(4):
                                c = half * 4 + i2
                                PE.inc(t.transpose(out=pT[:, i2 * 128:(i2 + 1) * 128],
                                                   in_=zbuf[pp][:, gg, c * 128:(c + 1) * 128],
                                                   identity=ident_bf[:]))
                        p.ten(tz)
                        pe_z = PE.n

                        def vz(v, g=g, half=half, pe_z=pe_z):
                            v.wait_ge(PE.h, pe_z)
                            for i2 in range(4):
                                c = half * 4 + i2
                                VE.inc(v.tensor_copy(out=zT[:, c, g * 128:(g + 1) * 128],
                                                     in_=pT[:, i2 * 128:(i2 + 1) * 128]))
                        p.vec(vz)
                zg_free[pp] = PE.n
            ve_zT = VE.n

            def sw(s, lzw_tot=lzw_tot):
                s.wait_ge(LZ[0].h, lzw_tot)
                AC.inc(s.activation(out=wsel[:], in_=wraw[:], func=AF.Tanh))
            p.sca(sw)
            ac_w = AC.n

            def vw(v, ac_w=ac_w):
                v.wait_ge(AC.h, ac_w)
                v.tensor_copy(out=gapt[:], in_=gapt[:])
                VE.inc(v.tensor_scalar(out=wsel[:], in0=wsel[:], scalar1=0.5, scalar2=0.5,
                                       op0=ALU.mult, op1=ALU.add))
            p.vec(vw)
            ve_w = VE.n

            GELU_S = 1.5957691216057308
            dn_ev = VE.n
            for q in range(4):
                qsl = slice(q * 512, (q + 1) * 512)
                for j in range(32):
                    pp = j % 2

                    def gf1(g_, j=j, pp=pp, m=PE.n):
                        g_.wait_ge(PE.h, m)
                        LF1[pp].inc(g_.dma_start(
                            out=f1p[:, pp, :, :],
                            in_=f1_d[:, j * 128:(j + 1) * 128]
                            .rearrange("(c p) h -> p c h", p=128)), 16)
                    p.gps(gf1)
                    lf1 = LF1[pp].n
                    bank = PSC[j % 2]

                    def tf1(t, j=j, pp=pp, bank=bank, lf1=lf1, ve_zT=ve_zT, m=AC.n):
                        t.wait_ge(LF1[pp].h, lf1)
                        t.wait_ge(VE.h, ve_zT)
                        t.wait_ge(AC.h, m)
                        for c in range(8):
                            PE.inc(t.matmul(out=bank[:, 0:512], lhsT=f1p[:, pp, c, :],
                                            rhs=zT[:, c, qsl], start=(c == 0), stop=(c == 7)))
                    p.ten(tf1)
                    pe_f1 = PE.n

                    def vx(v, j=j, bank=bank, pe_f1=pe_f1, m=AC.n, lg_c=lg_c):
                        v.wait_ge(PE.h, pe_f1)
                        v.wait_ge(LG.h, lg_c)
                        v.wait_ge(AC.h, m)
                        VE.inc(v.tensor_scalar(out=xg[:], in0=bank[:, 0:512],
                                               scalar1=b1_sb[:, j:j + 1], scalar2=None,
                                               op0=ALU.add))
                    p.vec(vx)
                    ve_x = VE.n

                    def sq_(s, ve_x=ve_x):
                        s.wait_ge(VE.h, ve_x)
                        AC.inc(s.activation(out=aa[:], in_=xg[:], func=AF.Square))
                    p.sca(sq_)
                    ac_a = AC.n

                    def vu(v, ac_a=ac_a):
                        v.wait_ge(AC.h, ac_a)
                        v.tensor_scalar(out=w1[:], in0=aa[:], scalar1=0.044715, scalar2=1.0,
                                        op0=ALU.mult, op1=ALU.add)
                        VE.inc(v.tensor_tensor(out=c2[:], in0=xg[:], in1=w1[:], op=ALU.mult))
                    p.vec(vu)
                    ve_c2 = VE.n

                    def sg_(s, ve_c2=ve_c2):
                        s.wait_ge(VE.h, ve_c2)
                        AC.inc(s.activation(out=sg[:], in_=c2[:], func=AF.Sigmoid,
                                            scale=GELU_S))
                    p.sca(sg_)
                    ac_g = AC.n

                    def vh(v, j=j, ac_g=ac_g):
                        v.wait_ge(AC.h, ac_g)
                        VE.inc(v.tensor_tensor(out=hmidQ[:, j, :], in0=xg[:], in1=sg[:],
                                               op=ALU.mult))
                    p.vec(vh)
                ve_hm = VE.n

                for dh in range(2):
                    f2_marks = []
                    for j in range(32):
                        pp = j % 2

                        def gf2(g_, j=j, dh=dh, pp=pp, m=PE.n, pco_tot=pco_tot):
                            g_.wait_ge(PE.h, m)
                            g_.wait_ge(PCO[0].h, pco_tot[0])
                            g_.wait_ge(PCO[1].h, pco_tot[1])
                            LF2[pp].inc(g_.dma_start(
                                out=f2p[:, pp, :],
                                in_=f2bf[j * 128:(j + 1) * 128,
                                         dh * 512:(dh + 1) * 512]), 16)
                        p.gps(gf2)
                        lf2 = LF2[pp].n

                        def tf2(t, j=j, pp=pp, lf2=lf2, ve_hm=ve_hm, m=dn_ev):
                            t.wait_ge(LF2[pp].h, lf2)
                            t.wait_ge(VE.h, ve_hm)
                            if j == 0:
                                t.wait_ge(VE.h, m)
                            for sc_ in range(4):
                                PE.inc(t.matmul(out=PSC[sc_][:, 0:512],
                                                lhsT=hmidQ[:, j, sc_ * 128:(sc_ + 1) * 128],
                                                rhs=f2p[:, pp, :],
                                                start=(j == 0), stop=(j == 31)))
                        p.ten(tf2)
                    pe_f2 = PE.n

                    for sc_ in range(4):
                        def vo(v, q=q, dh=dh, sc_=sc_, pe_f2=pe_f2, m=SOP.n):
                            v.wait_ge(PE.h, pe_f2)
                            v.wait_ge(SOP.h, m)
                            v.tensor_tensor(out=tvv[:, 0:512], in0=PSC[sc_][:, 0:512],
                                            in1=b2rep[:, dh * 512:(dh + 1) * 512], op=ALU.add)
                            VE.inc(v.tensor_scalar(out=orow4[:, sc_, dh * 512:(dh + 1) * 512],
                                                   in0=tvv[:, 0:512],
                                                   scalar1=wsel[:, q * 4 + sc_:q * 4 + sc_ + 1],
                                                   scalar2=None, op0=ALU.mult))
                        p.vec(vo)
                        if dh == 1:
                            ve_o = VE.n

                            if not fused:
                                def so(sy, q=q, sc_=sc_, ve_o=ve_o):
                                    sy.wait_ge(VE.h, ve_o)
                                    SOP.inc(sy.dma_start(
                                        out=op_o[q * 512 + sc_ * 128:q * 512 + (sc_ + 1) * 128, :],
                                        in_=orow4[:, sc_, :]), 16)
                                p.sync(so)
                            elif sc_ == 3:
                                def so(g_, q=q, ve_o=ve_o, zc=zc_all, erw_tot=erw_tot):
                                    g_.wait_ge(VE.h, ve_o)
                                    g_.wait_ge(ZC.h, zc)
                                    g_.wait_ge(ERW.h, erw_tot)
                                    SOP.inc(g_.dma_scatter_add(
                                        ext["contribZ"][:, :], orow4[:, :, :],
                                        widx[:, 32 * q:32 * (q + 1)], 512, 512, D), 16)
                                p.gps(so)
                    dn_ev = VE.n

            def fin(sy):
                sy.wait_ge(SOP.h, SOP.n)
                sy.wait_ge(STS.h, STS.n)
                DONE.inc(sy.nop())
            p.sync(fin)

    return _lower_ext(nc) if standalone else (nc, (DONE.h, DONE.n))



NRND = 8

def build_F(nc=None, ext=None, entry=(), fused=False):
    standalone = nc is None
    if standalone:
        nc = bass.Bass("TRN2", target_bir_lowering=False, detect_race_conditions=False)
        ext = {}
        ext["x1_own"] = nc.dram_tensor("x1_own", [TOK, D], F32, kind="ExternalInput").ap()
        ext["xn1_own"] = nc.dram_tensor("xn1_own", [TOK, D], F32, kind="ExternalInput").ap()
        ext["holoT_own"] = nc.dram_tensor("holoT_own", [D, TOK], F32, kind="ExternalInput").ap()
        ext["M_unbind"] = nc.dram_tensor("M_unbind", [D, D], F32, kind="ExternalInput").ap()
        ext["gate"] = nc.dram_tensor("gate", [1, 1], F32, kind="ExternalInput").ap()
        ext["crounds"] = nc.dram_tensor("crounds", [NRND, TOK, D], BF16, kind="ExternalInput").ap()
        ext["x2_own"] = nc.dram_tensor("x2_own", [TOK, D], F32, kind="ExternalOutput").ap()
    x1_d = ext["x1_own"]
    xn1_d = ext["xn1_own"]
    ht_d = ext.get("holoT_own")
    mu_d = ext["M_unbind"]
    gt_d = ext["gate"]
    cr_d = ext.get("crounds")
    x2_o = ext["x2_own"]

    es = ExitStack()
    sb = lambda n, s_, d_: es.enter_context(nc.sbuf_tensor("F" + n, s_, d_))
    psa = lambda n, s_, d_: es.enter_context(nc.psum_tensor("E" + n, s_, d_))
    with es:
        ones_row = sb("ones_row", [1, 128], F32)
        gt_s = sb("gt_s", [1, 1], F32)
        gtrep = sb("gtrep", [128, 1], F32)
        mu_s = sb("mu_s", [128, 8, D], BF16)
        hf = sb("hf", [128, 2, TOK], F32)
        hb = sb("hb", [128, 8, TOK], BF16)
        dsg = sb("dsg", [1, 1], F32)
        dpow = sb("dpow", [1, TOK], F32)
        carr_sb = sb("carr_sb", [8, 1024], F32)
        csel_sb = sb("csel_sb", [8, 1], F32)
        cown = sb("cown", [1, 1024], F32)
        res = sb("res", [128, 16, D], F32)
        xa = sb("xa", [128, 2, D], F32)
        xb = sb("xb", [128, 2, D], F32)
        crt = sb("crt", [128, 2, D], BF16)
        gapt = sb("gapt", [128, 1], F32)
        psA = psa("psA", [128, 512], F32)
        psB = psa("psB", [128, 512], F32)

        with nc.Block() as block:
            p = ext["_P"].bind(block) if "_P" in ext else P(nc, es, block, pfx="D_")
            DONE = p.S("DONE")
            for _eng in [block.sync, block.gpsimd, block.vector, block.scalar, block.tensor]:
                def _bar(e, entry=entry):
                    for _sh, _cnt in entry:
                        e.wait_ge(_sh, _cnt)
                _eng(_bar)
            PE, VE, AC, PL = p.PE, p.VE, p.AC, p.PL
            LG = p.S("LG"); LW = p.S("LW")
            LH = [p.S("LH0"), p.S("LH1")]
            LX = [p.S("LX0"), p.S("LX1")]
            LY = [p.S("LY0"), p.S("LY1")]
            LC = [p.S("LC0"), p.S("LC1")]
            SX = p.S("SX")

            def gc(g):
                LG.inc(g.dma_start(out=mu_s[:], in_=mu_d.rearrange("(c p) j -> p c j", p=128)), 16)
            p.gps(gc)
            lg_c = LG.n

            def sc(sy):
                LW.inc(sy.dma_start(out=gt_s[:], in_=gt_d[:, :]), 16)
            p.sync(sc)
            lw_c = LW.n

            def vc(v):
                v.memset(ones_row[:], 1.0)
                VE.inc(v.memset(gapt[:], 0.0))
            p.vec(vc)
            ve_c = VE.n

            def tg(t, ve_c=ve_c, lw_c=lw_c):
                t.wait_ge(VE.h, ve_c)
                t.wait_ge(LW.h, lw_c)
                PE.inc(t.matmul(out=psA[:, 0:1], lhsT=ones_row[:], rhs=gt_s[:], start=True,
                                stop=True))
            p.ten(tg)
            pe_g = PE.n

            def vg(v, pe_g=pe_g):
                v.wait_ge(PE.h, pe_g)
                v.tensor_copy(out=gapt[:], in_=gapt[:])
                VE.inc(v.tensor_copy(out=gtrep[:], in_=psA[:, 0:1]))
            p.vec(vg)
            ve_g = VE.n

            hm = [0, 0]
            if not fused:
                for i in range(8):
                    rp = i % 2

                    def sh(sy, i=i, rp=rp, m=hm[rp]):
                        sy.wait_ge(VE.h, m)
                        LH[rp].inc(sy.dma_start(out=hf[:, rp, :],
                                                in_=ht_d[i * 128:(i + 1) * 128, :]), 16)
                    p.sync(sh)
                    lh = LH[rp].n

                    def vh(v, i=i, rp=rp, lh=lh):
                        v.wait_ge(LH[rp].h, lh)
                        VE.inc(v.tensor_copy(out=hb[:, i, :], in_=hf[:, rp, :]))
                    p.vec(vh)
                    hm[rp] = VE.n
            else:
                LHB = p.S("LHB")

                def shb(sy):
                    LHB.inc(sy.dma_start(out=hb[:],
                                         in_=ext["holoT_bf"].rearrange("(c p) t -> p c t", p=128)),
                            16)
                    LHB.inc(sy.dma_start(out=dsg[:], in_=ext["decay"][:, :]), 16)
                    LHB.inc(sy.dma_start(out=carr_sb[:], in_=ext["carrag"][:, :]), 16)
                    LHB.inc(sy.dma_start(out=csel_sb[:], in_=ext["carry_sel"][:, :]), 16)
                p.sync(shb)
                lhb_c = LHB.n

                def sdp(sc_, lhb_c=lhb_c):
                    sc_.wait_ge(LHB.h, lhb_c)
                    sc_.activation(out=dsg[:], in_=dsg[:], func=AF.Sigmoid)
                    AC.inc(sc_.activation(out=dsg[:], in_=dsg[:], func=AF.Ln))
                p.sca(sdp)
                ac_ln = AC.n

                def gdp(g, ac_ln=ac_ln):
                    g.wait_ge(AC.h, ac_ln)
                    PL.inc(g.iota(out=dpow[:], pattern=[[1, TOK]], base=1, channel_multiplier=0,
                                  allow_small_or_imprecise_dtypes=True))
                p.gps(gdp)
                pl_dp = PL.n

                def vdp(v, pl_dp=pl_dp):
                    v.wait_ge(PL.h, pl_dp)
                    v.tensor_copy(out=gapt[:], in_=gapt[:])
                    VE.inc(v.tensor_scalar(out=dpow[:], in0=dpow[:], scalar1=dsg[0:1, :],
                                           scalar2=None, op0=ALU.mult))
                p.vec(vdp)
                ve_dp = VE.n

                def sdp2(sc_, ve_dp=ve_dp):
                    sc_.wait_ge(VE.h, ve_dp)
                    AC.inc(sc_.activation(out=dpow[:], in_=dpow[:], func=AF.Exp))
                p.sca(sdp2)
                ac_dp = AC.n

                def tco(t, ac_dp=ac_dp, lhb_c=lhb_c):
                    t.wait_ge(AC.h, ac_dp)
                    t.wait_ge(LHB.h, lhb_c)
                    PE.inc(t.matmul(out=psA[0:1, 0:512], lhsT=csel_sb[:], rhs=carr_sb[:, 0:512],
                                    start=True, stop=True))
                    PE.inc(t.matmul(out=psB[0:1, 0:512], lhsT=csel_sb[:], rhs=carr_sb[:, 512:1024],
                                    start=True, stop=True))
                p.ten(tco)
                pe_co = PE.n

                def vco(v, pe_co=pe_co):
                    v.wait_ge(PE.h, pe_co)
                    v.tensor_copy(out=cown[0:1, 0:512], in_=psA[0:1, 0:512])
                    VE.inc(v.tensor_copy(out=cown[0:1, 512:1024], in_=psB[0:1, 0:512]))
                p.vec(vco)
                ve_co2 = VE.n

                if _DEBUG_OUT:
                    dbg_f = nc.dram_tensor("dbg_f", [9, 4096], F32, kind="ExternalOutput").ap()
                    SDB = p.S("SDB")

                    def sdbg(sy, ve_co2=ve_co2, ac_dp=ac_dp, lhb_c=lhb_c):
                        sy.wait_ge(VE.h, ve_co2)
                        sy.wait_ge(AC.h, ac_dp)
                        sy.wait_ge(LHB.h, lhb_c)
                        SDB.inc(sy.dma_start(out=dbg_f[0:1, 0:TOK], in_=dpow[:]), 16)
                        SDB.inc(sy.dma_start(out=dbg_f[0:1, 2048:3072], in_=cown[:]), 16)
                        SDB.inc(sy.dma_start(out=dbg_f[0:1, 3072:3073], in_=dsg[:]), 16)
                        SDB.inc(sy.dma_start(out=dbg_f[1:9, 0:1024], in_=carr_sb[:]), 16)
                        SDB.inc(sy.dma_start(out=dbg_f[1:9, 1024:1025], in_=csel_sb[:],
                                             allow_slow_non_contiguous=True), 16)
                    p.sync(sdbg)

                cview = cown.ap().rearrange("o (pp jj) -> o jj pp", jj=8)
                for j in range(8):
                    for qc in range(4):
                        bank = [psA, psB][qc % 2]

                        def tfx(t, j=j, qc=qc, bank=bank, ve_co2=ve_co2, m=VE.n):
                            t.wait_ge(VE.h, ve_co2)
                            t.wait_ge(VE.h, m)
                            PE.inc(t.matmul(out=bank[:, 0:512], lhsT=cview[:, j, :],
                                            rhs=dpow[0:1, qc * 512:(qc + 1) * 512],
                                            start=True, stop=True))
                        p.ten(tfx)
                        pe_fx = PE.n

                        def vfx(v, j=j, qc=qc, bank=bank, pe_fx=pe_fx):
                            v.wait_ge(PE.h, pe_fx)
                            VE.inc(v.tensor_tensor(out=hb[:, j, qc * 512:(qc + 1) * 512],
                                                   in0=hb[:, j, qc * 512:(qc + 1) * 512],
                                                   in1=bank[:, 0:512], op=ALU.add))
                        p.vec(vfx)
                hm = [VE.n, VE.n]

            xm = [0, 0]
            for g in range(16):
                rp = g % 2

                def sx(sy, g=g, rp=rp, m=xm[rp]):
                    sy.wait_ge(VE.h, m)
                    LX[rp].inc(sy.dma_start(out=xa[:, rp, :],
                                            in_=x1_d[g * 128:(g + 1) * 128, :]), 16)
                    LY[rp].inc(sy.dma_start(out=xb[:, rp, :],
                                            in_=xn1_d[g * 128:(g + 1) * 128, :]), 16)
                p.sync(sx)
                lx, ly = LX[rp].n, LY[rp].n

                def vx(v, g=g, rp=rp, lx=lx, ly=ly):
                    v.wait_ge(LX[rp].h, lx)
                    v.wait_ge(LY[rp].h, ly)
                    VE.inc(v.scalar_tensor_tensor(out=res[:, g, :], in0=xb[:, rp, :],
                                                  scalar=2.0, in1=xa[:, rp, :],
                                                  op0=ALU.mult, op1=ALU.add))
                p.vec(vx)
                xm[rp] = VE.n
            ve_res = VE.n

            for tc in range(16):
                for dh in range(2):
                    bank = [psA, psB][dh]

                    def tr(t, tc=tc, dh=dh, bank=bank, ve_res=ve_res, lg_c=lg_c,
                           hm=hm, m=VE.n):
                        t.wait_ge(VE.h, ve_res)
                        t.wait_ge(VE.h, hm[0])
                        t.wait_ge(VE.h, hm[1])
                        t.wait_ge(LG.h, lg_c)
                        t.wait_ge(VE.h, m)
                        for i in range(8):
                            PE.inc(t.matmul(out=bank[:, 0:512],
                                            lhsT=hb[:, i, tc * 128:(tc + 1) * 128],
                                            rhs=mu_s[:, i, dh * 512:(dh + 1) * 512],
                                            start=(i == 0), stop=(i == 7)))
                    p.ten(tr)
                    pe_r = PE.n

                    def vr(v, tc=tc, dh=dh, bank=bank, pe_r=pe_r, ve_g=ve_g):
                        v.wait_ge(PE.h, pe_r)
                        v.wait_ge(VE.h, ve_g)
                        VE.inc(v.scalar_tensor_tensor(
                            out=res[:, tc, dh * 512:(dh + 1) * 512], in0=bank[:, 0:512],
                            scalar=gtrep[:, :], in1=res[:, tc, dh * 512:(dh + 1) * 512],
                            op0=ALU.mult, op1=ALU.add))
                    p.vec(vr)

            cm = [0, 0]
            if not fused:
                for r in range(NRND):
                    for g in range(16):
                        rp = (r * 16 + g) % 2

                        def scr_(sy, r=r, g=g, rp=rp, m=cm[rp]):
                            sy.wait_ge(VE.h, m)
                            LC[rp].inc(sy.dma_start(out=crt[:, rp, :],
                                                    in_=cr_d[r, g * 128:(g + 1) * 128, :]), 16)
                        p.sync(scr_)
                        lc = LC[rp].n

                        def vcr(v, r=r, g=g, rp=rp, lc=lc):
                            v.wait_ge(LC[rp].h, lc)
                            VE.inc(v.tensor_tensor(out=res[:, g, :], in0=res[:, g, :],
                                                   in1=crt[:, rp, :], op=ALU.add))
                        p.vec(vcr)
                        cm[rp] = VE.n
            else:
                CC = p.CC

                def grs(g):
                    CC.inc(g.collective_compute(
                        "ReduceScatter", ALU.add, replica_groups=[list(range(NC))],
                        ins=[ext["contribZ"].opt()], outs=[ext["contribOwn"].opt()]))
                p.gps(grs)
                cc_rs = CC.n
                for g in range(16):
                    rp = g % 2

                    def scr_(sy, g=g, rp=rp, m=cm[rp], cc_rs=cc_rs):
                        sy.wait_ge(p.CC.h, cc_rs)
                        sy.wait_ge(VE.h, m)
                        LC[rp].inc(sy.dma_start(out=crt[:, rp, :],
                                                in_=ext["contribOwn"][g * 128:(g + 1) * 128, :]),
                                   16)
                    p.sync(scr_)
                    lc = LC[rp].n

                    def vcr(v, g=g, rp=rp, lc=lc):
                        v.wait_ge(LC[rp].h, lc)
                        VE.inc(v.tensor_tensor(out=res[:, g, :], in0=res[:, g, :],
                                               in1=crt[:, rp, :], op=ALU.add))
                    p.vec(vcr)
                    cm[rp] = VE.n

            for g in range(16):
                def so(sy, g=g, m=VE.n):
                    sy.wait_ge(VE.h, m)
                    SX.inc(sy.dma_start(out=x2_o[g * 128:(g + 1) * 128, :], in_=res[:, g, :]), 16)
                p.sync(so)

            def fin(sy):
                sy.wait_ge(SX.h, SX.n)
                DONE.inc(sy.nop())
            p.sync(fin)

    return _lower_ext(nc) if standalone else (nc, (DONE.h, DONE.n))



_CACHE = {}

def _get(name, builder):
    if name not in _CACHE:
        _CACHE[name] = builder()
    return _CACHE[name]


def kernel_multi(**inp):
    f32 = lambda a: np.ascontiguousarray(np.asarray(a, np.float32))
    x = f32(inp["L0_attn_x"])
    i = np.arange(D)
    bind_key = f32(inp["L1_attn_bind_key"])
    unbind_key = f32(inp["L1_attn_unbind_key"])
    Mb = np.ascontiguousarray(bind_key[(i[None, :] - i[:, None]) % D])
    Mu = np.ascontiguousarray(unbind_key[(i[:, None] - i[None, :]) % D])

    ncA = _get("A", build_A)
    mapsA = _prep_A_inputs(x, inp["L0_attn_Wq"], inp["L0_attn_Wk"], inp["L0_attn_Wv"],
                           inp["L0_ffn_ln_w"], inp["L0_ffn_ln_b"])
    rA = run_bass_kernel_spmd(ncA, mapsA, core_ids=list(range(NC))).results

    ncB = _get("B", build_B)
    mapsB = _prep_B_inputs([rA[c]["nm_own"] for c in range(NC)],
                           inp["L0_ffn_router_w"], inp["L0_ffn_gate_w"], inp["L0_ffn_up_w"],
                           inp["L0_ffn_down_w"], inp["L0_ffn_sh_gate"], inp["L0_ffn_sh_up"],
                           inp["L0_ffn_sh_down"])
    rB = run_bass_kernel_spmd(ncB, mapsB, core_ids=list(range(NC))).results

    ncC = _get("C", build_C)
    gw1T = np.ascontiguousarray(f32(inp["L1_ffn_gate_w"]).T)
    lgam = f32(inp["L1_ffn_ln_gamma"])
    lbet = f32(inp["L1_ffn_ln_beta"])
    xf = x.reshape(NTOK, D)
    mapsC = []
    for c in range(NC):
        sl = slice(c * TOK, (c + 1) * TOK)
        mapsC.append({"x_own": np.ascontiguousarray(xf[sl]), "xn_own": rA[c]["xn_own"],
                      "att_own": rA[c]["att_own"], "moe_own": rB[c]["moe_own"],
                      "M_bind": Mb, "gw1T": gw1T, "lgam": lgam, "lbeta": lbet})
    rC = run_bass_kernel_spmd(ncC, mapsC, core_ids=list(range(NC))).results

    ncD = _get("D", build_D)
    decay = f32(inp["L1_attn_decay"]).reshape(1, 1)
    btf = [np.concatenate([rC[2 * b]["boundT_own"], rC[2 * b + 1]["boundT_own"]], axis=1)
           for b in range(4)]
    units = [(b, j) for b in range(4) for j in range(8)]
    mapsD = []
    for c in range(NC):
        us_ = units[c::NC]
        bu = np.stack([btf[b][j * 128:(j + 1) * 128, :] for (b, j) in us_])
        mapsD.append({"bound_units": np.ascontiguousarray(bu), "decay": decay})
    rD = run_bass_kernel_spmd(ncD, mapsD, core_ids=list(range(NC))).results

    ncE = _get("E", build_E)
    z1_all = np.ascontiguousarray(np.concatenate([rC[c]["z1_own"] for c in range(NC)], axis=0))
    sraw_all = np.ascontiguousarray(np.concatenate([rC[c]["sraw_own"] for c in range(NC)], axis=0))
    fc1s = f32(inp["L1_ffn_fc1s"]); b1s = f32(inp["L1_ffn_b1s"])
    fc2s = f32(inp["L1_ffn_fc2s"]); b2s = f32(inp["L1_ffn_b2s"])
    usmat = np.triu(np.ones((128, 128), np.float32), 1)
    mapsE = []
    for e in range(NC):
        mapsE.append({"z1_all": z1_all, "scol": np.ascontiguousarray(sraw_all[:, e:e + 1]),
                      "fc1T": np.ascontiguousarray(fc1s[e].T),
                      "b1": np.ascontiguousarray(b1s[e].reshape(HID1 // 128, 128).T),
                      "fc2T": np.ascontiguousarray(fc2s[e].T), "b2": b2s[e], "Ustrict": usmat})
    rE = run_bass_kernel_spmd(ncE, mapsE, core_ids=list(range(NC))).results

    crounds = [np.zeros((NRND, TOK, D), dtype=z1_all.dtype) for _ in range(NC)]
    occ = np.zeros(NTOK, np.int64)
    for e in range(NC):
        tsel = rE[e]["tsel"]
        tglob = np.zeros(K1, np.int64)
        for g in range(16):
            tglob[g * 128 + np.arange(128)] = tsel[:, g]
        rows = rE[e]["opack"]
        rr = occ[tglob].copy()
        occ[tglob] += 1
        owners = tglob // TOK
        tl = tglob % TOK
        for o in range(NC):
            m = owners == o
            crounds[o][rr[m], tl[m], :] = rows[m]

    ncF = _get("F", build_F)
    gate = f32(inp["L1_attn_gate"]).reshape(1, 1)
    hof = [np.zeros((D, S), np.float32) for _ in range(4)]
    for c in range(NC):
        us_ = units[c::NC]
        hu = rD[c]["holo_units"]
        for k, (b, j) in enumerate(us_):
            hof[b][j * 128:(j + 1) * 128, :] = hu[k]
    mapsF = []
    for c in range(NC):
        b, h = c // 2, c % 2
        mapsF.append({"x1_own": rC[c]["x1_own"], "xn1_own": rC[c]["xn1_own"],
                      "holoT_own": np.ascontiguousarray(hof[b][:, h * TOK:(h + 1) * TOK]),
                      "M_unbind": Mu, "gate": gate, "crounds": crounds[c]})
    rF = run_bass_kernel_spmd(ncF, mapsF, core_ids=list(range(NC))).results

    out = np.concatenate([rF[c]["x2_own"] for c in range(NC)], axis=0)
    return out.reshape(B, S, D)



_DEBUG_OUT = False
_SES_KEEP = []


def build_full():
    nc = bass.Bass("TRN2", target_bir_lowering=False, detect_race_conditions=False)
    ext = {}
    ses = ExitStack()
    ext["_P"] = P(nc, ses, None, pfx="G_")
    _SES_KEEP.append(ses)
    EI = lambda n, sh, dt: nc.dram_tensor(n, sh, dt, kind="ExternalInput").ap()
    _dbg = {"xn_own", "nm_own", "att_own", "moe_own", "x1_own", "xn1_own",
            "holoT_bf", "scolE"}
    IN = lambda n, sh, dt: (
        nc.dram_tensor(n, sh, dt, kind="ExternalOutput").ap()
        if (_DEBUG_OUT and n in _dbg) else nc.dram_tensor(n, sh, dt).ap())
    for n, sh, dt in [
        ("x_own", [TOK, D], F32), ("x_other", [TOK, D], F32),
        ("WqT", [D, D], F32), ("WkT", [D, DK], F32), ("WvT", [D, DK], F32),
        ("ln_w", [D], F32), ("ln_b", [D], F32),
        ("router_w", [D, E], F32), ("gate_w", [E, D, I0], F32), ("up_w", [E, D, I0], F32),
        ("down_w", [E, I0, D], F32), ("sh_gate", [D, I0], F32), ("sh_up", [D, I0], F32),
        ("sh_down", [I0, D], F32), ("Ustrict", [128, 128], F32), ("blkstrict", [128, 128], F32),
        ("ecrow", [1, 128], F32),
        ("M_bind", [D, D], F32), ("gw1T", [D, E], F32), ("lgam", [D], F32), ("lbeta", [D], F32),
        ("decay", [1, 1], F32),
        ("fc1T", [D, HID1], F32), ("b1", [128, HID1 // 128], F32), ("fc2T", [HID1, D], F32),
        ("b2", [D], F32), ("esel8", [128, 8], F32), ("esel_tiled", [128, 1024], F32),
        ("carry_sel", [8, 1], F32), ("M_unbind", [D, D], F32), ("gate", [1, 1], F32),
    ]:
        ext[n] = EI(n, sh, dt)
    ext["x2_own"] = nc.dram_tensor("x2_own", [TOK, D], F32, kind="ExternalOutput").ap()
    for n, sh, dt in [
        ("xn_own", [TOK, D], F32), ("nm_own", [TOK, D], F32), ("att_own", [TOK, D], F32),
        ("moe_own", [TOK, D], F32), ("x1_own", [TOK, D], F32), ("xn1_own", [TOK, D], F32),
        ("sraw_own", [TOK, E], F32), ("z1_own", [TOK, D], BF16),
        ("holoT_bf", [D, TOK], BF16), ("carr_my", [1, 1024], F32), ("carrag", [8, 1024], F32),
        ("z1ag", [NTOK, D], BF16), ("srawag", [NTOK, E], F32),
        ("contribZ", [NTOK, D], BF16), ("contribOwn", [TOK, D], BF16),
        ("scolE", [NTOK, 1], F32),
    ]:
        ext[n] = IN(n, sh, dt)
    ext["z1_all"] = ext["z1ag"]

    _, dA = build_A(nc, ext, entry=())
    _, dB = build_B(nc, ext, entry=[dA])
    _, dC = build_C(nc, ext, entry=[dA, dB], fused=True)
    _, dE = build_E(nc, ext, entry=[dC], fused=True)
    _, dF = build_F(nc, ext, entry=[dC, dE], fused=True)
    return _lower_ext(nc)


def kernel(**inp):
    f32 = lambda a: np.ascontiguousarray(np.asarray(a, np.float32))
    x = f32(inp["L0_attn_x"])
    i = np.arange(D)
    bind_key = f32(inp["L1_attn_bind_key"])
    unbind_key = f32(inp["L1_attn_unbind_key"])
    Mb = np.ascontiguousarray(bind_key[(i[None, :] - i[:, None]) % D])
    Mu = np.ascontiguousarray(unbind_key[(i[:, None] - i[None, :]) % D])
    WqT = np.ascontiguousarray(f32(inp["L0_attn_Wq"]).T)
    WkT = np.ascontiguousarray(f32(inp["L0_attn_Wk"]).T)
    WvT = np.ascontiguousarray(f32(inp["L0_attn_Wv"]).T)
    usmat = np.triu(np.ones((128, 128), np.float32), 1)
    gi, ei = np.divmod(np.arange(128), 8)
    bsmat = ((ei[:, None] == ei[None, :]) & (gi[:, None] < gi[None, :])).astype(np.float32)
    ecr = (ei * C0).astype(np.float32)[None, :]
    gw1T = np.ascontiguousarray(f32(inp["L1_ffn_gate_w"]).T)
    fc1s = f32(inp["L1_ffn_fc1s"]); b1s = f32(inp["L1_ffn_b1s"])
    fc2s = f32(inp["L1_ffn_fc2s"]); b2s = f32(inp["L1_ffn_b2s"])
    decay = f32(inp["L1_attn_decay"]).reshape(1, 1)
    gate = f32(inp["L1_attn_gate"]).reshape(1, 1)
    xf = x.reshape(NTOK, D)

    nc = _get("FULL", build_full)
    maps = []
    for c in range(NC):
        b, h = c // 2, c % 2
        oh = np.zeros(8, np.float32); oh[c] = 1.0
        csel = np.zeros((8, 1), np.float32)
        if h == 1:
            csel[c - 1, 0] = 1.0
        maps.append({
            "x_own": np.ascontiguousarray(x[b, h * TOK:(h + 1) * TOK, :]),
            "x_other": np.ascontiguousarray(x[b, (1 - h) * TOK:(2 - h) * TOK, :]),
            "WqT": WqT, "WkT": WkT, "WvT": WvT,
            "ln_w": f32(inp["L0_ffn_ln_w"]), "ln_b": f32(inp["L0_ffn_ln_b"]),
            "router_w": f32(inp["L0_ffn_router_w"]), "gate_w": f32(inp["L0_ffn_gate_w"]),
            "up_w": f32(inp["L0_ffn_up_w"]), "down_w": f32(inp["L0_ffn_down_w"]),
            "sh_gate": f32(inp["L0_ffn_sh_gate"]), "sh_up": f32(inp["L0_ffn_sh_up"]),
            "sh_down": f32(inp["L0_ffn_sh_down"]), "Ustrict": usmat, "blkstrict": bsmat,
            "ecrow": ecr,
            "M_bind": Mb, "gw1T": gw1T, "lgam": f32(inp["L1_ffn_ln_gamma"]),
            "lbeta": f32(inp["L1_ffn_ln_beta"]), "decay": decay,
            "fc1T": np.ascontiguousarray(fc1s[c].T),
            "b1": np.ascontiguousarray(b1s[c].reshape(HID1 // 128, 128).T),
            "fc2T": np.ascontiguousarray(fc2s[c].T), "b2": b2s[c],
            "esel8": np.ascontiguousarray(np.tile(oh, (128, 1))),
            "esel_tiled": np.ascontiguousarray(np.tile(oh, (128, 128))),
            "carry_sel": csel, "M_unbind": Mu, "gate": gate,
        })
    res = run_bass_kernel_spmd(nc, maps, core_ids=list(range(NC))).results
    out = np.concatenate([res[c]["x2_own"] for c in range(NC)], axis=0)
    return out.reshape(B, S, D)


# version 07 (final)
# speedup vs baseline: 1.0220x; 1.0220x over previous
import numpy as np
from contextlib import ExitStack

import bass_rust as _bass_rust
import concourse.bass as bass
import concourse.mybir as mybir
from concourse.bass import IndirectOffsetOnAxis
from concourse.bass_utils import run_bass_kernel_spmd
from concourse import library_config as _libcfg

F32 = mybir.dt.float32
BF16 = mybir.dt.bfloat16
U32 = mybir.dt.uint32
I16 = mybir.dt.int16
AF = mybir.ActivationFunctionType
ALU = mybir.AluOpType
AXX = mybir.AxisListType.X


def _lower_ext(nc):
    m = {}
    for lib in _libcfg.all_libraries:
        for it in lib.instructions:
            m[it] = m.get(it, 0) | (1 << lib.index)
    _bass_rust.insert_library_loads(nc, m, len(_libcfg.all_libraries), _libcfg.standard.index)
    mybir.codegen_inst_isa_subclasses(nc)
    return nc

B, S, D, NH, DK = 4, 4096, 1024, 16, 64
TOK = 2048
NT = TOK // 128
NC = 8
E = 8
I0 = 2048
C0 = 384
HID1 = 4096
K1 = 2048
LN_EPS = 1e-5


class Sem:
    def __init__(self, es, nc, name):
        self.h = es.enter_context(nc.semaphore(name))
        self.n = 0

    def inc(self, inst, k=1):
        inst.then_inc(self.h, k)
        self.n += k
        return self.n


class P:
    def __init__(self, nc, es, block=None, pfx=""):
        self.nc, self.es, self.block = nc, es, block
        self.pfx = pfx
        self._sems = {}
        for nm in ["PE", "VE", "AC", "PL", "CC"]:
            setattr(self, nm, self.S(nm))

    def bind(self, block):
        self.block = block
        return self

    def S(self, nm):
        if nm not in self._sems:
            self._sems[nm] = Sem(self.es, self.nc, self.pfx + nm)
        return self._sems[nm]

    def sync(self, fn): self.block.sync(fn)
    def gps(self, fn): self.block.gpsimd(fn)
    def vec(self, fn): self.block.vector(fn)
    def sca(self, fn): self.block.scalar(fn)
    def ten(self, fn): self.block.tensor(fn)


def emit_ln(p, xs, xn_out, scr, out_waits=(), eps=LN_EPS):
    VE, AC = p.VE, p.AC
    Dfree = 1024
    ac_entry = AC.n

    def v1(v):
        v.wait_ge(AC.h, ac_entry)
        v.reduce_sum(out=scr["musum"][:], in_=xs, axis=AXX)
        v.tensor_tensor(out=scr["sq"][:], in0=xs, in1=xs, op=ALU.mult)
        v.tensor_scalar(out=scr["negmu"][:], in0=scr["musum"][:], scalar1=-1.0 / Dfree,
                        scalar2=None, op0=ALU.mult)
        v.reduce_sum(out=scr["ssq"][:], in_=scr["sq"][:], axis=AXX)
        v.tensor_scalar(out=scr["d"][:], in0=xs, scalar1=scr["negmu"][:, :],
                        scalar2=None, op0=ALU.add)
        v.tensor_scalar(out=scr["ex2"][:], in0=scr["ssq"][:], scalar1=1.0 / Dfree,
                        scalar2=float(eps), op0=ALU.mult, op1=ALU.add)
        v.tensor_tensor(out=scr["musq"][:], in0=scr["negmu"][:], in1=scr["negmu"][:],
                        op=ALU.mult)
        v.tensor_copy(out=scr["gap"][:], in_=scr["musum"][:])
        VE.inc(v.tensor_tensor(out=scr["veps"][:], in0=scr["ex2"][:], in1=scr["musq"][:],
                               op=ALU.subtract))
    p.vec(v1)
    m1 = VE.n

    def s2(s):
        s.wait_ge(VE.h, m1)
        AC.inc(s.activation(out=scr["sd"][:], in_=scr["veps"][:], func=AF.Sqrt))
    p.sca(s2)
    m4 = AC.n

    def v3(v):
        v.wait_ge(AC.h, m4)
        for sem, cnt in out_waits:
            v.wait_ge(sem.h, cnt)
        v.tensor_copy(out=scr["gap"][:], in_=scr["gap"][:])
        v.reciprocal(out=scr["rstd"][:], in_=scr["sd"][:])
        v.tensor_copy(out=scr["gap"][:], in_=scr["gap"][:])
        VE.inc(v.tensor_scalar(out=xn_out, in0=scr["d"][:], scalar1=scr["rstd"][:, :],
                               scalar2=None, op0=ALU.mult))
    p.vec(v3)
    return VE.n


def emit_wrap_idx(p, src16, slotT, w32, w16, ident_f, ps1, ps2, RW, ve_src):
    PE, VE = p.PE, p.VE

    def t1(t, m=ve_src):
        t.wait_ge(VE.h, m)
        PE.inc(t.transpose(out=ps1[0:16, 0:128], in_=src16, identity=ident_f[:]))
    p.ten(t1)
    pe1 = PE.n

    def v1(v, pe1=pe1):
        v.wait_ge(PE.h, pe1)
        VE.inc(v.tensor_copy(out=slotT, in_=ps1[0:16, 0:128]))
    p.vec(v1)
    ve1 = VE.n

    def t2(t, ve1=ve1):
        t.wait_ge(VE.h, ve1)
        for k in range(8):
            PE.inc(t.transpose(out=ps2[0:16, k * 16:(k + 1) * 16],
                               in_=slotT[:, 16 * k:16 * (k + 1)],
                               identity=ident_f[0:16, 0:16]))
    p.ten(t2)
    pe2 = PE.n

    def v2(v, pe2=pe2):
        v.wait_ge(PE.h, pe2)
        w32v = w32.rearrange("r (j k) -> r j k", k=8)
        for k in range(8):
            v.tensor_copy(out=w32v[:, :, k], in_=ps2[0:16, k * 16:(k + 1) * 16])
        v.tensor_copy(out=slotT, in_=slotT)
        v.tensor_copy(out=slotT, in_=slotT)
        VE.inc(v.tensor_copy(out=w16[0:16, :], in_=w32))
    p.vec(v2)
    ve2 = VE.n

    def s1(sy, ve2=ve2):
        sy.wait_ge(VE.h, ve2)
        for mrep in range(1, 8):
            RW.inc(sy.dma_start(out=w16[16 * mrep:16 * (mrep + 1), :],
                                in_=w16[0:16, :]), 16)
    p.sync(s1)
    return RW.n



def build_A(nc=None, ext=None, entry=()):
    standalone = nc is None
    if standalone:
        nc = bass.Bass("TRN2", target_bir_lowering=False, detect_race_conditions=False)
        ext = {}
        ext["x_own"] = nc.dram_tensor("x_own", [TOK, D], F32, kind="ExternalInput").ap()
        ext["x_other"] = nc.dram_tensor("x_other", [TOK, D], F32, kind="ExternalInput").ap()
        ext["WqT"] = nc.dram_tensor("WqT", [D, D], F32, kind="ExternalInput").ap()
        ext["WkT"] = nc.dram_tensor("WkT", [D, DK], F32, kind="ExternalInput").ap()
        ext["WvT"] = nc.dram_tensor("WvT", [D, DK], F32, kind="ExternalInput").ap()
        ext["ln_w"] = nc.dram_tensor("ln_w", [D], F32, kind="ExternalInput").ap()
        ext["ln_b"] = nc.dram_tensor("ln_b", [D], F32, kind="ExternalInput").ap()
        ext["xn_own"] = nc.dram_tensor("xn_own", [TOK, D], F32, kind="ExternalOutput").ap()
        ext["nm_own"] = nc.dram_tensor("nm_own", [TOK, D], F32, kind="ExternalOutput").ap()
        ext["att_own"] = nc.dram_tensor("att_own", [TOK, D], F32, kind="ExternalOutput").ap()
    x_own = ext["x_own"]
    x_oth = ext["x_other"]
    wq_d = ext["WqT"]
    wk_d = ext["WkT"]
    wv_d = ext["WvT"]
    lnw_d = ext["ln_w"]
    lnb_d = ext["ln_b"]
    xn_o = ext["xn_own"]
    nm_o = ext["nm_own"]
    att_o = ext["att_own"]

    es = ExitStack()
    sb = lambda n, s, d: es.enter_context(nc.sbuf_tensor("A" + n, s, d))
    psa = lambda n, s, d: es.enter_context(nc.psum_tensor("A" + n, s, d))
    with es:
        pid = sb("pid", [128, 1], F32)
        cid = sb("cid", [128, 128], F32)
        ident_bf = sb("ident_bf", [128, 128], BF16)
        ones_row = sb("ones_row", [1, 128], F32)
        one11 = sb("one11", [1, 1], F32)
        onescol = sb("onescol", [128, 1], F32)
        onescol_bf = sb("onescol_bf", [128, 1], BF16)
        wqT = sb("wqT", [128, 8, D], BF16)
        wkT = sb("wkT", [128, 8, DK], BF16)
        wvT = sb("wvT", [128, 8, DK], BF16)
        lnw_row = sb("lnw_row", [1, D], F32)
        lnb_row = sb("lnb_row", [1, D], F32)
        lnw_rep = sb("lnw_rep", [128, D], F32)
        lnb_rep = sb("lnb_rep", [128, D], F32)
        xs = sb("xs", [128, 2, D], F32)
        xnf = sb("xnf", [128, 2, D], F32)
        nmf = sb("nmf", [128, D], F32)
        nmf2 = sb("nmf2", [128, 2, D], F32)
        xnb = sb("xnb", [128, 2, D], BF16)
        xnTg = sb("xnTg", [128, 2, 8, 128], BF16)
        qT = sb("qT", [128, 8, TOK], BF16)
        kT = sb("kT", [128, 32, 128], BF16)
        vsb = sb("vsb", [128, 32, 65], BF16)
        q2 = sb("q2", [64, TOK], BF16)
        bq = sb("bq", [1, 16], F32)
        kq = sb("kq", [1, 1], F32)
        mscr = sb("mscr", [1, 1], F32)
        negB = sb("negB", [128, 16], F32)
        pbuf = sb("pbuf", [128, 2, 2, 512], BF16)
        attscr = sb("attscr", [64, 512], BF16)
        srow = sb("srow", [1, 512], F32)
        r4 = sb("r4", [128, 4], F32)
        att_sb = sb("att_sb", [128, 16, D], F32)
        lns = dict(musum=sb("musum", [128, 1], F32), negmu=sb("negmu", [128, 1], F32),
                   ssq=sb("ssq", [128, 1], F32), veps=sb("veps", [128, 1], F32),
                   sd=sb("sd", [128, 1], F32), rstd=sb("rstd", [128, 1], F32),
                   d=sb("lnd", [128, D], F32), sq=sb("sq", [128, D], F32),
                   ex2=sb("lnex2", [128, 1], F32), musq=sb("lnmusq", [128, 1], F32),
                   gap=sb("lngap", [128, 1], F32))
        psA = psa("psA", [128, 512], F32)
        psB = psa("psB", [128, 512], BF16)
        psC = psa("psC", [128, 512], F32)
        psD = psa("psD", [128, 512], F32)
        psE = psa("psE", [128, 512], F32)
        psF = psa("psF", [128, 512], F32)
        psG = psa("psG", [128, 512], F32)
        psH = psa("psH", [128, 512], F32)
        SCB = [psC, psD, psE, psF]

        with nc.Block() as block:
            p = ext["_P"].bind(block) if "_P" in ext else P(nc, es, block, pfx="A_")
            DONE = p.S("DONE")
            for _eng in [block.sync, block.gpsimd, block.vector, block.scalar, block.tensor]:
                def _bar(e, entry=entry):
                    for _sh, _cnt in entry:
                        e.wait_ge(_sh, _cnt)
                _eng(_bar)
            PE, VE, AC, PL = p.PE, p.VE, p.AC, p.PL
            LG, LW = p.S("LG"), p.S("LW")
            LDp = [p.S("LD0"), p.S("LD1")]
            SXp = [p.S("SX0"), p.S("SX1")]
            SNp = [p.S("SN0"), p.S("SN1")]
            SA = p.S("SA")

            def gload(g):
                LG.inc(g.dma_start(out=wqT[:], in_=wq_d.rearrange("(c p) q -> p c q", p=128)), 16)
                LG.inc(g.dma_start(out=wkT[:], in_=wk_d.rearrange("(c p) q -> p c q", p=128)), 16)
                LG.inc(g.dma_start(out=wvT[:], in_=wv_d.rearrange("(c p) q -> p c q", p=128)), 16)
                g.iota(out=pid[:], pattern=[[0, 1]], base=0, channel_multiplier=1,
                       allow_small_or_imprecise_dtypes=True)
                PL.inc(g.iota(out=cid[:], pattern=[[1, 128]], base=0, channel_multiplier=0,
                              allow_small_or_imprecise_dtypes=True))
            p.gps(gload)
            lg_w = LG.n
            pl_c = PL.n

            def sload(sy):
                LW.inc(sy.dma_start(out=lnw_row[:], in_=lnw_d[None, :]), 16)
                LW.inc(sy.dma_start(out=lnb_row[:], in_=lnb_d[None, :]), 16)
            p.sync(sload)
            ld_ln = LW.n

            def vcst(v):
                v.wait_ge(PL.h, pl_c)
                v.memset(ones_row[:], 1.0)
                v.memset(one11[:], 1.0)
                v.memset(onescol[:], 1.0)
                v.memset(onescol_bf[:], 1.0)
                v.memset(bq[:], 0.0)
                v.memset(kq[:], 0.0)
                v.memset(lns["gap"][:], 0.0)
                v.memset(mscr[:], 0.0)
                v.memset(srow[:], 0.0)
                v.memset(vsb[:, :, 64:65], 1.0)
                VE.inc(v.tensor_scalar(out=ident_bf[:], in0=cid[:], scalar1=pid[:, :],
                                       scalar2=None, op0=ALU.is_equal))
            p.vec(vcst)
            ve_cst = VE.n

            prev_rep = 0
            for (row, rep) in [(lnw_row, lnw_rep), (lnb_row, lnb_rep)]:
                for i in range(2):
                    def tr(t, row=row, i=i, prev_rep=prev_rep):
                        t.wait_ge(VE.h, ve_cst)
                        t.wait_ge(VE.h, prev_rep)
                        t.wait_ge(LW.h, ld_ln)
                        PE.inc(t.matmul(out=psA[:, 0:512], lhsT=ones_row[:],
                                        rhs=row[0:1, i * 512:(i + 1) * 512], start=True, stop=True))
                    p.ten(tr)
                    mpe = PE.n

                    def vr(v, rep=rep, i=i, mpe=mpe):
                        v.wait_ge(PE.h, mpe)
                        VE.inc(v.tensor_copy(out=rep[:, i * 512:(i + 1) * 512], in_=psA[:, 0:512]))
                    p.vec(vr)
                    prev_rep = VE.n
            rep_done = VE.n

            xs_free = [0, 0]
            xnf_free = [0, 0]
            xnf_st = [0, 0]
            xnb_free = [0, 0]
            xnTg_free = [0, 0]
            nmf2_st = [0, 0]
            for gt in range(32):
                own = gt < 16
                g = gt if own else gt - 16
                src = x_own if own else x_oth
                pp = gt % 2

                mfree = xs_free[pp]

                LD = LDp[pp]

                def syl(sy, g=g, src=src, pp=pp, mfree=mfree, LD=LD):
                    sy.wait_ge(VE.h, mfree)
                    LD.inc(sy.dma_start(out=xs[:, pp, :], in_=src[g * 128:(g + 1) * 128, :]), 16)
                p.sync(syl)
                ld_m = LD.n

                def vwt(v, ld_m=ld_m, LD=LD):
                    v.wait_ge(LD.h, ld_m)
                p.vec(vwt)
                ve_xn = emit_ln(p, xs[:, pp, :], xnf[:, pp, :], lns,
                                out_waits=[(SXp[pp], xnf_st[pp])])

                if own:
                    def syo(sy, g=g, pp=pp, ve_xn=ve_xn):
                        sy.wait_ge(VE.h, ve_xn)
                        SXp[pp].inc(sy.dma_start(out=xn_o[g * 128:(g + 1) * 128, :],
                                                 in_=xnf[:, pp, :]), 16)
                    p.sync(syo)
                    xnf_st[pp] = SXp[pp].n

                def vb(v, pp=pp, pm=xnb_free[pp]):
                    v.wait_ge(PE.h, pm)
                    VE.inc(v.tensor_copy(out=xnb[:, pp, :], in_=xnf[:, pp, :]))
                p.vec(vb)
                ve_xnb = VE.n
                xs_free[pp] = ve_xnb

                evac_marks = []
                for half in range(2):
                    wv = evac_marks[-1] if evac_marks else None

                    def tt(t, half=half, pp=pp, ve_xnb=ve_xnb, wv=wv):
                        t.wait_ge(VE.h, ve_xnb if wv is None else wv)
                        for i in range(4):
                            blk = half * 4 + i
                            PE.inc(t.transpose(out=psB[:, i * 128:(i + 1) * 128],
                                               in_=xnb[:, pp, blk * 128:(blk + 1) * 128],
                                               identity=ident_bf[:]))
                    p.ten(tt)
                    pe_t = PE.n

                    def vt(v, half=half, pp=pp, pe_t=pe_t):
                        v.wait_ge(PE.h, pe_t)
                        VE.inc(v.tensor_copy(
                            out=xnTg[:, pp, half * 4:(half + 1) * 4, :].rearrange("p a b -> p (a b)"),
                            in_=psB[:, 0:512]))
                    p.vec(vt)
                    evac_marks.append(VE.n)
                ve_xt = VE.n
                xnb_free[pp] = pe_t

                xnb_free[pp] = PE.n

                def tk(t, pp=pp, ve_xt=ve_xt, lg_w=lg_w):
                    t.wait_ge(VE.h, ve_xt)
                    t.wait_ge(LG.h, lg_w)
                    for i in range(8):
                        PE.inc(t.matmul(out=psA[0:64, 0:128], lhsT=wkT[:, i, :],
                                        rhs=xnTg[:, pp, i, :], start=(i == 0), stop=(i == 7)))
                p.ten(tk)
                pe_k = PE.n

                def vk(v, gt=gt, pe_k=pe_k):
                    v.wait_ge(PE.h, pe_k)
                    VE.inc(v.tensor_copy(out=kT[0:64, gt, :], in_=psA[0:64, 0:128]))
                    VE.inc(v.tensor_copy(out=kT[64:128, gt, :], in_=psA[0:64, 0:128]))
                p.vec(vk)
                ve_k = VE.n

                def tv(t, pp=pp, ve_k=ve_k):
                    t.wait_ge(VE.h, ve_k)
                    for i in range(8):
                        PE.inc(t.matmul(out=psG[:, 0:64], lhsT=xnTg[:, pp, i, :],
                                        rhs=wvT[:, i, :], start=(i == 0), stop=(i == 7)))
                p.ten(tv)
                pe_v = PE.n

                def vv(v, gt=gt, pe_v=pe_v):
                    v.wait_ge(PE.h, pe_v)
                    VE.inc(v.tensor_copy(out=vsb[:, gt, 0:64], in_=psG[:, 0:64]))
                p.vec(vv)
                ve_v = VE.n

                if own:
                    for qd in range(2):
                        def tq(t, qd=qd, pp=pp, ve_v=ve_v):
                            t.wait_ge(VE.h, ve_v)
                            for sub in range(4):
                                i = qd * 4 + sub
                                for k in range(8):
                                    PE.inc(t.matmul(out=SCB[sub][:, 0:128],
                                                    lhsT=wqT[:, k, i * 128:(i + 1) * 128],
                                                    rhs=xnTg[:, pp, k, :],
                                                    start=(k == 0), stop=(k == 7)))
                        p.ten(tq)
                        pe_q = PE.n

                        def vq(v, qd=qd, g=g, pe_q=pe_q):
                            v.wait_ge(PE.h, pe_q)
                            for sub in range(4):
                                i = qd * 4 + sub
                                VE.inc(v.tensor_copy(out=qT[:, i, g * 128:(g + 1) * 128],
                                                     in_=SCB[sub][:, 0:128]))
                        p.vec(vq)
                        ve_v = VE.n
                    xnTg_free[pp] = PE.n

                    emit_ln(p, xnf[:, pp, :], nmf[:], lns)

                    def vaff(v, pp=pp, rep_done=rep_done, stm=nmf2_st[pp]):
                        v.wait_ge(VE.h, rep_done)
                        v.wait_ge(SNp[pp].h, stm)
                        v.tensor_tensor(out=nmf2[:, pp, :], in0=nmf[:], in1=lnw_rep[:], op=ALU.mult)
                        VE.inc(v.tensor_tensor(out=nmf2[:, pp, :], in0=nmf2[:, pp, :],
                                               in1=lnb_rep[:], op=ALU.add))
                    p.vec(vaff)
                    ve_nm = VE.n

                    def syn(sy, g=g, pp=pp, ve_nm=ve_nm):
                        sy.wait_ge(VE.h, ve_nm)
                        SNp[pp].inc(sy.dma_start(out=nm_o[g * 128:(g + 1) * 128, :],
                                                 in_=nmf2[:, pp, :]), 16)
                    p.sync(syn)
                    nmf2_st[pp] = SNp[pp].n
                else:
                    xnTg_free[pp] = PE.n

            prev_vbq = VE.n
            for h in range(16):
                def sqi(s, h=h, pw=prev_vbq):
                    s.wait_ge(VE.h, pw)
                    r0 = (h % 2) * 64
                    AC.inc(s.activation(out=q2[:, :], in_=qT[r0:r0 + 64, h // 2, :],
                                        func=AF.Square))
                p.sca(sqi)
                ac_q2 = AC.n
                for qc in range(4):
                    def tb(t, qc=qc, ac_q2=ac_q2, pw=prev_vbq):
                        t.wait_ge(AC.h, ac_q2)
                        t.wait_ge(VE.h, pw)
                        PE.inc(t.matmul(out=psA[0:1, 0:512], lhsT=onescol_bf[0:64, :],
                                        rhs=q2[:, qc * 512:(qc + 1) * 512], start=True, stop=True))
                    p.ten(tb)
                    pe_b = PE.n

                    def vbq(v, h=h, pe_b=pe_b):
                        v.wait_ge(PE.h, pe_b)
                        v.reduce_max(out=mscr[:], in_=psA[0:1, 0:512], axis=AXX)
                        VE.inc(v.tensor_tensor(out=bq[0:1, h:h + 1], in0=bq[0:1, h:h + 1],
                                               in1=mscr[:], op=ALU.max))
                    p.vec(vbq)
                    prev_vbq = VE.n
            for kk in range(4):
                def sqk(s, kk=kk, pw=prev_vbq):
                    s.wait_ge(VE.h, pw)
                    AC.inc(s.activation(out=q2[:, 0:1024],
                                        in_=kT[0:64, kk * 8:(kk + 1) * 8, :].rearrange("p a b -> p (a b)"),
                                        func=AF.Square))
                p.sca(sqk)
                ac_k2 = AC.n
                for qc in range(2):
                    def tkb(t, qc=qc, ac_k2=ac_k2, pw=prev_vbq):
                        t.wait_ge(AC.h, ac_k2)
                        t.wait_ge(VE.h, pw)
                        PE.inc(t.matmul(out=psA[0:1, 0:512], lhsT=onescol_bf[0:64, :],
                                        rhs=q2[:, qc * 512:(qc + 1) * 512], start=True, stop=True))
                    p.ten(tkb)
                    pe_kb = PE.n

                    def vkq(v, pe_kb=pe_kb):
                        v.wait_ge(PE.h, pe_kb)
                        v.tensor_copy(out=srow[0:1, 0:1], in_=kq[:])
                        v.reduce_max(out=mscr[:], in_=psA[0:1, 0:512], axis=AXX)
                        v.tensor_copy(out=srow[0:1, 0:1], in_=kq[:])
                        VE.inc(v.tensor_tensor(out=kq[:], in0=kq[:], in1=mscr[:], op=ALU.max))
                    p.vec(vkq)
                    prev_vbq = VE.n

            def vfin(v, pw=prev_vbq):
                v.wait_ge(VE.h, pw)
                v.tensor_copy(out=mscr[:], in_=kq[:])
                VE.inc(v.tensor_scalar(out=bq[:], in0=bq[:], scalar1=kq[0:1, :], scalar2=None,
                                       op0=ALU.mult))
            p.vec(vfin)
            ve_b1 = VE.n

            def sfin(s, ve_b1=ve_b1):
                s.wait_ge(VE.h, ve_b1)
                AC.inc(s.activation(out=bq[:], in_=bq[:], func=AF.Sqrt))
            p.sca(sfin)
            ac_b = AC.n

            def vfin2(v, ac_b=ac_b):
                v.wait_ge(AC.h, ac_b)
                v.tensor_copy(out=mscr[:], in_=mscr[:])
                VE.inc(v.tensor_scalar(out=bq[:], in0=bq[:], scalar1=-0.125, scalar2=None,
                                       op0=ALU.mult))
            p.vec(vfin2)
            ve_b2 = VE.n

            def trep(t, ve_b2=ve_b2):
                t.wait_ge(VE.h, ve_b2)
                PE.inc(t.matmul(out=psA[:, 0:16], lhsT=ones_row[:], rhs=bq[0:1, :],
                                start=True, stop=True))
            p.ten(trep)
            pe_nb = PE.n

            def vnb(v, pe_nb=pe_nb):
                v.wait_ge(PE.h, pe_nb)
                VE.inc(v.tensor_copy(out=negB[:], in_=psA[:, 0:16]))
            p.vec(vnb)
            ve_negB = VE.n

            sc_free_ac = {0: 0, 1: 0}
            last_vev = 0
            last_vacc = 0
            last_ver = 0
            for h in range(16):
                for qc in range(4):
                    pending_av = []
                    for kcg in range(16):
                        gp = kcg % 2
                        bank0, bank1 = (psC, psD) if gp == 0 else (psE, psF)
                        pb = pbuf[:, gp, :, :]
                        ac_prev = sc_free_ac[gp]

                        def tqk(t, h=h, qc=qc, kcg=kcg, bank0=bank0, bank1=bank1,
                                ac_prev=ac_prev, ve_negB=ve_negB):
                            t.wait_ge(VE.h, ve_negB)
                            if ac_prev:
                                t.wait_ge(AC.h, ac_prev)
                            r0 = (h % 2) * 64
                            for j in range(2):
                                kc = kcg * 2 + j
                                bank = bank0 if j == 0 else bank1
                                PE.inc(t.matmul(out=bank[:, 0:512], lhsT=kT[r0:r0 + 64, kc, :],
                                                rhs=qT[r0:r0 + 64, h // 2, qc * 512:(qc + 1) * 512],
                                                start=True, stop=True))
                        p.ten(tqk)
                        pe_qk = PE.n

                        def sexp(s, h=h, bank0=bank0, bank1=bank1, pb=pb, pe_qk=pe_qk):
                            s.wait_ge(PE.h, pe_qk)
                            s.activation(out=pb[:, 0, :], in_=bank0[:, 0:512], func=AF.Exp,
                                         scale=0.125, bias=negB[:, h:h + 1])
                            AC.inc(s.activation(out=pb[:, 1, :], in_=bank1[:, 0:512], func=AF.Exp,
                                                scale=0.125, bias=negB[:, h:h + 1]))
                        p.sca(sexp)
                        sc_free_ac[gp] = AC.n
                        ac_exp = AC.n

                        def tav(t, kcg=kcg, pb=pb, ac_exp=ac_exp, lv=last_vev):
                            t.wait_ge(AC.h, ac_exp)
                            if kcg == 0:
                                t.wait_ge(VE.h, lv)
                            for j in range(2):
                                kc = kcg * 2 + j
                                PE.inc(t.matmul(out=psG[0:65, 0:512], lhsT=vsb[:, kc, :],
                                                rhs=pb[:, j, :], start=(kc == 0), stop=(kc == 31)))
                        pending_av.append(tav)
                        if kcg >= 1:
                            p.ten(pending_av.pop(0))
                    p.ten(pending_av.pop(0))
                    pe_av = PE.n

                    def vev(v, pe_av=pe_av, lpa=last_vacc):
                        v.wait_ge(PE.h, pe_av)
                        v.wait_ge(PE.h, lpa)
                        v.tensor_copy(out=attscr[:], in_=psG[0:64, 0:512])
                        VE.inc(v.tensor_copy(out=srow[:], in_=psG[64:65, 0:512]))
                    p.vec(vev)
                    last_vev = VE.n

                    def trt(t, lv=last_vev, lvr=last_ver):
                        t.wait_ge(VE.h, lv)
                        t.wait_ge(VE.h, lvr)
                        for tb in range(4):
                            PE.inc(t.transpose(out=psH[:, tb:tb + 1],
                                               in_=srow[0:1, tb * 128:(tb + 1) * 128],
                                               identity=one11[:]))
                    p.ten(trt)
                    pe_rt = PE.n

                    def vrc(v, pe_rt=pe_rt):
                        v.wait_ge(PE.h, pe_rt)
                        v.tensor_copy(out=mscr[:], in_=srow[0:1, 0:1])
                        v.reciprocal(out=r4[:], in_=psH[:, 0:4])
                        VE.inc(v.tensor_copy(out=mscr[:], in_=srow[0:1, 0:1]))
                    p.vec(vrc)
                    last_ver = VE.n

                    def tat(t, lv=last_ver):
                        t.wait_ge(VE.h, lv)
                        for tb in range(4):
                            PE.inc(t.transpose(out=psB[:, tb * 128:tb * 128 + 64],
                                               in_=attscr[:, tb * 128:(tb + 1) * 128],
                                               identity=ident_bf[0:64, 0:64]))
                    p.ten(tat)
                    pe_at = PE.n
                    last_vacc = pe_at

                    def vacc(v, h=h, qc=qc, pe_at=pe_at):
                        v.wait_ge(PE.h, pe_at)
                        for tb in range(4):
                            g = 4 * qc + tb
                            VE.inc(v.tensor_scalar(out=att_sb[:, g, h * 64:(h + 1) * 64],
                                                   in0=psB[:, tb * 128:tb * 128 + 64],
                                                   scalar1=r4[:, tb:tb + 1], scalar2=None,
                                                   op0=ALU.mult))
                    p.vec(vacc)

            ve_all = VE.n

            def syatt(sy, ve_all=ve_all):
                sy.wait_ge(VE.h, ve_all)
                for g in range(16):
                    SA.inc(sy.dma_start(out=att_o[g * 128:(g + 1) * 128, :], in_=att_sb[:, g, :]), 16)
                sy.wait_ge(SA.h, SA.n)
                sy.wait_ge(SXp[0].h, SXp[0].n)
                sy.wait_ge(SXp[1].h, SXp[1].n)
                sy.wait_ge(SNp[0].h, SNp[0].n)
                sy.wait_ge(SNp[1].h, SNp[1].n)
                DONE.inc(sy.nop())
            p.sync(syatt)

    return _lower_ext(nc) if standalone else (nc, (DONE.h, DONE.n))



def _prep_A_inputs(x, Wq, Wk, Wv, ln_w, ln_b):
    x = np.ascontiguousarray(np.asarray(x, np.float32))
    WqT = np.ascontiguousarray(np.asarray(Wq, np.float32).T)
    WkT = np.ascontiguousarray(np.asarray(Wk, np.float32).T)
    WvT = np.ascontiguousarray(np.asarray(Wv, np.float32).T)
    ln_w = np.ascontiguousarray(np.asarray(ln_w, np.float32))
    ln_b = np.ascontiguousarray(np.asarray(ln_b, np.float32))
    maps = []
    for c in range(NC):
        b, h = c // 2, c % 2
        xo = np.ascontiguousarray(x[b, h * TOK:(h + 1) * TOK, :])
        xt = np.ascontiguousarray(x[b, (1 - h) * TOK:(2 - h) * TOK, :])
        maps.append({"x_own": xo, "x_other": xt, "WqT": WqT, "WkT": WkT, "WvT": WvT,
                     "ln_w": ln_w, "ln_b": ln_b})
    return maps


def kernel(**inputs):
    raise NotImplementedError("under construction")



BIG_OOB = 4.0e6

def build_B(nc=None, ext=None, entry=()):
    standalone = nc is None
    if standalone:
        nc = bass.Bass("TRN2", target_bir_lowering=False, detect_race_conditions=False)
        ext = {}
        ext["nm_own"] = nc.dram_tensor("nm_own", [TOK, D], F32, kind="ExternalInput").ap()
        ext["router_w"] = nc.dram_tensor("router_w", [D, E], F32, kind="ExternalInput").ap()
        ext["gate_w"] = nc.dram_tensor("gate_w", [E, D, I0], F32, kind="ExternalInput").ap()
        ext["up_w"] = nc.dram_tensor("up_w", [E, D, I0], F32, kind="ExternalInput").ap()
        ext["down_w"] = nc.dram_tensor("down_w", [E, I0, D], F32, kind="ExternalInput").ap()
        ext["sh_gate"] = nc.dram_tensor("sh_gate", [D, I0], F32, kind="ExternalInput").ap()
        ext["sh_up"] = nc.dram_tensor("sh_up", [D, I0], F32, kind="ExternalInput").ap()
        ext["sh_down"] = nc.dram_tensor("sh_down", [I0, D], F32, kind="ExternalInput").ap()
        ext["Ustrict"] = nc.dram_tensor("Ustrict", [128, 128], F32, kind="ExternalInput").ap()
        ext["blkstrict"] = nc.dram_tensor("blkstrict", [128, 128], F32, kind="ExternalInput").ap()
        ext["ecrow"] = nc.dram_tensor("ecrow", [1, 128], F32, kind="ExternalInput").ap()
        ext["moe_own"] = nc.dram_tensor("moe_own", [TOK, D], F32, kind="ExternalOutput").ap()
    nm_d = ext["nm_own"]
    rw_d = ext["router_w"]
    gw_d = ext["gate_w"]
    uw_d = ext["up_w"]
    dw_d = ext["down_w"]
    shg_d = ext["sh_gate"]
    shu_d = ext["sh_up"]
    shd_d = ext["sh_down"]
    us_d = ext["Ustrict"]
    bs_d = ext["blkstrict"]
    ec_d = ext["ecrow"]
    moe_o = ext["moe_own"]
    pack_dram = nc.dram_tensor("B_pack", [E * C0 + 128, D], BF16)
    pout_dram = nc.dram_tensor("B_pout", [E * C0 + 128, D], BF16)

    es = ExitStack()
    sb = lambda n, s_, d_: es.enter_context(nc.sbuf_tensor("B" + n, s_, d_))
    psa = lambda n, s_, d_: es.enter_context(nc.psum_tensor("B" + n, s_, d_))
    with es:
        ones_row = sb("ones_row", [1, 128], F32)
        onescol = sb("onescol", [128, 1], F32)
        nmT = sb("nmT", [128, 8, TOK], BF16)
        nmb = sb("nmb", [128, 16, D], BF16)
        us_s = sb("us_s", [128, 128], F32)
        bs_s = sb("bs_s", [128, 128], F32)
        ec_row = sb("ec_row", [1, 128], F32)
        rw32 = sb("rw32", [128, 8, E], F32)
        xT32 = sb("xT32", [128, 8, 128], F32)
        nmf = sb("nmf", [128, 2, D], F32)
        logit = sb("logit", [128, 16, E], F32)
        m8 = sb("m8", [128, 16, 8], F32)
        argi = sb("argi", [128, 16, 8], U32)
        argf = sb("argf", [128, 16], F32)
        wsig = sb("wsig", [128, 16], F32)
        mask_all = sb("mask_all", [128, 128], F32)
        prefx = sb("prefx", [128, 128], F32)
        tots = sb("tots", [128, 1], F32)
        coloff = sb("coloff", [128, 1], F32)
        corow = sb("corow", [1, 128], F32)
        rankf = sb("rankf", [128, 128], F32)
        ecrep = sb("ecrep", [128, 128], F32)
        valid = sb("valid", [128, 128], F32)
        mval = sb("mval", [128, 128], F32)
        t1 = sb("t1", [128, 128], F32)
        slotA = sb("slotA", [128, 16], F32)
        slotB = sb("slotB", [128, 16], F32)
        slotf = sb("slotf", [128, 16], F32)
        widx = sb("widx", [128, 128], I16)
        gapt = sb("gapt", [128, 1], F32)
        zwt = sb("zwt", [128, 2, D], BF16)
        moe_acc = sb("moe_acc", [128, 16, D], F32)
        shgp = sb("shgp", [128, 2, 8, 128], BF16)
        shup = sb("shup", [128, 2, 8, 128], BF16)
        hmid = sb("hmid", [128, 16, 512], BF16)
        hgt = sb("hgt", [128, 512], BF16)
        hge = sb("hge", [128, C0], BF16)
        orow = sb("orow", [128, D], BF16)
        ident_bf = sb("ident_bf", [128, 128], BF16)
        ident_f = sb("ident_f", [128, 128], F32)
        pid = sb("pid", [128, 1], F32)
        cid = sb("cid", [128, 128], F32)
        psA = psa("psA", [128, 512], F32)
        psB = psa("psB", [128, 512], BF16)
        psC = psa("psC", [128, 512], F32)
        psD = psa("psD", [128, 512], F32)
        psE = psa("psE", [128, 512], F32)
        psF = psa("psF", [128, 512], F32)

        with nc.Block() as block:
            p = ext["_P"].bind(block) if "_P" in ext else P(nc, es, block, pfx="B_")
            DONE = p.S("DONE")
            for _eng in [block.sync, block.gpsimd, block.vector, block.scalar, block.tensor]:
                def _bar(e, entry=entry):
                    for _sh, _cnt in entry:
                        e.wait_ge(_sh, _cnt)
                _eng(_bar)
            PE, VE, AC, PL = p.PE, p.VE, p.AC, p.PL
            LG = p.S("LG"); LW = p.S("LW")
            LDp = [p.S("LD0"), p.S("LD1")]
            LSH = p.S("LSH"); LSH2 = p.S("LSH2"); LSH3 = p.S("LSH3")
            LEX = p.S("LEX"); LEX2 = p.S("LEX2")
            LPK = p.S("LPK"); SCT = p.S("SCT"); SMO = p.S("SMO"); GIX = p.S("GIX")
            SPO = p.S("SPO")

            def gc(g):
                g.iota(out=pid[:], pattern=[[0, 1]], base=0, channel_multiplier=1,
                       allow_small_or_imprecise_dtypes=True)
                PL.inc(g.iota(out=cid[:], pattern=[[1, 128]], base=0, channel_multiplier=0,
                              allow_small_or_imprecise_dtypes=True))
            p.gps(gc)
            pl_c = PL.n
            lg_w = LG.n

            def sc(sy):
                LW.inc(sy.dma_start(out=rw32[:], in_=rw_d.rearrange("(c p) e -> p c e", p=128)), 16)
                LW.inc(sy.dma_start(out=us_s[:], in_=us_d[:, :]), 16)
                LW.inc(sy.dma_start(out=bs_s[:], in_=bs_d[:, :]), 16)
                LW.inc(sy.dma_start(out=ec_row[:], in_=ec_d[:, :]), 16)
            p.sync(sc)
            lw_c = LW.n

            def vc(v):
                v.wait_ge(PL.h, pl_c)
                v.memset(ones_row[:], 1.0)
                v.memset(onescol[:], 1.0)
                v.memset(gapt[:], 0.0)
                v.tensor_scalar(out=ident_bf[:], in0=cid[:], scalar1=pid[:, :],
                                scalar2=None, op0=ALU.is_equal)
                VE.inc(v.tensor_scalar(out=ident_f[:], in0=cid[:], scalar1=pid[:, :],
                                       scalar2=None, op0=ALU.is_equal))
            p.vec(vc)

            nmf_free = [0, 0]
            nmf_free_pe = [0, 0]
            for g in range(16):
                pp = g % 2
                LD = LDp[pp]

                def syl(sy, g=g, pp=pp, LD=LD, m=nmf_free[pp], mp=nmf_free_pe[pp]):
                    sy.wait_ge(VE.h, m)
                    sy.wait_ge(PE.h, mp)
                    LD.inc(sy.dma_start(out=nmf[:, pp, :], in_=nm_d[g * 128:(g + 1) * 128, :]), 16)
                p.sync(syl)
                ld_m = LD.n

                def vb(v, g=g, pp=pp, LD=LD, ld_m=ld_m):
                    v.wait_ge(LD.h, ld_m)
                    VE.inc(v.tensor_copy(out=nmb[:, g, :], in_=nmf[:, pp, :]))
                p.vec(vb)
                nmf_free[pp] = VE.n
                ve_b = VE.n

                evm = []
                for half in range(2):
                    wv = evm[-1] if evm else None

                    def tt(t, g=g, half=half, ve_b=ve_b, wv=wv):
                        t.wait_ge(VE.h, ve_b if wv is None else wv)
                        for i in range(4):
                            blk = half * 4 + i
                            PE.inc(t.transpose(out=psB[:, i * 128:(i + 1) * 128],
                                               in_=nmb[:, g, blk * 128:(blk + 1) * 128],
                                               identity=ident_bf[:]))
                    p.ten(tt)
                    pe_t = PE.n

                    def vt(v, g=g, half=half, pe_t=pe_t):
                        v.wait_ge(PE.h, pe_t)
                        for i in range(4):
                            blk = half * 4 + i
                            VE.inc(v.tensor_copy(out=nmT[:, blk, g * 128:(g + 1) * 128],
                                                 in_=psB[:, i * 128:(i + 1) * 128]))
                    p.vec(vt)
                    evm.append(VE.n)
                ve_tp = VE.n

                evm2 = []
                for half in range(2):
                    wv2 = evm2[-1] if evm2 else None

                    def tt32(t, g=g, pp=pp, half=half, wv2=wv2, m=VE.n):
                        t.wait_ge(VE.h, m if wv2 is None else wv2)
                        for i in range(4):
                            blk = half * 4 + i
                            PE.inc(t.transpose(out=psC[:, i * 128:(i + 1) * 128],
                                               in_=nmf[:, pp, blk * 128:(blk + 1) * 128],
                                               identity=ident_f[:]))
                    p.ten(tt32)
                    pe_t32 = PE.n

                    def vt32(v, half=half, pe_t32=pe_t32):
                        v.wait_ge(PE.h, pe_t32)
                        for i in range(4):
                            blk = half * 4 + i
                            VE.inc(v.tensor_copy(out=xT32[:, blk, :],
                                                 in_=psC[:, i * 128:(i + 1) * 128]))
                    p.vec(vt32)
                    evm2.append(VE.n)
                ve_x32 = VE.n
                nmf_free_pe[pp] = PE.n

                def tl(t, g=g, ve_x32=ve_x32, lw_c=lw_c):
                    t.wait_ge(VE.h, ve_x32)
                    t.wait_ge(LW.h, lw_c)
                    for c in range(8):
                        PE.inc(t.matmul(out=psA[:, 0:8], lhsT=xT32[:, c, :],
                                        rhs=rw32[:, c, :], start=(c == 0), stop=(c == 7)))
                p.ten(tl)
                pe_l = PE.n

                def vl(v, g=g, pe_l=pe_l):
                    v.wait_ge(PE.h, pe_l)
                    v.tensor_copy(out=gapt[:], in_=gapt[:])
                    VE.inc(v.tensor_copy(out=logit[:, g, :], in_=psA[:, 0:8]))
                p.vec(vl)

            ve_log = VE.n

            def vmax(v, ve_log=ve_log):
                v.wait_ge(VE.h, ve_log)
                for g in range(16):
                    VE.inc(v.max(out=m8[:, g, :], in_=logit[:, g, :]))
            p.vec(vmax)
            ve_m8 = VE.n

            def shop(s):
                s.wait_ge(VE.h, ve_m8)
                AC.inc(s.nop())
            p.sca(shop)
            ac_hop = AC.n

            def vmi(v, ac_hop=ac_hop):
                v.wait_ge(AC.h, ac_hop)
                for g in range(16):
                    VE.inc(v.max_index(out=argi[:, g, :], in_max=m8[:, g, :],
                                       in_values=logit[:, g, :]))
                v.tensor_copy(out=gapt[:], in_=gapt[:])
                VE.inc(v.tensor_copy(out=argf[:], in_=argi[:, :, 0]))
            p.vec(vmi)
            ve_arg = VE.n

            def ssig(s, ve_arg=ve_arg):
                s.wait_ge(VE.h, ve_arg)
                AC.inc(s.activation(out=wsig[:], in_=m8[:, :, 0], func=AF.Sigmoid))
            p.sca(ssig)
            ac_sig = AC.n

            def vmask(v):
                v.tensor_copy(out=gapt[:], in_=gapt[:])
                for e in range(E):
                    VE.inc(v.tensor_scalar(
                        out=mask_all.ap().rearrange("p (g e) -> p g e", e=8)[:, :, e],
                        in0=argf[:], scalar1=float(e), scalar2=None, op0=ALU.is_equal))
            p.vec(vmask)
            ve_mask = VE.n

            def tpre(t, ve_mask=ve_mask, lw_c=lw_c):
                t.wait_ge(VE.h, ve_mask)
                t.wait_ge(LW.h, lw_c)
                PE.inc(t.matmul(out=psC[:, 0:128], lhsT=us_s[:], rhs=mask_all[:],
                                start=True, stop=True))
                PE.inc(t.matmul(out=psD[0:128, 0:1], lhsT=mask_all[:], rhs=onescol[:],
                                start=True, stop=True))
            p.ten(tpre)
            pe_pre = PE.n

            def vpre(v, pe_pre=pe_pre):
                v.wait_ge(PE.h, pe_pre)
                v.tensor_copy(out=prefx[:], in_=psC[:, 0:128])
                v.tensor_copy(out=gapt[:], in_=gapt[:])
                VE.inc(v.tensor_copy(out=tots[:], in_=psD[:, 0:1]))
            p.vec(vpre)
            ve_tot = VE.n

            def tco(t, ve_tot=ve_tot):
                t.wait_ge(VE.h, ve_tot)
                PE.inc(t.matmul(out=psE[0:128, 0:1], lhsT=bs_s[:], rhs=tots[:],
                                start=True, stop=True))
            p.ten(tco)
            pe_co = PE.n

            def vco(v, pe_co=pe_co):
                v.wait_ge(PE.h, pe_co)
                v.tensor_copy(out=gapt[:], in_=gapt[:])
                VE.inc(v.tensor_copy(out=coloff[:], in_=psE[:, 0:1]))
            p.vec(vco)
            ve_co = VE.n

            def tct(t, ve_co=ve_co):
                t.wait_ge(VE.h, ve_co)
                PE.inc(t.matmul(out=psF[0:1, 0:128], lhsT=coloff[:], rhs=ident_f[:],
                                start=True, stop=True))
            p.ten(tct)
            pe_ct = PE.n

            def vct(v, pe_ct=pe_ct):
                v.wait_ge(PE.h, pe_ct)
                v.tensor_copy(out=gapt[:], in_=gapt[:])
                VE.inc(v.tensor_copy(out=corow[:], in_=psF[0:1, 0:128]))
            p.vec(vct)
            ve_ct = VE.n

            def trep(t, ve_ct=ve_ct):
                t.wait_ge(VE.h, ve_ct)
                PE.inc(t.matmul(out=psC[:, 0:128], lhsT=ones_row[:], rhs=corow[:],
                                start=True, stop=True))
                PE.inc(t.matmul(out=psD[:, 0:128], lhsT=ones_row[:], rhs=ec_row[:],
                                start=True, stop=True))
            p.ten(trep)
            pe_rep = PE.n

            def vrank(v, pe_rep=pe_rep, ac_sig=ac_sig):
                v.wait_ge(PE.h, pe_rep)
                v.wait_ge(AC.h, ac_sig)
                v.tensor_tensor(out=rankf[:], in0=prefx[:], in1=psC[:, 0:128], op=ALU.add)
                v.tensor_copy(out=ecrep[:], in_=psD[:, 0:128])
                v.tensor_scalar(out=valid[:], in0=rankf[:], scalar1=float(C0), scalar2=None,
                                op0=ALU.is_lt)
                v.tensor_tensor(out=mval[:], in0=valid[:], in1=mask_all[:], op=ALU.mult)
                v.tensor_tensor(out=t1[:], in0=rankf[:], in1=ecrep[:], op=ALU.add)
                v.tensor_tensor(out=t1[:], in0=t1[:], in1=mval[:], op=ALU.mult)
                v.tensor_reduce(out=slotA[:], in_=t1.ap().rearrange("p (g e) -> p g e", e=8),
                                axis=AXX, op=ALU.add)
                v.tensor_tensor(out=mval[:], in0=mask_all[:], in1=mval[:], op=ALU.subtract)
                v.tensor_reduce(out=slotB[:], in_=mval.ap().rearrange("p (g e) -> p g e", e=8),
                                axis=AXX, op=ALU.add)
                v.tensor_copy(out=gapt[:], in_=gapt[:])
                v.scalar_tensor_tensor(out=slotf[:], in0=slotB[:], scalar=BIG_OOB,
                                       in1=slotA[:], op0=ALU.mult, op1=ALU.add)
                v.tensor_copy(out=gapt[:], in_=gapt[:])
                VE.inc(v.tensor_scalar(out=slotf[:], in0=slotf[:], scalar1=float(E * C0),
                                       scalar2=None, op0=ALU.min))
            p.vec(vrank)
            ve_slot = VE.n

            BRW = p.S("BRW")
            xT32_flat = xT32.ap().rearrange("p a b -> p (a b)")
            brw_tot = emit_wrap_idx(p, slotf[:, :], xT32_flat[0:16, 0:128],
                                    xT32_flat[0:16, 128:256], widx, ident_f[:],
                                    psC, psD, BRW, ve_slot)

            ZP = p.S("ZP")

            def vzz(v):
                VE.inc(v.memset(zwt[:, 0, :], 0.0))
            p.vec(vzz)
            ve_zz = VE.n

            def szz(sy, ve_zz=ve_zz):
                sy.wait_ge(VE.h, ve_zz)
                for a in range(24):
                    ZP.inc(sy.dma_start(out=pack_dram[a * 128:(a + 1) * 128, :],
                                        in_=zwt[:, 0, :]), 16)
                ZP.inc(sy.dma_start(out=pout_dram[E * C0:E * C0 + 128, :],
                                    in_=zwt[:, 0, :]), 16)
            p.sync(szz)
            zp_all = ZP.n

            for g in range(16):
                def vzw(v, g=g, ve_slot=ve_slot):
                    v.wait_ge(VE.h, ve_slot)
                    if g < 2:
                        v.tensor_copy(out=gapt[:], in_=gapt[:])
                    VE.inc(v.tensor_scalar(out=nmb[:, g, :], in0=nmb[:, g, :],
                                           scalar1=wsig[:, g:g + 1], scalar2=None,
                                           op0=ALU.mult))
                p.vec(vzw)
            ve_zw_all = VE.n

            def gsc(g_, ve_zw_all=ve_zw_all, zp_all=zp_all, brw_tot=brw_tot):
                g_.wait_ge(VE.h, ve_zw_all)
                g_.wait_ge(ZP.h, zp_all)
                g_.wait_ge(BRW.h, brw_tot)
                SCT.inc(g_.dma_scatter_add(pack_dram[:, :], nmb[:, :, :], widx[:, :],
                                           TOK, TOK, D), 16)
            p.gps(gsc)
            sct_all = SCT.n

            shd_res = nmb

            def gshd(g_, sct_all=sct_all):
                g_.wait_ge(SCT.h, sct_all)
                LSH3.inc(g_.dma_start(out=shd_res[:],
                                      in_=shd_d.rearrange("(c p) d -> p c d", p=128)), 16)
            p.gps(gshd)
            lsh3_c = LSH3.n

            shp_free = [PE.n, PE.n]
            dn_evac = VE.n
            for tg in range(4):
                tsl = slice(tg * 512, (tg + 1) * 512)
                hm_marks = []
                for j in range(16):
                    pp = j % 2

                    def gg(g_, j=j, pp=pp, tg=tg, m=shp_free[pp]):
                        g_.wait_ge(PE.h, m)
                        LSH.inc(g_.dma_start(
                            out=shgp[:, pp, :, :],
                            in_=shg_d[:, j * 128:(j + 1) * 128]
                            .rearrange("(c p) h -> p c h", p=128)), 16)
                        LSH2.inc(g_.dma_start(
                            out=shup[:, pp, :, :],
                            in_=shu_d[:, j * 128:(j + 1) * 128]
                            .rearrange("(c p) h -> p c h", p=128)), 16)
                    p.gps(gg)
                    l1, l2 = LSH.n, LSH2.n

                    def tgu(t, j=j, pp=pp, tg=tg, l1=l1, l2=l2, ac_prev=AC.n, ve_prev=VE.n):
                        t.wait_ge(LSH.h, l1)
                        t.wait_ge(LSH2.h, l2)
                        t.wait_ge(AC.h, ac_prev)
                        t.wait_ge(VE.h, ve_prev)
                        for c in range(8):
                            PE.inc(t.matmul(out=psC[:, 0:512], lhsT=shgp[:, pp, c, :],
                                            rhs=nmT[:, c, tsl], start=(c == 0), stop=(c == 7)))
                        for c in range(8):
                            PE.inc(t.matmul(out=psD[:, 0:512], lhsT=shup[:, pp, c, :],
                                            rhs=nmT[:, c, tsl], start=(c == 0), stop=(c == 7)))
                    p.ten(tgu)
                    shp_free[pp] = PE.n
                    pe_gu = PE.n

                    def sgu(s, pe_gu=pe_gu):
                        s.wait_ge(PE.h, pe_gu)
                        AC.inc(s.activation(out=hgt[:], in_=psC[:, 0:512], func=AF.Silu))
                    p.sca(sgu)
                    ac_si = AC.n

                    def vgu(v, j=j, ac_si=ac_si):
                        v.wait_ge(AC.h, ac_si)
                        VE.inc(v.tensor_tensor(out=hmid[:, j, :], in0=hgt[:], in1=psD[:, 0:512],
                                               op=ALU.mult))
                    p.vec(vgu)
                ve_hm = VE.n

                for tc in range(4):
                    for dh in range(2):
                        bank = [psE, psF][dh]

                        def tdn(t, tg=tg, tc=tc, dh=dh, bank=bank, ve_hm=ve_hm,
                                lsh3_c=lsh3_c, vprev=dn_evac):
                            t.wait_ge(VE.h, ve_hm)
                            t.wait_ge(LSH3.h, lsh3_c)
                            t.wait_ge(VE.h, vprev)
                            for j in range(16):
                                PE.inc(t.matmul(out=bank[:, 0:512],
                                                lhsT=hmid[:, j, tc * 128:(tc + 1) * 128],
                                                rhs=shd_res[:, j, dh * 512:(dh + 1) * 512],
                                                start=(j == 0), stop=(j == 15)))
                        p.ten(tdn)
                        pe_dn = PE.n

                        def vdn(v, tg=tg, tc=tc, dh=dh, bank=bank, pe_dn=pe_dn):
                            v.wait_ge(PE.h, pe_dn)
                            VE.inc(v.tensor_copy(
                                out=moe_acc[:, tg * 4 + tc, dh * 512:(dh + 1) * 512],
                                in_=bank[:, 0:512]))
                        p.vec(vdn)
                        dn_evac = VE.n

            nmT_flat = nmT.ap().rearrange("p a b -> p (a b)")
            zpk = nmT_flat[:, 0:3072].rearrange("p (a d) -> p a d", d=D)
            zTe = nmT_flat[:, 3072:6144].rearrange("p (c s) -> p c s", s=C0)
            hmidE = nmT_flat[:, 6144:12288].rearrange("p (j s) -> p j s", s=C0)
            de_res = nmb
            pe_sh_done = PE.n
            pk_free = PE.n
            for e in range(E):
                def gpk(g_, e=e, sct_all=sct_all, m=pk_free):
                    g_.wait_ge(SCT.h, sct_all)
                    g_.wait_ge(PE.h, m)
                    LPK.inc(g_.dma_start(out=zpk[:],
                                         in_=pack_dram[e * C0:(e + 1) * C0, :]
                                         .rearrange("(a p) d -> p a d", p=128)), 16)
                p.gps(gpk)
                lpk_m = LPK.n

                def gdw(g_, e=e, m=pe_sh_done):
                    g_.wait_ge(PE.h, m)
                    LSH3.inc(g_.dma_start(out=de_res[:],
                                          in_=dw_d[e].rearrange("(c p) d -> p c d", p=128)), 16)
                p.gps(gdw)
                ldw_m = LSH3.n

                ztev = VE.n
                for a in range(3):
                    for half in range(2):
                        def tz(t, a=a, half=half, lpk_m=lpk_m, vprev=ztev):
                            t.wait_ge(LPK.h, lpk_m)
                            t.wait_ge(VE.h, vprev)
                            for i in range(4):
                                c = half * 4 + i
                                PE.inc(t.transpose(out=psB[:, i * 128:(i + 1) * 128],
                                                   in_=zpk[:, a, c * 128:(c + 1) * 128],
                                                   identity=ident_bf[:]))
                        p.ten(tz)
                        pe_z = PE.n

                        def vz(v, a=a, half=half, pe_z=pe_z):
                            v.wait_ge(PE.h, pe_z)
                            for i in range(4):
                                c = half * 4 + i
                                VE.inc(v.tensor_copy(out=zTe[:, c, a * 128:(a + 1) * 128],
                                                     in_=psB[:, i * 128:(i + 1) * 128]))
                        p.vec(vz)
                        ztev = VE.n
                ve_zT = VE.n

                ex_free = [PE.n, PE.n]
                for j in range(16):
                    pp = j % 2

                    def ge(g_, e=e, j=j, pp=pp, m=ex_free[pp]):
                        g_.wait_ge(PE.h, m)
                        LEX.inc(g_.dma_start(
                            out=shgp[:, pp, :, :],
                            in_=gw_d[e, :, j * 128:(j + 1) * 128]
                            .rearrange("(c p) h -> p c h", p=128)), 16)
                        LEX2.inc(g_.dma_start(
                            out=shup[:, pp, :, :],
                            in_=uw_d[e, :, j * 128:(j + 1) * 128]
                            .rearrange("(c p) h -> p c h", p=128)), 16)
                    p.gps(ge)
                    le1, le2 = LEX.n, LEX2.n

                    def teg(t, j=j, pp=pp, le1=le1, le2=le2, ve_zT=ve_zT,
                            ac_prev=AC.n, ve_prev=VE.n):
                        t.wait_ge(LEX.h, le1)
                        t.wait_ge(LEX2.h, le2)
                        t.wait_ge(VE.h, ve_zT)
                        t.wait_ge(AC.h, ac_prev)
                        t.wait_ge(VE.h, ve_prev)
                        for c in range(8):
                            PE.inc(t.matmul(out=psC[:, 0:C0], lhsT=shgp[:, pp, c, :],
                                            rhs=zTe[:, c, :], start=(c == 0), stop=(c == 7)))
                        for c in range(8):
                            PE.inc(t.matmul(out=psD[:, 0:C0], lhsT=shup[:, pp, c, :],
                                            rhs=zTe[:, c, :], start=(c == 0), stop=(c == 7)))
                    p.ten(teg)
                    ex_free[pp] = PE.n
                    pe_eg = PE.n

                    def seg(s, pe_eg=pe_eg):
                        s.wait_ge(PE.h, pe_eg)
                        AC.inc(s.activation(out=hge[:], in_=psC[:, 0:C0], func=AF.Silu))
                    p.sca(seg)
                    ac_se = AC.n

                    def veg(v, j=j, ac_se=ac_se):
                        v.wait_ge(AC.h, ac_se)
                        VE.inc(v.tensor_tensor(out=hmidE[:, j, :], in0=hge[:], in1=psD[:, 0:C0],
                                               op=ALU.mult))
                    p.vec(veg)
                ve_hE = VE.n

                for tc in range(3):
                    spo_prev = SPO.n
                    for dh in range(2):
                        bank = [psE, psF][dh]

                        def tdne(t, tc=tc, dh=dh, bank=bank, ve_hE=ve_hE, ldw_m=ldw_m,
                                 vprev=dn_evac):
                            t.wait_ge(VE.h, ve_hE)
                            t.wait_ge(LSH3.h, ldw_m)
                            t.wait_ge(VE.h, vprev)
                            for j in range(16):
                                PE.inc(t.matmul(out=bank[:, 0:512],
                                                lhsT=hmidE[:, j, tc * 128:(tc + 1) * 128],
                                                rhs=de_res[:, j, dh * 512:(dh + 1) * 512],
                                                start=(j == 0), stop=(j == 15)))
                        p.ten(tdne)
                        pe_dne = PE.n

                        def vdne(v, tc=tc, dh=dh, bank=bank, pe_dne=pe_dne, sm=spo_prev):
                            v.wait_ge(PE.h, pe_dne)
                            v.wait_ge(SPO.h, sm)
                            VE.inc(v.tensor_copy(out=orow[:, dh * 512:(dh + 1) * 512],
                                                 in_=bank[:, 0:512]))
                        p.vec(vdne)
                        dn_evac = VE.n
                    ve_or = VE.n

                    def spo(sy, e=e, tc=tc, ve_or=ve_or):
                        sy.wait_ge(VE.h, ve_or)
                        SPO.inc(sy.dma_start(
                            out=pout_dram[e * C0 + tc * 128:e * C0 + (tc + 1) * 128, :],
                            in_=orow[:]), 16)
                    p.sync(spo)
                pe_sh_done = PE.n
                pk_free = PE.n

            spo_all = SPO.n

            rbuf = nmT_flat[:, 0:16384].rearrange("p (c d) -> p c d", d=D)

            def ggx(g_, spo_all=spo_all, zp_all=zp_all, brw_tot=brw_tot, m=PE.n):
                g_.wait_ge(SPO.h, spo_all)
                g_.wait_ge(ZP.h, zp_all)
                g_.wait_ge(BRW.h, brw_tot)
                g_.wait_ge(PE.h, m)
                GIX.inc(g_.dma_gather(rbuf[:, 0:8, :], pout_dram[:, :],
                                      widx[:, 0:64], 1024, 1024, D), 16)
                GIX.inc(g_.dma_gather(rbuf[:, 8:16, :], pout_dram[:, :],
                                      widx[:, 64:128], 1024, 1024, D), 16)
            p.gps(ggx)
            gix_m = GIX.n

            for g in range(16):
                def vadd(v, g=g, gix_m=gix_m):
                    v.wait_ge(GIX.h, gix_m)
                    VE.inc(v.tensor_tensor(out=moe_acc[:, g, :], in0=moe_acc[:, g, :],
                                           in1=rbuf[:, g, :], op=ALU.add))
                p.vec(vadd)
                ve_add = VE.n

                def smo(sy, g=g, ve_add=ve_add):
                    sy.wait_ge(VE.h, ve_add)
                    SMO.inc(sy.dma_start(out=moe_o[g * 128:(g + 1) * 128, :],
                                         in_=moe_acc[:, g, :]), 16)
                p.sync(smo)

            def fin(sy):
                sy.wait_ge(SMO.h, SMO.n)
                DONE.inc(sy.nop())
            p.sync(fin)

    return _lower_ext(nc) if standalone else (nc, (DONE.h, DONE.n))


def _prep_B_inputs(nm_by_core, router_w, gate_w, up_w, down_w, sh_gate, sh_up, sh_down):
    f = lambda a: np.ascontiguousarray(np.asarray(a, np.float32))
    us = np.triu(np.ones((128, 128), np.float32), 1)
    gi, ei = np.divmod(np.arange(128), 8)
    bs = ((ei[:, None] == ei[None, :]) & (gi[:, None] < gi[None, :])).astype(np.float32)
    ec = (ei * C0).astype(np.float32)[None, :]
    maps = []
    for c in range(NC):
        maps.append({"nm_own": f(nm_by_core[c]), "router_w": f(router_w), "gate_w": f(gate_w),
                     "up_w": f(up_w), "down_w": f(down_w), "sh_gate": f(sh_gate),
                     "sh_up": f(sh_up), "sh_down": f(sh_down), "Ustrict": us,
                     "blkstrict": bs, "ecrow": ec})
    return maps



def build_C(nc=None, ext=None, entry=(), fused=False):
    standalone = nc is None
    if standalone:
        nc = bass.Bass("TRN2", target_bir_lowering=False, detect_race_conditions=False)
        ext = {}
        ext["x_own"] = nc.dram_tensor("x_own", [TOK, D], F32, kind="ExternalInput").ap()
        ext["xn_own"] = nc.dram_tensor("xn_own", [TOK, D], F32, kind="ExternalInput").ap()
        ext["att_own"] = nc.dram_tensor("att_own", [TOK, D], F32, kind="ExternalInput").ap()
        ext["moe_own"] = nc.dram_tensor("moe_own", [TOK, D], F32, kind="ExternalInput").ap()
        ext["M_bind"] = nc.dram_tensor("M_bind", [D, D], F32, kind="ExternalInput").ap()
        ext["gw1T"] = nc.dram_tensor("gw1T", [D, E], F32, kind="ExternalInput").ap()
        ext["lgam"] = nc.dram_tensor("lgam", [D], F32, kind="ExternalInput").ap()
        ext["lbeta"] = nc.dram_tensor("lbeta", [D], F32, kind="ExternalInput").ap()
        ext["x1_own"] = nc.dram_tensor("x1_own", [TOK, D], F32, kind="ExternalOutput").ap()
        ext["xn1_own"] = nc.dram_tensor("xn1_own", [TOK, D], F32, kind="ExternalOutput").ap()
        ext["sraw_own"] = nc.dram_tensor("sraw_own", [TOK, E], F32, kind="ExternalOutput").ap()
        ext["boundT_own"] = nc.dram_tensor("boundT_own", [D, TOK], F32, kind="ExternalOutput").ap()
        ext["z1_own"] = nc.dram_tensor("z1_own", [TOK, D], BF16, kind="ExternalOutput").ap()
    x_d = ext["x_own"]
    xn_d = ext["xn_own"]
    att_d = ext["att_own"]
    moe_d = ext["moe_own"]
    mb_d = ext["M_bind"]
    gw_d = ext["gw1T"]
    lg_d = ext["lgam"]
    lb_d = ext["lbeta"]
    x1_o = ext["x1_own"]
    xn1_o = ext["xn1_own"]
    sr_o = ext["sraw_own"]
    bt_o = ext.get("boundT_own")
    z1_o = ext["z1_own"]

    es = ExitStack()
    sb = lambda n, s_, d_: es.enter_context(nc.sbuf_tensor("C" + n, s_, d_))
    psa = lambda n, s_, d_: es.enter_context(nc.psum_tensor("C" + n, s_, d_))
    with es:
        ones_row = sb("ones_row", [1, 128], F32)
        ident_bf = sb("ident_bf", [128, 128], BF16)
        ident_f = sb("ident_f", [128, 128], F32)
        pid = sb("pid", [128, 1], F32)
        cid = sb("cid", [128, 128], F32)
        xs = sb("xs", [128, 2, D], F32)
        ys = sb("ys", [128, 2, D], F32)
        res = sb("res", [128, 2, D], F32)
        xn1f = sb("xn1f", [128, 2, D], F32)
        xn1b = sb("xn1b", [128, D], BF16)
        n2 = sb("n2", [128, D], F32)
        z1t = sb("z1t", [128, 2, D], BF16)
        xn1T = sb("xn1T", [128, 8, TOK], BF16)
        xT32 = sb("xT32", [128, 8, 128], F32)
        gw32 = sb("gw32", [128, 8, E], F32)
        mb_s = sb("mb_s", [128, 8, D], BF16)
        lgr = sb("lgr", [128, D], F32)
        lbr = sb("lbr", [128, D], F32)
        lg_row = sb("lg_row", [1, D], F32)
        lb_row = sb("lb_row", [1, D], F32)
        sr_sb = sb("sr_sb", [128, 16, E], F32)
        btile = sb("btile", [128, 2, 2048], F32)
        prov = sb("prov", [128, 2048], F32)
        hbt = sb("hbt", [128, 2, 2048], BF16)
        dtile = sb("dtile", [128, 2048], F32)
        carry_sb = sb("carry_sb", [128, 8], F32)
        dsg = sb("dsg", [1, 1], F32)
        drep = sb("drep", [128, 1], F32)
        gapt = sb("gapt", [128, 1], F32)
        lns = dict(musum=sb("musum", [128, 1], F32), negmu=sb("negmu", [128, 1], F32),
                   ssq=sb("ssq", [128, 1], F32), veps=sb("veps", [128, 1], F32),
                   sd=sb("sd", [128, 1], F32), rstd=sb("rstd", [128, 1], F32),
                   d=sb("lnd", [128, D], F32), sq=sb("sq", [128, D], F32),
                   ex2=sb("lnex2", [128, 1], F32), musq=sb("lnmusq", [128, 1], F32),
                   gap=sb("lngap", [128, 1], F32))
        psA = psa("psA", [128, 512], F32)
        psB = psa("psB", [128, 512], BF16)
        psC = psa("psC", [128, 512], F32)
        psD = psa("psD", [128, 512], F32)
        psE = psa("psE", [128, 512], F32)
        psF = psa("psF", [128, 512], F32)

        with nc.Block() as block:
            p = ext["_P"].bind(block) if "_P" in ext else P(nc, es, block, pfx="C_")
            DONE = p.S("DONE")
            for _eng in [block.sync, block.gpsimd, block.vector, block.scalar, block.tensor]:
                def _bar(e, entry=entry):
                    for _sh, _cnt in entry:
                        e.wait_ge(_sh, _cnt)
                _eng(_bar)
            PE, VE, AC, PL = p.PE, p.VE, p.AC, p.PL
            LW = p.S("LW"); LG = p.S("LG")
            LD1 = [p.S("LA0"), p.S("LA1")]
            LD2 = [p.S("LB0"), p.S("LB1")]
            LD3 = [p.S("LC0"), p.S("LC1")]
            LD4 = [p.S("LDD0"), p.S("LDD1")]
            SX1 = [p.S("SX10"), p.S("SX11")]
            SXN = [p.S("SXN0"), p.S("SXN1")]
            SZ1 = [p.S("SZ10"), p.S("SZ11")]
            SBT = p.S("SBT")
            SSR = p.S("SSR")

            def gc(g):
                g.iota(out=pid[:], pattern=[[0, 1]], base=0, channel_multiplier=1,
                       allow_small_or_imprecise_dtypes=True)
                PL.inc(g.iota(out=cid[:], pattern=[[1, 128]], base=0, channel_multiplier=0,
                              allow_small_or_imprecise_dtypes=True))
                LG.inc(g.dma_start(out=mb_s[:], in_=mb_d.rearrange("(c p) j -> p c j", p=128)), 16)
            p.gps(gc)
            pl_c = PL.n
            lg_c = LG.n

            def sc(sy):
                LW.inc(sy.dma_start(out=gw32[:], in_=gw_d.rearrange("(c p) e -> p c e", p=128)), 16)
                LW.inc(sy.dma_start(out=lg_row[:], in_=lg_d[None, :]), 16)
                LW.inc(sy.dma_start(out=lb_row[:], in_=lb_d[None, :]), 16)
            p.sync(sc)
            lw_c = LW.n

            def vc(v):
                v.wait_ge(PL.h, pl_c)
                v.memset(ones_row[:], 1.0)
                v.memset(gapt[:], 0.0)
                v.memset(lns["gap"][:], 0.0)
                v.tensor_scalar(out=ident_bf[:], in0=cid[:], scalar1=pid[:, :],
                                scalar2=None, op0=ALU.is_equal)
                VE.inc(v.tensor_scalar(out=ident_f[:], in0=cid[:], scalar1=pid[:, :],
                                       scalar2=None, op0=ALU.is_equal))
            p.vec(vc)
            ve_cst = VE.n

            prev_rep = 0
            for (row, rep) in [(lg_row, lgr), (lb_row, lbr)]:
                for i in range(2):
                    def tr(t, row=row, i=i, prev_rep=prev_rep, ve_cst=ve_cst, lw_c=lw_c):
                        t.wait_ge(VE.h, max(ve_cst, prev_rep))
                        t.wait_ge(LW.h, lw_c)
                        PE.inc(t.matmul(out=psA[:, 0:512], lhsT=ones_row[:],
                                        rhs=row[0:1, i * 512:(i + 1) * 512], start=True, stop=True))
                    p.ten(tr)
                    mpe = PE.n

                    def vr(v, rep=rep, i=i, mpe=mpe):
                        v.wait_ge(PE.h, mpe)
                        VE.inc(v.tensor_copy(out=rep[:, i * 512:(i + 1) * 512], in_=psA[:, 0:512]))
                    p.vec(vr)
                    prev_rep = VE.n
            rep_done = VE.n

            xs_free = [0, 0]
            xs_free_pe = [0, 0]
            xn1b_free = [0]
            st_x1 = [0, 0]
            st_xn = [0, 0]
            st_z1 = [0, 0]
            for g in range(16):
                pp = g % 2

                def syl(sy, g=g, pp=pp, m=xs_free[pp], mp=xs_free_pe[pp],
                        s1=st_x1[pp], s2=st_xn[pp]):
                    sy.wait_ge(VE.h, m)
                    sy.wait_ge(PE.h, mp)
                    sy.wait_ge(SX1[pp].h, s1)
                    sy.wait_ge(SXN[pp].h, s2)
                    LD1[pp].inc(sy.dma_start(out=xs[:, pp, :],
                                             in_=x_d[g * 128:(g + 1) * 128, :]), 16)
                    LD2[pp].inc(sy.dma_start(out=ys[:, pp, :],
                                             in_=xn_d[g * 128:(g + 1) * 128, :]), 16)
                p.sync(syl)
                l1, l2 = LD1[pp].n, LD2[pp].n

                def vr1(v, pp=pp, l1=l1, l2=l2):
                    v.wait_ge(LD1[pp].h, l1)
                    v.wait_ge(LD2[pp].h, l2)
                    VE.inc(v.tensor_tensor(out=res[:, pp, :], in0=xs[:, pp, :], in1=ys[:, pp, :],
                                           op=ALU.add))
                p.vec(vr1)
                ve_r1 = VE.n

                def syl2(sy, g=g, pp=pp, ve_r1=ve_r1):
                    sy.wait_ge(VE.h, ve_r1)
                    LD3[pp].inc(sy.dma_start(out=xs[:, pp, :],
                                             in_=att_d[g * 128:(g + 1) * 128, :]), 16)
                    LD4[pp].inc(sy.dma_start(out=ys[:, pp, :],
                                             in_=moe_d[g * 128:(g + 1) * 128, :]), 16)
                p.sync(syl2)
                l3, l4 = LD3[pp].n, LD4[pp].n

                def vr2(v, pp=pp, l3=l3, l4=l4):
                    v.wait_ge(LD3[pp].h, l3)
                    v.wait_ge(LD4[pp].h, l4)
                    v.tensor_tensor(out=res[:, pp, :], in0=res[:, pp, :], in1=xs[:, pp, :],
                                    op=ALU.add)
                    VE.inc(v.tensor_tensor(out=res[:, pp, :], in0=res[:, pp, :], in1=ys[:, pp, :],
                                           op=ALU.add))
                p.vec(vr2)
                ve_x1 = VE.n
                xs_free[pp] = ve_x1

                def sx1(sy, g=g, pp=pp, ve_x1=ve_x1):
                    sy.wait_ge(VE.h, ve_x1)
                    SX1[pp].inc(sy.dma_start(out=x1_o[g * 128:(g + 1) * 128, :],
                                             in_=res[:, pp, :]), 16)
                p.sync(sx1)
                st_x1[pp] = SX1[pp].n

                emit_ln(p, res[:, pp, :], xn1f[:, pp, :], lns,
                        out_waits=[(SXN[pp], st_xn[pp]), (PE, xs_free_pe[pp])])
                ve_xn1 = VE.n

                def sxn(sy, g=g, pp=pp, ve_xn1=ve_xn1):
                    sy.wait_ge(VE.h, ve_xn1)
                    SXN[pp].inc(sy.dma_start(out=xn1_o[g * 128:(g + 1) * 128, :],
                                             in_=xn1f[:, pp, :]), 16)
                p.sync(sxn)
                st_xn[pp] = SXN[pp].n

                def vb(v, pp=pp, pm=xn1b_free[0]):
                    v.wait_ge(PE.h, pm)
                    VE.inc(v.tensor_copy(out=xn1b[:], in_=xn1f[:, pp, :]))
                p.vec(vb)
                ve_bb = VE.n

                evm = []
                for half in range(2):
                    wv = evm[-1] if evm else None

                    def tt(t, half=half, ve_bb=ve_bb, wv=wv):
                        t.wait_ge(VE.h, ve_bb if wv is None else wv)
                        for i in range(4):
                            blk = half * 4 + i
                            PE.inc(t.transpose(out=psB[:, i * 128:(i + 1) * 128],
                                               in_=xn1b[:, blk * 128:(blk + 1) * 128],
                                               identity=ident_bf[:]))
                    p.ten(tt)
                    pe_t = PE.n

                    def vt(v, g=g, half=half, pe_t=pe_t):
                        v.wait_ge(PE.h, pe_t)
                        for i in range(4):
                            blk = half * 4 + i
                            VE.inc(v.tensor_copy(out=xn1T[:, blk, g * 128:(g + 1) * 128],
                                                 in_=psB[:, i * 128:(i + 1) * 128]))
                    p.vec(vt)
                    evm.append(VE.n)
                xn1b_free[0] = PE.n

                evm2 = []
                for half in range(2):
                    wv2 = evm2[-1] if evm2 else None

                    def tt32(t, pp=pp, half=half, wv2=wv2, m=VE.n):
                        t.wait_ge(VE.h, m if wv2 is None else wv2)
                        for i in range(4):
                            blk = half * 4 + i
                            PE.inc(t.transpose(out=psC[:, i * 128:(i + 1) * 128],
                                               in_=xn1f[:, pp, blk * 128:(blk + 1) * 128],
                                               identity=ident_f[:]))
                    p.ten(tt32)
                    pe_t32 = PE.n

                    def vt32(v, half=half, pe_t32=pe_t32):
                        v.wait_ge(PE.h, pe_t32)
                        for i in range(4):
                            blk = half * 4 + i
                            VE.inc(v.tensor_copy(out=xT32[:, blk, :],
                                                 in_=psC[:, i * 128:(i + 1) * 128]))
                    p.vec(vt32)
                    evm2.append(VE.n)
                ve_x32 = VE.n
                xs_free_pe[pp] = PE.n

                def ts_(t, ve_x32=ve_x32, lw_c=lw_c):
                    t.wait_ge(VE.h, ve_x32)
                    t.wait_ge(LW.h, lw_c)
                    for c in range(8):
                        PE.inc(t.matmul(out=psA[:, 0:8], lhsT=xT32[:, c, :], rhs=gw32[:, c, :],
                                        start=(c == 0), stop=(c == 7)))
                p.ten(ts_)
                pe_s = PE.n

                def vs_(v, g=g, pe_s=pe_s):
                    v.wait_ge(PE.h, pe_s)
                    v.tensor_copy(out=gapt[:], in_=gapt[:])
                    VE.inc(v.tensor_copy(out=sr_sb[:, g, :], in_=psA[:, 0:8]))
                p.vec(vs_)

                emit_ln(p, xn1f[:, pp, :], n2[:], lns)

                def vaf(v, pp=pp, rep_done=rep_done, sz=st_z1[pp]):
                    v.wait_ge(VE.h, rep_done)
                    v.wait_ge(SZ1[pp].h, sz)
                    v.tensor_tensor(out=n2[:], in0=n2[:], in1=lgr[:], op=ALU.mult)
                    VE.inc(v.tensor_tensor(out=z1t[:, pp, :], in0=n2[:], in1=lbr[:], op=ALU.add))
                p.vec(vaf)
                ve_z1 = VE.n

                def sz1(sy, g=g, pp=pp, ve_z1=ve_z1):
                    sy.wait_ge(VE.h, ve_z1)
                    SZ1[pp].inc(sy.dma_start(out=z1_o[g * 128:(g + 1) * 128, :],
                                             in_=z1t[:, pp, :]), 16)
                p.sync(sz1)
                st_z1[pp] = SZ1[pp].n

            ve_sr = VE.n

            def ssr(sy, ve_sr=ve_sr):
                sy.wait_ge(VE.h, ve_sr)
                SSR.inc(sy.dma_start(out=sr_o.rearrange("(g p) e -> p g e", p=128),
                                     in_=sr_sb[:]), 16)
            p.sync(ssr)

            if fused:
                CCe = p.CC

                def gcc1(g, m0=SZ1[0].n, m1=SZ1[1].n, msr=SSR.n):
                    g.wait_ge(SZ1[0].h, m0)
                    g.wait_ge(SZ1[1].h, m1)
                    g.wait_ge(SSR.h, msr)
                    CCe.inc(g.collective_compute(
                        "AllGather", ALU.bypass, replica_groups=[list(range(NC))],
                        ins=[ext["z1_own"].opt()], outs=[ext["z1ag"].opt()]))
                    CCe.inc(g.collective_compute(
                        "AllGather", ALU.bypass, replica_groups=[list(range(NC))],
                        ins=[ext["sraw_own"].opt()], outs=[ext["srawag"].opt()]))
                p.gps(gcc1)

            if fused:
                dec_ap = ext["decay"]
                LDC = p.S("LDC")

                def sdc(sy):
                    LDC.inc(sy.dma_start(out=dsg[:], in_=dec_ap[:, :]), 16)
                p.sync(sdc)
                ldc_c = LDC.n

                def sdc2(sc_, ldc_c=ldc_c):
                    sc_.wait_ge(LDC.h, ldc_c)
                    AC.inc(sc_.activation(out=dsg[:], in_=dsg[:], func=AF.Sigmoid))
                p.sca(sdc2)
                ac_dd = AC.n

                def tdc(t, ac_dd=ac_dd):
                    t.wait_ge(AC.h, ac_dd)
                    PE.inc(t.matmul(out=psA[:, 0:1], lhsT=ones_row[:], rhs=dsg[:],
                                    start=True, stop=True))
                p.ten(tdc)
                pe_dd = PE.n

                def vdc(v, pe_dd=pe_dd):
                    v.wait_ge(PE.h, pe_dd)
                    v.memset(dtile[:], 1.0)
                    v.tensor_copy(out=gapt[:], in_=gapt[:])
                    v.tensor_copy(out=drep[:], in_=psA[:, 0:1])
                    v.tensor_copy(out=gapt[:], in_=gapt[:])
                    VE.inc(v.tensor_scalar(out=dtile[:], in0=dtile[:], scalar1=drep[:, :],
                                           scalar2=None, op0=ALU.mult))
                p.vec(vdc)
                ve_dd = VE.n

            bt_free = [0, 0]
            for j in range(8):
                pp = j % 2

                def tbd(t, j=j, pp=pp, lg_c=lg_c, m=bt_free[pp], ve_all=VE.n):
                    t.wait_ge(LG.h, lg_c)
                    t.wait_ge(VE.h, ve_all)
                    t.wait_ge(SBT.h, m)
                    for qc in range(4):
                        for c in range(8):
                            PE.inc(t.matmul(out=[psC, psD, psE, psF][qc][:, 0:512],
                                            lhsT=mb_s[:, c, j * 128:(j + 1) * 128],
                                            rhs=xn1T[:, c, qc * 512:(qc + 1) * 512],
                                            start=(c == 0), stop=(c == 7)))
                p.ten(tbd)
                pe_b = PE.n

                def vbd(v, j=j, pp=pp, pe_b=pe_b):
                    v.wait_ge(PE.h, pe_b)
                    for qc in range(4):
                        VE.inc(v.tensor_copy(out=btile[:, pp, qc * 512:(qc + 1) * 512],
                                             in_=[psC, psD, psE, psF][qc][:, 0:512]))
                p.vec(vbd)
                ve_bt = VE.n

                if not fused:
                    def sbt(sy, j=j, pp=pp, ve_bt=ve_bt):
                        sy.wait_ge(VE.h, ve_bt)
                        SBT.inc(sy.dma_start(out=bt_o[j * 128:(j + 1) * 128, :],
                                             in_=btile[:, pp, :]), 16)
                    p.sync(sbt)
                    bt_free[pp] = SBT.n
                else:
                    def vscn(v, j=j, pp=pp, ve_dd=ve_dd, m=SBT.n):
                        v.wait_ge(VE.h, ve_dd)
                        v.wait_ge(SBT.h, m)
                        v.tensor_copy(out=gapt[:], in_=gapt[:])
                        v.tensor_copy(out=gapt[:], in_=gapt[:])
                        v.tensor_tensor_scan(out=prov[:], data0=dtile[:], data1=btile[:, pp, :],
                                             initial=0.0, op0=ALU.mult, op1=ALU.add)
                        v.tensor_copy(out=gapt[:], in_=gapt[:])
                        v.tensor_copy(out=carry_sb[:, j:j + 1], in_=prov[:, 2047:2048])
                        VE.inc(v.tensor_copy(out=hbt[:, pp, :], in_=prov[:]))
                    p.vec(vscn)
                    ve_sc = VE.n

                    def sht(sy, j=j, pp=pp, ve_sc=ve_sc):
                        sy.wait_ge(VE.h, ve_sc)
                        SBT.inc(sy.dma_start(out=ext["holoT_bf"][j * 128:(j + 1) * 128, :],
                                             in_=hbt[:, pp, :]), 16)
                    p.sync(sht)
                    bt_free[pp] = SBT.n

            if fused:
                ve_cy = VE.n
                SCY = p.S("SCY")
                CC = p.CC

                def scy(sy, ve_cy=ve_cy):
                    sy.wait_ge(VE.h, ve_cy)
                    SCY.inc(sy.dma_start(
                        out=ext["carr_my"][0, :].rearrange("(pp jj) -> pp jj", pp=128),
                        in_=carry_sb[:]), 16)
                p.sync(scy)
                scy_c = SCY.n

                def gcc(g, scy_c=scy_c):
                    g.wait_ge(SCY.h, scy_c)
                    CC.inc(g.collective_compute(
                        "AllGather", ALU.bypass, replica_groups=[list(range(NC))],
                        ins=[ext["carr_my"].opt()], outs=[ext["carrag"].opt()]))
                p.gps(gcc)

            def fin(sy):
                sy.wait_ge(SBT.h, SBT.n)
                sy.wait_ge(SSR.h, SSR.n)
                for pp in range(2):
                    sy.wait_ge(SX1[pp].h, SX1[pp].n)
                    sy.wait_ge(SXN[pp].h, SXN[pp].n)
                    sy.wait_ge(SZ1[pp].h, SZ1[pp].n)
                if fused:
                    sy.wait_ge(p.CC.h, p.CC.n)
                DONE.inc(sy.nop())
            p.sync(fin)

    return _lower_ext(nc) if standalone else (nc, (DONE.h, DONE.n))



def build_D(nc=None, ext=None, entry=()):
    standalone = nc is None
    if standalone:
        nc = bass.Bass("TRN2", target_bir_lowering=False, detect_race_conditions=False)
        ext = {}
        ext["bound_units"] = nc.dram_tensor("bound_units", [4, 128, S], F32, kind="ExternalInput").ap()
        ext["decay"] = nc.dram_tensor("decay", [1, 1], F32, kind="ExternalInput").ap()
        ext["holo_units"] = nc.dram_tensor("holo_units", [4, 128, S], F32, kind="ExternalOutput").ap()
    bu_d = ext["bound_units"]
    dec_d = ext["decay"]
    ho_o = ext["holo_units"]

    es = ExitStack()
    sb = lambda n, s_, d_: es.enter_context(nc.sbuf_tensor("D" + n, s_, d_))
    with es:
        dval = sb("dval", [1, 1], F32)
        dsg = sb("dsg", [1, 1], F32)
        drep = sb("drep", [128, 1], F32)
        ones_row = sb("ones_row", [1, 128], F32)
        dtile = sb("dtile", [128, S], F32)
        bsb = sb("bsb", [128, 2, S], F32)
        hsb = sb("hsb", [128, 2, S], F32)
        gapt = sb("gapt", [128, 1], F32)
        psA = es.enter_context(nc.psum_tensor("DpsA", [128, 512], F32))

        with nc.Block() as block:
            p = ext["_P"].bind(block) if "_P" in ext else P(nc, es, block, pfx="E_")
            DONE = p.S("DONE")
            for _eng in [block.sync, block.gpsimd, block.vector, block.scalar, block.tensor]:
                def _bar(e, entry=entry):
                    for _sh, _cnt in entry:
                        e.wait_ge(_sh, _cnt)
                _eng(_bar)
            PE, VE, AC = p.PE, p.VE, p.AC
            LW = p.S("LW")
            LDu = [p.S("LD0"), p.S("LD1")]
            STu = [p.S("ST0"), p.S("ST1")]

            def sc(sy):
                LW.inc(sy.dma_start(out=dval[:], in_=dec_d[:, :]), 16)
            p.sync(sc)
            lw_c = LW.n

            def s1(s):
                s.wait_ge(LW.h, lw_c)
                AC.inc(s.activation(out=dsg[:], in_=dval[:], func=AF.Sigmoid))
            p.sca(s1)
            ac_d = AC.n

            def v0(v, ac_d=ac_d):
                v.wait_ge(AC.h, ac_d)
                v.memset(ones_row[:], 1.0)
                v.memset(gapt[:], 0.0)
                VE.inc(v.memset(dtile[:], 1.0))
            p.vec(v0)
            ve_0 = VE.n

            def t0(t, ve_0=ve_0):
                t.wait_ge(VE.h, ve_0)
                PE.inc(t.matmul(out=psA[:, 0:1], lhsT=ones_row[:], rhs=dsg[:], start=True,
                                stop=True))
            p.ten(t0)
            pe_0 = PE.n

            def v1(v, pe_0=pe_0):
                v.wait_ge(PE.h, pe_0)
                v.tensor_copy(out=gapt[:], in_=gapt[:])
                v.tensor_copy(out=drep[:], in_=psA[:, 0:1])
                v.tensor_copy(out=gapt[:], in_=gapt[:])
                VE.inc(v.tensor_scalar(out=dtile[:], in0=dtile[:], scalar1=drep[:, :],
                                       scalar2=None, op0=ALU.mult))
            p.vec(v1)
            ve_d = VE.n

            st_m = [0, 0]
            for u in range(4):
                pp = u % 2

                def syl(sy, u=u, pp=pp, m=st_m[pp]):
                    sy.wait_ge(STu[pp].h, m)
                    LDu[pp].inc(sy.dma_start(out=bsb[:, pp, :], in_=bu_d[u, :, :]), 16)
                p.sync(syl)
                lm = LDu[pp].n

                def vsc(v, u=u, pp=pp, lm=lm, ve_d=ve_d):
                    v.wait_ge(LDu[pp].h, lm)
                    v.wait_ge(VE.h, ve_d)
                    VE.inc(v.tensor_tensor_scan(out=hsb[:, pp, :], data0=dtile[:],
                                                data1=bsb[:, pp, :], initial=0.0,
                                                op0=ALU.mult, op1=ALU.add))
                p.vec(vsc)
                vm = VE.n

                def sst(sy, u=u, pp=pp, vm=vm):
                    sy.wait_ge(VE.h, vm)
                    STu[pp].inc(sy.dma_start(out=ho_o[u, :, :], in_=hsb[:, pp, :]), 16)
                p.sync(sst)
                st_m[pp] = STu[pp].n

            def fin(sy):
                sy.wait_ge(STu[0].h, STu[0].n)
                sy.wait_ge(STu[1].h, STu[1].n)
                DONE.inc(sy.nop())
            p.sync(fin)

    return _lower_ext(nc) if standalone else (nc, (DONE.h, DONE.n))



NTOK = NC * TOK

def build_E(nc=None, ext=None, entry=(), fused=False):
    standalone = nc is None
    if standalone:
        nc = bass.Bass("TRN2", target_bir_lowering=False, detect_race_conditions=False)
        ext = {}
        ext["z1_all"] = nc.dram_tensor("z1_all", [NTOK, D], BF16, kind="ExternalInput").ap()
        ext["scol"] = nc.dram_tensor("scol", [NTOK, 1], F32, kind="ExternalInput").ap()
        ext["fc1T"] = nc.dram_tensor("fc1T", [D, HID1], F32, kind="ExternalInput").ap()
        ext["b1"] = nc.dram_tensor("b1", [128, HID1 // 128], F32, kind="ExternalInput").ap()
        ext["fc2T"] = nc.dram_tensor("fc2T", [HID1, D], F32, kind="ExternalInput").ap()
        ext["b2"] = nc.dram_tensor("b2", [D], F32, kind="ExternalInput").ap()
        ext["Ustrict"] = nc.dram_tensor("Ustrict", [128, 128], F32, kind="ExternalInput").ap()
        ext["opack"] = nc.dram_tensor("opack", [K1, D], BF16, kind="ExternalOutput").ap()
        ext["tsel"] = nc.dram_tensor("tsel", [128, 16], U32, kind="ExternalOutput").ap()
    zall_d = ext["z1_all"]
    scol_d = ext.get("scol")
    f1_d = ext["fc1T"]
    b1_d = ext["b1"]
    f2_d = ext["fc2T"]
    b2_d = ext["b2"]
    us_d = ext["Ustrict"]
    op_o = ext.get("opack")
    ts_o = ext.get("tsel")
    fa_dram = nc.dram_tensor("E_fa_scr", [NTOK], F32)

    es = ExitStack()
    sb = lambda n, s_, d_: es.enter_context(nc.sbuf_tensor("E" + n, s_, d_))
    psa = lambda n, s_, d_: es.enter_context(nc.psum_tensor("D" + n, s_, d_))
    with es:
        ones_row = sb("ones_row", [1, 128], F32)
        onescol = sb("onescol", [128, 1], F32)
        onetile = sb("onetile", [128, 128], F32)
        ident_bf = sb("ident_bf", [128, 128], BF16)
        pid = sb("pid", [128, 1], F32)
        cid = sb("cid", [128, 128], F32)
        us_s = sb("us_s", [128, 128], F32)
        s_sb = sb("s_sb", [128, 128], F32)
        sr8 = sb("sr8", [128, 1024], F32)
        wr8 = sb("wr8", [128, 8], F32)
        esel8_sb = sb("esel8_sb", [128, 8], F32)
        eselt_sb = sb("eselt_sb", [128, 1024], F32)
        zbf = sb("zbf", [128, D], BF16)
        maskb = sb("maskb", [128, 128], F32)
        maskhi = sb("maskhi", [128, 128], F32)
        maskeq = sb("maskeq", [128, 128], F32)
        incl = sb("incl", [128, 128], F32)
        rankeq = sb("rankeq", [128, 128], F32)
        mask = sb("mask", [128, 128], F32)
        rank = sb("rank", [128, 128], F32)
        Atile = sb("Atile", [128, 128], F32)
        rowc = sb("rowc", [128, 1], F32)
        rowoff = sb("rowoff", [128, 1], F32)
        threp = sb("threp", [128, 1], F32)
        hirep = sb("hirep", [128, 1], F32)
        lorep = sb("lorep", [128, 1], F32)
        needrep = sb("needrep", [128, 1], F32)
        st = sb("st", [1, 16], F32)
        gapt = sb("gapt", [128, 1], F32)
        rowbuf = sb("rowbuf", [1, 2, 512], F32)
        Arep = sb("Arep", [128, NTOK], F32)
        iot = sb("iot", [128, 2, 8], F32)
        idx2 = sb("idx2", [128, 2, 8], U32)
        tokf = sb("tokf", [128, 16], F32)
        widx = sb("widx", [128, 128], I16)
        ident_f = sb("ident_f", [128, 128], F32)
        zselA = sb("zselA", [128, 2, D], BF16)
        zselB = sb("zselB", [128, 2, D], BF16)
        zT = sb("zT", [128, 8, K1], BF16)
        wraw = sb("wraw", [128, 16], F32)
        wsel = sb("wsel", [128, 16], F32)
        b1_sb = sb("b1_sb", [128, 32], F32)
        b2row = sb("b2row", [1, D], F32)
        b2rep = sb("b2rep", [128, D], F32)
        f1p = sb("f1p", [128, 2, 8, 128], BF16)
        f2p = sb("f2p", [128, 2, 512], BF16)
        hmidQ = sb("hmidQ", [128, 32, 512], BF16)
        xg = sb("xg", [128, 512], F32)
        aa = sb("aa", [128, 512], F32)
        w1 = sb("w1", [128, 512], F32)
        c2 = sb("c2", [128, 512], F32)
        sg = sb("sg", [128, 512], F32)
        tvv = sb("tvv", [128, 1024], F32)
        orow4 = sb("orow4", [128, 4, 1024], BF16)
        pT = psa("pT", [128, 512], BF16)
        p1 = psa("p1", [128, 512], F32)
        p2 = psa("p2", [128, 512], F32)
        p3 = psa("p3", [128, 512], F32)
        p4 = psa("p4", [128, 512], F32)
        PSC = [p1, p2, p3, p4]

        with nc.Block() as block:
            p = ext["_P"].bind(block) if "_P" in ext else P(nc, es, block, pfx="F_")
            DONE = p.S("DONE")
            for _eng in [block.sync, block.gpsimd, block.vector, block.scalar, block.tensor]:
                def _bar(e, entry=entry):
                    for _sh, _cnt in entry:
                        e.wait_ge(_sh, _cnt)
                _eng(_bar)
            PE, VE, AC, PL = p.PE, p.VE, p.AC, p.PL
            LW = p.S("LW"); LG = p.S("LG")
            LZ = [p.S("LZ0"), p.S("LZ1")]
            LS = p.S("LS")
            LF1 = [p.S("LF10"), p.S("LF11")]
            LF2 = [p.S("LF20"), p.S("LF21")]
            SOP = p.S("SOP"); STS = p.S("STS")
            DFA = p.S("DFA"); SSC = p.S("SSC")
            ssc_c = 0

            def gc(g):
                g.iota(out=pid[:], pattern=[[0, 1]], base=0, channel_multiplier=1,
                       allow_small_or_imprecise_dtypes=True)
                g.iota(out=cid[:], pattern=[[1, 128]], base=0, channel_multiplier=0,
                       allow_small_or_imprecise_dtypes=True)
                g.iota(out=iot[:, 0, :], pattern=[[128, 8]], base=0, channel_multiplier=1,
                       allow_small_or_imprecise_dtypes=True)
                PL.inc(g.iota(out=iot[:, 1, :], pattern=[[128, 8]], base=1024,
                              channel_multiplier=1, allow_small_or_imprecise_dtypes=True))
                LG.inc(g.dma_start(out=b1_sb[:], in_=b1_d[:, :]), 16)
            p.gps(gc)
            pl_c = PL.n
            lg_c = LG.n

            def sc(sy):
                LW.inc(sy.dma_start(out=us_s[:], in_=us_d[:, :]), 16)
                LW.inc(sy.dma_start(out=b2row[:], in_=b2_d[None, :]), 16)
                if not fused:
                    LS.inc(sy.dma_start(out=s_sb[:],
                                        in_=scol_d.rearrange("(p f) o -> p (f o)", p=128)), 16)
                else:
                    LS.inc(sy.dma_start(out=sr8[:],
                                        in_=ext["srawag"].rearrange("(p f) e -> p (f e)", p=128)),
                           16)
            p.sync(sc)
            lw_c = LW.n
            ls_c = LS.n
            if fused:
                ESL = p.S("ESL")

                def sesl(sy):
                    ESL.inc(sy.dma_start(out=esel8_sb[:], in_=ext["esel8"][:, :]), 16)
                    ESL.inc(sy.dma_start(out=eselt_sb[:], in_=ext["esel_tiled"][:, :]), 16)
                p.sync(sesl)
                esl_c = ESL.n

                def vex(v, ls_c=ls_c, esl_c=esl_c):
                    v.wait_ge(LS.h, ls_c)
                    v.wait_ge(ESL.h, esl_c)
                    v.tensor_tensor(out=sr8[:], in0=sr8[:], in1=eselt_sb[:], op=ALU.mult)
                    VE.inc(v.tensor_reduce(
                        out=s_sb[:], in_=sr8.ap().rearrange("p (f e) -> p f e", e=8),
                        axis=AXX, op=ALU.add))
                p.vec(vex)
                ls_c = VE.n
                scol_d = ext["scolE"]

                def sscd(sy, m=VE.n):
                    sy.wait_ge(VE.h, m)
                    SSC.inc(sy.dma_start(out=scol_d.rearrange("(pp f) o -> pp (f o)", pp=128),
                                         in_=s_sb[:]), 16)
                p.sync(sscd)
                ssc_c = SSC.n


            def vc(v):
                v.wait_ge(PL.h, pl_c)
                v.memset(ones_row[:], 1.0)
                v.memset(onescol[:], 1.0)
                v.memset(onetile[:], 1.0)
                v.memset(gapt[:], 0.0)
                v.memset(st[:], 0.0)
                v.memset(st[0:1, 0:1], -1000.0)
                v.memset(st[0:1, 1:2], 1000.0)
                if fused:
                    v.memset(zbf[:], 0.0)
                v.tensor_scalar(out=ident_f[:], in0=cid[:], scalar1=pid[:, :],
                                scalar2=None, op0=ALU.is_equal)
                VE.inc(v.tensor_scalar(out=ident_bf[:], in0=cid[:], scalar1=pid[:, :],
                                       scalar2=None, op0=ALU.is_equal))
            p.vec(vc)
            ve_cst = VE.n

            ZC = p.S("ZC")
            zc_all = 0
            if fused:
                def szc(sy, m=VE.n):
                    sy.wait_ge(VE.h, m)
                    for a in range(NTOK // 128):
                        ZC.inc(sy.dma_start(out=ext["contribZ"][a * 128:(a + 1) * 128, :],
                                            in_=zbf[:]), 16)
                p.sync(szc)
                zc_all = ZC.n

            prev = 0
            for i in range(2):
                def tr(t, i=i, prev=prev, ve_cst=ve_cst, lw_c=lw_c):
                    t.wait_ge(VE.h, max(ve_cst, prev))
                    t.wait_ge(LW.h, lw_c)
                    PE.inc(t.matmul(out=p1[:, 0:512], lhsT=ones_row[:],
                                    rhs=b2row[0:1, i * 512:(i + 1) * 512], start=True, stop=True))
                p.ten(tr)
                mpe = PE.n

                def vr(v, i=i, mpe=mpe):
                    v.wait_ge(PE.h, mpe)
                    VE.inc(v.tensor_copy(out=b2rep[:, i * 512:(i + 1) * 512], in_=p1[:, 0:512]))
                p.vec(vr)
                prev = VE.n

            lo, hi, th, cnt, sel = (st[0:1, k:k + 1] for k in range(5))
            ta, tcv, tb, td, ssum, ngt, need = (st[0:1, k:k + 1] for k in range(5, 12))
            ve_mark = VE.n
            for it in range(40):
                def t1_(t, m=ve_mark, lw_c=lw_c):
                    t.wait_ge(VE.h, m)
                    t.wait_ge(LW.h, lw_c)
                    PE.inc(t.matmul(out=p1[:, 0:1], lhsT=ones_row[:], rhs=th, start=True,
                                    stop=True))
                p.ten(t1_)
                pe1 = PE.n

                def v1_(v, pe1=pe1, ls_c=ls_c):
                    v.wait_ge(PE.h, pe1)
                    if fused:
                        v.wait_ge(VE.h, ls_c)
                    else:
                        v.wait_ge(LS.h, ls_c)
                    v.tensor_copy(out=gapt[:], in_=gapt[:])
                    v.tensor_copy(out=threp[:], in_=p1[:, 0:1])
                    v.tensor_copy(out=gapt[:], in_=gapt[:])
                    v.tensor_scalar(out=maskb[:], in0=s_sb[:], scalar1=threp[:, :],
                                    scalar2=None, op0=ALU.is_gt)
                    VE.inc(v.reduce_sum(out=rowc[:], in_=maskb[:], axis=AXX))
                p.vec(v1_)
                ve1 = VE.n

                def t2_(t, ve1=ve1):
                    t.wait_ge(VE.h, ve1)
                    PE.inc(t.matmul(out=p2[0:1, 0:1], lhsT=onescol[:], rhs=rowc[:],
                                    start=True, stop=True))
                p.ten(t2_)
                pe2 = PE.n

                def v2_(v, pe2=pe2):
                    v.wait_ge(PE.h, pe2)
                    v.tensor_copy(out=gapt[:], in_=gapt[:])
                    v.tensor_copy(out=cnt, in_=p2[0:1, 0:1])
                    v.tensor_tensor(out=ta, in0=th, in1=lo, op=ALU.subtract)
                    v.tensor_scalar(out=sel, in0=cnt, scalar1=float(K1), scalar2=None,
                                    op0=ALU.is_gt)
                    v.tensor_tensor(out=tcv, in0=hi, in1=th, op=ALU.subtract)
                    v.tensor_tensor(out=tb, in0=ta, in1=sel, op=ALU.mult)
                    v.tensor_tensor(out=td, in0=tcv, in1=sel, op=ALU.mult)
                    v.tensor_tensor(out=lo, in0=lo, in1=tb, op=ALU.add)
                    v.tensor_tensor(out=hi, in0=th, in1=td, op=ALU.add)
                    v.tensor_copy(out=gapt[:], in_=gapt[:])
                    v.tensor_tensor(out=ssum, in0=lo, in1=hi, op=ALU.add)
                    v.tensor_copy(out=gapt[:], in_=gapt[:])
                    VE.inc(v.tensor_scalar(out=th, in0=ssum, scalar1=0.5, scalar2=None,
                                           op0=ALU.mult))
                p.vec(v2_)
                ve_mark = VE.n

            def t3_(t, m=ve_mark):
                t.wait_ge(VE.h, m)
                PE.inc(t.matmul(out=p1[:, 0:1], lhsT=ones_row[:], rhs=hi, start=True, stop=True))
                PE.inc(t.matmul(out=p2[:, 0:1], lhsT=ones_row[:], rhs=lo, start=True, stop=True))
            p.ten(t3_)
            pe3 = PE.n

            def v3_(v, pe3=pe3):
                v.wait_ge(PE.h, pe3)
                v.tensor_copy(out=gapt[:], in_=gapt[:])
                v.tensor_copy(out=hirep[:], in_=p1[:, 0:1])
                v.tensor_copy(out=lorep[:], in_=p2[:, 0:1])
                v.tensor_copy(out=gapt[:], in_=gapt[:])
                v.tensor_scalar(out=maskhi[:], in0=s_sb[:], scalar1=hirep[:, :], scalar2=None,
                                op0=ALU.is_gt)
                v.tensor_scalar(out=maskb[:], in0=s_sb[:], scalar1=lorep[:, :], scalar2=None,
                                op0=ALU.is_gt)
                v.tensor_tensor(out=maskeq[:], in0=maskb[:], in1=maskhi[:], op=ALU.subtract)
                VE.inc(v.reduce_sum(out=rowc[:], in_=maskhi[:], axis=AXX))
            p.vec(v3_)
            ve3 = VE.n

            def t4_(t, ve3=ve3):
                t.wait_ge(VE.h, ve3)
                PE.inc(t.matmul(out=p3[0:1, 0:1], lhsT=onescol[:], rhs=rowc[:], start=True,
                                stop=True))
            p.ten(t4_)
            pe4 = PE.n

            def v4_(v, pe4=pe4):
                v.wait_ge(PE.h, pe4)
                v.tensor_copy(out=gapt[:], in_=gapt[:])
                v.tensor_copy(out=ngt, in_=p3[0:1, 0:1])
                v.tensor_copy(out=gapt[:], in_=gapt[:])
                v.tensor_scalar(out=need, in0=ngt, scalar1=-1.0, scalar2=float(K1),
                                op0=ALU.mult, op1=ALU.add)
                v.tensor_tensor_scan(out=incl[:], data0=onetile[:], data1=maskeq[:],
                                     initial=0.0, op0=ALU.mult, op1=ALU.add)
                v.tensor_tensor(out=rankeq[:], in0=incl[:], in1=maskeq[:], op=ALU.subtract)
                VE.inc(v.reduce_sum(out=rowc[:], in_=maskeq[:], axis=AXX))
            p.vec(v4_)
            ve4 = VE.n

            def t5_(t, ve4=ve4):
                t.wait_ge(VE.h, ve4)
                PE.inc(t.matmul(out=p4[:, 0:1], lhsT=us_s[:], rhs=rowc[:], start=True, stop=True))
                PE.inc(t.matmul(out=p1[:, 0:1], lhsT=ones_row[:], rhs=need, start=True, stop=True))
            p.ten(t5_)
            pe5 = PE.n

            def v5_(v, pe5=pe5):
                v.wait_ge(PE.h, pe5)
                v.tensor_copy(out=gapt[:], in_=gapt[:])
                v.tensor_copy(out=rowoff[:], in_=p4[:, 0:1])
                v.tensor_copy(out=needrep[:], in_=p1[:, 0:1])
                v.tensor_copy(out=gapt[:], in_=gapt[:])
                v.tensor_scalar(out=rankeq[:], in0=rankeq[:], scalar1=rowoff[:, :],
                                scalar2=None, op0=ALU.add)
                v.tensor_copy(out=gapt[:], in_=gapt[:])
                v.tensor_scalar(out=maskb[:], in0=rankeq[:], scalar1=needrep[:, :],
                                scalar2=None, op0=ALU.is_lt)
                v.tensor_tensor(out=maskb[:], in0=maskb[:], in1=maskeq[:], op=ALU.mult)
                v.tensor_tensor(out=mask[:], in0=maskhi[:], in1=maskb[:], op=ALU.add)
                v.tensor_tensor_scan(out=incl[:], data0=onetile[:], data1=mask[:],
                                     initial=0.0, op0=ALU.mult, op1=ALU.add)
                v.tensor_tensor(out=rank[:], in0=incl[:], in1=mask[:], op=ALU.subtract)
                VE.inc(v.reduce_sum(out=rowc[:], in_=mask[:], axis=AXX))
            p.vec(v5_)
            ve5 = VE.n

            def t6_(t, ve5=ve5):
                t.wait_ge(VE.h, ve5)
                PE.inc(t.matmul(out=p2[:, 0:1], lhsT=us_s[:], rhs=rowc[:], start=True, stop=True))
            p.ten(t6_)
            pe6 = PE.n

            def v6_(v, pe6=pe6):
                v.wait_ge(PE.h, pe6)
                v.tensor_copy(out=gapt[:], in_=gapt[:])
                v.tensor_copy(out=rowoff[:], in_=p2[:, 0:1])
                v.tensor_copy(out=gapt[:], in_=gapt[:])
                v.tensor_scalar(out=rank[:], in0=rank[:], scalar1=rowoff[:, :], scalar2=None,
                                op0=ALU.add)
                v.tensor_scalar(out=Atile[:], in0=rank[:], scalar1=-BIG_OOB, scalar2=None,
                                op0=ALU.add)
                v.tensor_tensor(out=Atile[:], in0=Atile[:], in1=mask[:], op=ALU.mult)
                VE.inc(v.tensor_scalar(out=Atile[:], in0=Atile[:], scalar1=BIG_OOB,
                                       scalar2=None, op0=ALU.add))
            p.vec(v6_)
            ve6 = VE.n

            def sfa(sy, ve6=ve6):
                sy.wait_ge(VE.h, ve6)
                DFA.inc(sy.dma_start(out=fa_dram.rearrange("(p f) -> p f", p=128),
                                     in_=Atile[:]), 16)
            p.sync(sfa)
            dfa1 = DFA.n

            DFR = p.S("DFR0"); DFR2 = p.S("DFR1")
            rb_free = [0, 0]
            prevv = 0
            for i in range(32):
                rp = i % 2
                RB = [DFR, DFR2][rp]

                def sfr(sy, i=i, rp=rp, RB=RB, dfa1=dfa1, m=rb_free[rp]):
                    sy.wait_ge(DFA.h, dfa1)
                    sy.wait_ge(PE.h, m)
                    RB.inc(sy.dma_start(out=rowbuf[:, rp, :],
                                        in_=fa_dram[None, i * 512:(i + 1) * 512]), 16)
                p.sync(sfr)
                rb_m = RB.n

                def trp(t, i=i, rp=rp, RB=RB, rb_m=rb_m, prevv=prevv):
                    t.wait_ge(RB.h, rb_m)
                    t.wait_ge(VE.h, prevv)
                    PE.inc(t.matmul(out=PSC[i % 4][:, 0:512], lhsT=ones_row[:],
                                    rhs=rowbuf[0:1, rp, :], start=True, stop=True))
                p.ten(trp)
                mpe = PE.n
                rb_free[rp] = mpe

                def vrp(v, i=i, mpe=mpe):
                    v.wait_ge(PE.h, mpe)
                    VE.inc(v.tensor_copy(out=Arep[:, i * 512:(i + 1) * 512],
                                         in_=PSC[i % 4][:, 0:512]))
                p.vec(vrp)
                if i % 4 == 3:
                    prevv = VE.n
            ve_arep = VE.n

            def vmi1(v, ve_arep=ve_arep):
                v.wait_ge(VE.h, ve_arep)
                VE.inc(v.tensor_copy(out=wraw[:], in_=iot.ap().rearrange("p a b -> p (a b)")))
            p.vec(vmi1)
            ve_hop = VE.n

            def shp(s, ve_hop=ve_hop):
                s.wait_ge(VE.h, ve_hop)
                AC.inc(s.nop())
            p.sca(shp)
            ac_hop = AC.n

            def vmi2(v, ac_hop=ac_hop):
                v.wait_ge(AC.h, ac_hop)
                VE.inc(v.max_index(out=idx2[:, 0, :], in_max=iot[:, 0, :], in_values=Arep[:]))
                VE.inc(v.max_index(out=idx2[:, 1, :], in_max=iot[:, 1, :], in_values=Arep[:]))
            p.vec(vmi2)
            ve_idx = VE.n

            if not fused:
                def sts_(sy, ve_idx=ve_idx):
                    sy.wait_ge(VE.h, ve_idx)
                    STS.inc(sy.dma_start(out=ts_o[:, :],
                                         in_=idx2.ap().rearrange("p a b -> p (a b)")), 16)
                p.sync(sts_)

            idxf = idx2.ap().rearrange("p a b -> p (a b)")
            ERW = p.S("ERW")

            def vtk(v, ve_idx=ve_idx):
                v.wait_ge(VE.h, ve_idx)
                VE.inc(v.tensor_copy(out=tokf[:], in_=idxf))
            p.vec(vtk)
            ve_tok = VE.n
            erw_tot = emit_wrap_idx(p, tokf[:, :], sr8.ap()[0:16, 0:128],
                                    sr8.ap()[0:16, 128:256], widx, ident_f[:],
                                    p3, p4, ERW, ve_tok)

            for g2 in range(16):
                def gw_(g_, g2=g2, ve_idx=ve_idx, sscm=ssc_c):
                    g_.wait_ge(VE.h, ve_idx)
                    if fused:
                        g_.wait_ge(SSC.h, sscm)
                    LZ[0].inc(g_.indirect_dma_start(
                        out=wraw[:, g2:g2 + 1], out_offset=None, in_=scol_d[:, :],
                        in_offset=IndirectOffsetOnAxis(ap=idxf[:, g2:g2 + 1], axis=0),
                        bounds_check=NTOK - 1, oob_is_err=False), 16)
                p.gps(gw_)
            lzw_tot = LZ[0].n

            zbuf = [zselA, zselB]
            ZG = [p.S("ZG0"), p.S("ZG1")]
            zg_free = [0, 0]
            for tch in range(8):
                pp = tch % 2

                def gz(g_, tch=tch, pp=pp, erw_tot=erw_tot, m=zg_free[pp]):
                    g_.wait_ge(ERW.h, erw_tot)
                    g_.wait_ge(PE.h, m)
                    ZG[pp].inc(g_.dma_gather(zbuf[pp][:, :, :], zall_d[:, :],
                                             widx[:, 16 * tch:16 * (tch + 1)],
                                             256, 256, D), 16)
                p.gps(gz)
                zg_m = ZG[pp].n

                for gg in range(2):
                    g = 2 * tch + gg
                    for half in range(2):
                        def tz(t, gg=gg, pp=pp, half=half, zg_m=zg_m, wv=VE.n):
                            t.wait_ge(ZG[pp].h, zg_m)
                            t.wait_ge(VE.h, wv)
                            for i2 in range(4):
                                c = half * 4 + i2
                                PE.inc(t.transpose(out=pT[:, i2 * 128:(i2 + 1) * 128],
                                                   in_=zbuf[pp][:, gg, c * 128:(c + 1) * 128],
                                                   identity=ident_bf[:]))
                        p.ten(tz)
                        pe_z = PE.n

                        def vz(v, g=g, half=half, pe_z=pe_z):
                            v.wait_ge(PE.h, pe_z)
                            for i2 in range(4):
                                c = half * 4 + i2
                                VE.inc(v.tensor_copy(out=zT[:, c, g * 128:(g + 1) * 128],
                                                     in_=pT[:, i2 * 128:(i2 + 1) * 128]))
                        p.vec(vz)
                zg_free[pp] = PE.n
            ve_zT = VE.n

            def sw(s, lzw_tot=lzw_tot):
                s.wait_ge(LZ[0].h, lzw_tot)
                AC.inc(s.activation(out=wsel[:], in_=wraw[:], func=AF.Tanh))
            p.sca(sw)
            ac_w = AC.n

            def vw(v, ac_w=ac_w):
                v.wait_ge(AC.h, ac_w)
                v.tensor_copy(out=gapt[:], in_=gapt[:])
                VE.inc(v.tensor_scalar(out=wsel[:], in0=wsel[:], scalar1=0.5, scalar2=0.5,
                                       op0=ALU.mult, op1=ALU.add))
            p.vec(vw)
            ve_w = VE.n

            GELU_S = 1.5957691216057308
            dn_ev = VE.n
            for q in range(4):
                qsl = slice(q * 512, (q + 1) * 512)
                for j in range(32):
                    pp = j % 2

                    def gf1(g_, j=j, pp=pp, m=PE.n):
                        g_.wait_ge(PE.h, m)
                        LF1[pp].inc(g_.dma_start(
                            out=f1p[:, pp, :, :],
                            in_=f1_d[:, j * 128:(j + 1) * 128]
                            .rearrange("(c p) h -> p c h", p=128)), 16)
                    p.gps(gf1)
                    lf1 = LF1[pp].n
                    bank = PSC[j % 2]

                    def tf1(t, j=j, pp=pp, bank=bank, lf1=lf1, ve_zT=ve_zT, m=AC.n):
                        t.wait_ge(LF1[pp].h, lf1)
                        t.wait_ge(VE.h, ve_zT)
                        t.wait_ge(AC.h, m)
                        for c in range(8):
                            PE.inc(t.matmul(out=bank[:, 0:512], lhsT=f1p[:, pp, c, :],
                                            rhs=zT[:, c, qsl], start=(c == 0), stop=(c == 7)))
                    p.ten(tf1)
                    pe_f1 = PE.n

                    def vx(v, j=j, bank=bank, pe_f1=pe_f1, m=AC.n, lg_c=lg_c):
                        v.wait_ge(PE.h, pe_f1)
                        v.wait_ge(LG.h, lg_c)
                        v.wait_ge(AC.h, m)
                        VE.inc(v.tensor_scalar(out=xg[:], in0=bank[:, 0:512],
                                               scalar1=b1_sb[:, j:j + 1], scalar2=None,
                                               op0=ALU.add))
                    p.vec(vx)
                    ve_x = VE.n

                    def sq_(s, ve_x=ve_x):
                        s.wait_ge(VE.h, ve_x)
                        AC.inc(s.activation(out=aa[:], in_=xg[:], func=AF.Square))
                    p.sca(sq_)
                    ac_a = AC.n

                    def vu(v, ac_a=ac_a):
                        v.wait_ge(AC.h, ac_a)
                        v.tensor_scalar(out=w1[:], in0=aa[:], scalar1=0.044715, scalar2=1.0,
                                        op0=ALU.mult, op1=ALU.add)
                        VE.inc(v.tensor_tensor(out=c2[:], in0=xg[:], in1=w1[:], op=ALU.mult))
                    p.vec(vu)
                    ve_c2 = VE.n

                    def sg_(s, ve_c2=ve_c2):
                        s.wait_ge(VE.h, ve_c2)
                        AC.inc(s.activation(out=sg[:], in_=c2[:], func=AF.Sigmoid,
                                            scale=GELU_S))
                    p.sca(sg_)
                    ac_g = AC.n

                    def vh(v, j=j, ac_g=ac_g):
                        v.wait_ge(AC.h, ac_g)
                        VE.inc(v.tensor_tensor(out=hmidQ[:, j, :], in0=xg[:], in1=sg[:],
                                               op=ALU.mult))
                    p.vec(vh)
                ve_hm = VE.n

                for dh in range(2):
                    f2_marks = []
                    for j in range(32):
                        pp = j % 2

                        def gf2(g_, j=j, dh=dh, pp=pp, m=PE.n):
                            g_.wait_ge(PE.h, m)
                            LF2[pp].inc(g_.dma_start(
                                out=f2p[:, pp, :],
                                in_=f2_d[j * 128:(j + 1) * 128,
                                         dh * 512:(dh + 1) * 512]), 16)
                        p.gps(gf2)
                        lf2 = LF2[pp].n

                        def tf2(t, j=j, pp=pp, lf2=lf2, ve_hm=ve_hm, m=dn_ev):
                            t.wait_ge(LF2[pp].h, lf2)
                            t.wait_ge(VE.h, ve_hm)
                            if j == 0:
                                t.wait_ge(VE.h, m)
                            for sc_ in range(4):
                                PE.inc(t.matmul(out=PSC[sc_][:, 0:512],
                                                lhsT=hmidQ[:, j, sc_ * 128:(sc_ + 1) * 128],
                                                rhs=f2p[:, pp, :],
                                                start=(j == 0), stop=(j == 31)))
                        p.ten(tf2)
                    pe_f2 = PE.n

                    for sc_ in range(4):
                        def vo(v, q=q, dh=dh, sc_=sc_, pe_f2=pe_f2, m=SOP.n):
                            v.wait_ge(PE.h, pe_f2)
                            v.wait_ge(SOP.h, m)
                            v.tensor_tensor(out=tvv[:, 0:512], in0=PSC[sc_][:, 0:512],
                                            in1=b2rep[:, dh * 512:(dh + 1) * 512], op=ALU.add)
                            VE.inc(v.tensor_scalar(out=orow4[:, sc_, dh * 512:(dh + 1) * 512],
                                                   in0=tvv[:, 0:512],
                                                   scalar1=wsel[:, q * 4 + sc_:q * 4 + sc_ + 1],
                                                   scalar2=None, op0=ALU.mult))
                        p.vec(vo)
                        if dh == 1:
                            ve_o = VE.n

                            if not fused:
                                def so(sy, q=q, sc_=sc_, ve_o=ve_o):
                                    sy.wait_ge(VE.h, ve_o)
                                    SOP.inc(sy.dma_start(
                                        out=op_o[q * 512 + sc_ * 128:q * 512 + (sc_ + 1) * 128, :],
                                        in_=orow4[:, sc_, :]), 16)
                                p.sync(so)
                            elif sc_ == 3:
                                def so(g_, q=q, ve_o=ve_o, zc=zc_all, erw_tot=erw_tot):
                                    g_.wait_ge(VE.h, ve_o)
                                    g_.wait_ge(ZC.h, zc)
                                    g_.wait_ge(ERW.h, erw_tot)
                                    SOP.inc(g_.dma_scatter_add(
                                        ext["contribZ"][:, :], orow4[:, :, :],
                                        widx[:, 32 * q:32 * (q + 1)], 512, 512, D), 16)
                                p.gps(so)
                    dn_ev = VE.n

            def fin(sy):
                sy.wait_ge(SOP.h, SOP.n)
                sy.wait_ge(STS.h, STS.n)
                DONE.inc(sy.nop())
            p.sync(fin)

    return _lower_ext(nc) if standalone else (nc, (DONE.h, DONE.n))



NRND = 8

def build_F(nc=None, ext=None, entry=(), fused=False):
    standalone = nc is None
    if standalone:
        nc = bass.Bass("TRN2", target_bir_lowering=False, detect_race_conditions=False)
        ext = {}
        ext["x1_own"] = nc.dram_tensor("x1_own", [TOK, D], F32, kind="ExternalInput").ap()
        ext["xn1_own"] = nc.dram_tensor("xn1_own", [TOK, D], F32, kind="ExternalInput").ap()
        ext["holoT_own"] = nc.dram_tensor("holoT_own", [D, TOK], F32, kind="ExternalInput").ap()
        ext["M_unbind"] = nc.dram_tensor("M_unbind", [D, D], F32, kind="ExternalInput").ap()
        ext["gate"] = nc.dram_tensor("gate", [1, 1], F32, kind="ExternalInput").ap()
        ext["crounds"] = nc.dram_tensor("crounds", [NRND, TOK, D], BF16, kind="ExternalInput").ap()
        ext["x2_own"] = nc.dram_tensor("x2_own", [TOK, D], F32, kind="ExternalOutput").ap()
    x1_d = ext["x1_own"]
    xn1_d = ext["xn1_own"]
    ht_d = ext.get("holoT_own")
    mu_d = ext["M_unbind"]
    gt_d = ext["gate"]
    cr_d = ext.get("crounds")
    x2_o = ext["x2_own"]

    es = ExitStack()
    sb = lambda n, s_, d_: es.enter_context(nc.sbuf_tensor("F" + n, s_, d_))
    psa = lambda n, s_, d_: es.enter_context(nc.psum_tensor("E" + n, s_, d_))
    with es:
        ones_row = sb("ones_row", [1, 128], F32)
        gt_s = sb("gt_s", [1, 1], F32)
        gtrep = sb("gtrep", [128, 1], F32)
        mu_s = sb("mu_s", [128, 8, D], BF16)
        hf = sb("hf", [128, 2, TOK], F32)
        hb = sb("hb", [128, 8, TOK], BF16)
        dsg = sb("dsg", [1, 1], F32)
        dpow = sb("dpow", [1, TOK], F32)
        carr_sb = sb("carr_sb", [8, 1024], F32)
        csel_sb = sb("csel_sb", [8, 1], F32)
        cown = sb("cown", [1, 1024], F32)
        res = sb("res", [128, 16, D], F32)
        xa = sb("xa", [128, 2, D], F32)
        xb = sb("xb", [128, 2, D], F32)
        crt = sb("crt", [128, 2, D], BF16)
        gapt = sb("gapt", [128, 1], F32)
        psA = psa("psA", [128, 512], F32)
        psB = psa("psB", [128, 512], F32)

        with nc.Block() as block:
            p = ext["_P"].bind(block) if "_P" in ext else P(nc, es, block, pfx="D_")
            DONE = p.S("DONE")
            for _eng in [block.sync, block.gpsimd, block.vector, block.scalar, block.tensor]:
                def _bar(e, entry=entry):
                    for _sh, _cnt in entry:
                        e.wait_ge(_sh, _cnt)
                _eng(_bar)
            PE, VE, AC, PL = p.PE, p.VE, p.AC, p.PL
            LG = p.S("LG"); LW = p.S("LW")
            LH = [p.S("LH0"), p.S("LH1")]
            LX = [p.S("LX0"), p.S("LX1")]
            LY = [p.S("LY0"), p.S("LY1")]
            LC = [p.S("LC0"), p.S("LC1")]
            SX = p.S("SX")

            def gc(g):
                LG.inc(g.dma_start(out=mu_s[:], in_=mu_d.rearrange("(c p) j -> p c j", p=128)), 16)
            p.gps(gc)
            lg_c = LG.n

            def sc(sy):
                LW.inc(sy.dma_start(out=gt_s[:], in_=gt_d[:, :]), 16)
            p.sync(sc)
            lw_c = LW.n

            def vc(v):
                v.memset(ones_row[:], 1.0)
                VE.inc(v.memset(gapt[:], 0.0))
            p.vec(vc)
            ve_c = VE.n

            def tg(t, ve_c=ve_c, lw_c=lw_c):
                t.wait_ge(VE.h, ve_c)
                t.wait_ge(LW.h, lw_c)
                PE.inc(t.matmul(out=psA[:, 0:1], lhsT=ones_row[:], rhs=gt_s[:], start=True,
                                stop=True))
            p.ten(tg)
            pe_g = PE.n

            def vg(v, pe_g=pe_g):
                v.wait_ge(PE.h, pe_g)
                v.tensor_copy(out=gapt[:], in_=gapt[:])
                VE.inc(v.tensor_copy(out=gtrep[:], in_=psA[:, 0:1]))
            p.vec(vg)
            ve_g = VE.n

            hm = [0, 0]
            if not fused:
                for i in range(8):
                    rp = i % 2

                    def sh(sy, i=i, rp=rp, m=hm[rp]):
                        sy.wait_ge(VE.h, m)
                        LH[rp].inc(sy.dma_start(out=hf[:, rp, :],
                                                in_=ht_d[i * 128:(i + 1) * 128, :]), 16)
                    p.sync(sh)
                    lh = LH[rp].n

                    def vh(v, i=i, rp=rp, lh=lh):
                        v.wait_ge(LH[rp].h, lh)
                        VE.inc(v.tensor_copy(out=hb[:, i, :], in_=hf[:, rp, :]))
                    p.vec(vh)
                    hm[rp] = VE.n
            else:
                LHB = p.S("LHB")

                def shb(sy):
                    LHB.inc(sy.dma_start(out=hb[:],
                                         in_=ext["holoT_bf"].rearrange("(c p) t -> p c t", p=128)),
                            16)
                    LHB.inc(sy.dma_start(out=dsg[:], in_=ext["decay"][:, :]), 16)
                    LHB.inc(sy.dma_start(out=carr_sb[:], in_=ext["carrag"][:, :]), 16)
                    LHB.inc(sy.dma_start(out=csel_sb[:], in_=ext["carry_sel"][:, :]), 16)
                p.sync(shb)
                lhb_c = LHB.n

                def sdp(sc_, lhb_c=lhb_c):
                    sc_.wait_ge(LHB.h, lhb_c)
                    sc_.activation(out=dsg[:], in_=dsg[:], func=AF.Sigmoid)
                    AC.inc(sc_.activation(out=dsg[:], in_=dsg[:], func=AF.Ln))
                p.sca(sdp)
                ac_ln = AC.n

                def gdp(g, ac_ln=ac_ln):
                    g.wait_ge(AC.h, ac_ln)
                    PL.inc(g.iota(out=dpow[:], pattern=[[1, TOK]], base=1, channel_multiplier=0,
                                  allow_small_or_imprecise_dtypes=True))
                p.gps(gdp)
                pl_dp = PL.n

                def vdp(v, pl_dp=pl_dp):
                    v.wait_ge(PL.h, pl_dp)
                    v.tensor_copy(out=gapt[:], in_=gapt[:])
                    VE.inc(v.tensor_scalar(out=dpow[:], in0=dpow[:], scalar1=dsg[0:1, :],
                                           scalar2=None, op0=ALU.mult))
                p.vec(vdp)
                ve_dp = VE.n

                def sdp2(sc_, ve_dp=ve_dp):
                    sc_.wait_ge(VE.h, ve_dp)
                    AC.inc(sc_.activation(out=dpow[:], in_=dpow[:], func=AF.Exp))
                p.sca(sdp2)
                ac_dp = AC.n

                def tco(t, ac_dp=ac_dp, lhb_c=lhb_c):
                    t.wait_ge(AC.h, ac_dp)
                    t.wait_ge(LHB.h, lhb_c)
                    PE.inc(t.matmul(out=psA[0:1, 0:512], lhsT=csel_sb[:], rhs=carr_sb[:, 0:512],
                                    start=True, stop=True))
                    PE.inc(t.matmul(out=psB[0:1, 0:512], lhsT=csel_sb[:], rhs=carr_sb[:, 512:1024],
                                    start=True, stop=True))
                p.ten(tco)
                pe_co = PE.n

                def vco(v, pe_co=pe_co):
                    v.wait_ge(PE.h, pe_co)
                    v.tensor_copy(out=cown[0:1, 0:512], in_=psA[0:1, 0:512])
                    VE.inc(v.tensor_copy(out=cown[0:1, 512:1024], in_=psB[0:1, 0:512]))
                p.vec(vco)
                ve_co2 = VE.n

                if _DEBUG_OUT:
                    dbg_f = nc.dram_tensor("dbg_f", [9, 4096], F32, kind="ExternalOutput").ap()
                    SDB = p.S("SDB")

                    def sdbg(sy, ve_co2=ve_co2, ac_dp=ac_dp, lhb_c=lhb_c):
                        sy.wait_ge(VE.h, ve_co2)
                        sy.wait_ge(AC.h, ac_dp)
                        sy.wait_ge(LHB.h, lhb_c)
                        SDB.inc(sy.dma_start(out=dbg_f[0:1, 0:TOK], in_=dpow[:]), 16)
                        SDB.inc(sy.dma_start(out=dbg_f[0:1, 2048:3072], in_=cown[:]), 16)
                        SDB.inc(sy.dma_start(out=dbg_f[0:1, 3072:3073], in_=dsg[:]), 16)
                        SDB.inc(sy.dma_start(out=dbg_f[1:9, 0:1024], in_=carr_sb[:]), 16)
                        SDB.inc(sy.dma_start(out=dbg_f[1:9, 1024:1025], in_=csel_sb[:],
                                             allow_slow_non_contiguous=True), 16)
                    p.sync(sdbg)

                cview = cown.ap().rearrange("o (pp jj) -> o jj pp", jj=8)
                for j in range(8):
                    for qc in range(4):
                        bank = [psA, psB][qc % 2]

                        def tfx(t, j=j, qc=qc, bank=bank, ve_co2=ve_co2, m=VE.n):
                            t.wait_ge(VE.h, ve_co2)
                            t.wait_ge(VE.h, m)
                            PE.inc(t.matmul(out=bank[:, 0:512], lhsT=cview[:, j, :],
                                            rhs=dpow[0:1, qc * 512:(qc + 1) * 512],
                                            start=True, stop=True))
                        p.ten(tfx)
                        pe_fx = PE.n

                        def vfx(v, j=j, qc=qc, bank=bank, pe_fx=pe_fx):
                            v.wait_ge(PE.h, pe_fx)
                            VE.inc(v.tensor_tensor(out=hb[:, j, qc * 512:(qc + 1) * 512],
                                                   in0=hb[:, j, qc * 512:(qc + 1) * 512],
                                                   in1=bank[:, 0:512], op=ALU.add))
                        p.vec(vfx)
                hm = [VE.n, VE.n]

            xm = [0, 0]
            for g in range(16):
                rp = g % 2

                def sx(sy, g=g, rp=rp, m=xm[rp]):
                    sy.wait_ge(VE.h, m)
                    LX[rp].inc(sy.dma_start(out=xa[:, rp, :],
                                            in_=x1_d[g * 128:(g + 1) * 128, :]), 16)
                    LY[rp].inc(sy.dma_start(out=xb[:, rp, :],
                                            in_=xn1_d[g * 128:(g + 1) * 128, :]), 16)
                p.sync(sx)
                lx, ly = LX[rp].n, LY[rp].n

                def vx(v, g=g, rp=rp, lx=lx, ly=ly):
                    v.wait_ge(LX[rp].h, lx)
                    v.wait_ge(LY[rp].h, ly)
                    VE.inc(v.scalar_tensor_tensor(out=res[:, g, :], in0=xb[:, rp, :],
                                                  scalar=2.0, in1=xa[:, rp, :],
                                                  op0=ALU.mult, op1=ALU.add))
                p.vec(vx)
                xm[rp] = VE.n
            ve_res = VE.n

            for tc in range(16):
                for dh in range(2):
                    bank = [psA, psB][dh]

                    def tr(t, tc=tc, dh=dh, bank=bank, ve_res=ve_res, lg_c=lg_c,
                           hm=hm, m=VE.n):
                        t.wait_ge(VE.h, ve_res)
                        t.wait_ge(VE.h, hm[0])
                        t.wait_ge(VE.h, hm[1])
                        t.wait_ge(LG.h, lg_c)
                        t.wait_ge(VE.h, m)
                        for i in range(8):
                            PE.inc(t.matmul(out=bank[:, 0:512],
                                            lhsT=hb[:, i, tc * 128:(tc + 1) * 128],
                                            rhs=mu_s[:, i, dh * 512:(dh + 1) * 512],
                                            start=(i == 0), stop=(i == 7)))
                    p.ten(tr)
                    pe_r = PE.n

                    def vr(v, tc=tc, dh=dh, bank=bank, pe_r=pe_r, ve_g=ve_g):
                        v.wait_ge(PE.h, pe_r)
                        v.wait_ge(VE.h, ve_g)
                        VE.inc(v.scalar_tensor_tensor(
                            out=res[:, tc, dh * 512:(dh + 1) * 512], in0=bank[:, 0:512],
                            scalar=gtrep[:, :], in1=res[:, tc, dh * 512:(dh + 1) * 512],
                            op0=ALU.mult, op1=ALU.add))
                    p.vec(vr)

            cm = [0, 0]
            if not fused:
                for r in range(NRND):
                    for g in range(16):
                        rp = (r * 16 + g) % 2

                        def scr_(sy, r=r, g=g, rp=rp, m=cm[rp]):
                            sy.wait_ge(VE.h, m)
                            LC[rp].inc(sy.dma_start(out=crt[:, rp, :],
                                                    in_=cr_d[r, g * 128:(g + 1) * 128, :]), 16)
                        p.sync(scr_)
                        lc = LC[rp].n

                        def vcr(v, r=r, g=g, rp=rp, lc=lc):
                            v.wait_ge(LC[rp].h, lc)
                            VE.inc(v.tensor_tensor(out=res[:, g, :], in0=res[:, g, :],
                                                   in1=crt[:, rp, :], op=ALU.add))
                        p.vec(vcr)
                        cm[rp] = VE.n
            else:
                CC = p.CC

                def grs(g):
                    CC.inc(g.collective_compute(
                        "ReduceScatter", ALU.add, replica_groups=[list(range(NC))],
                        ins=[ext["contribZ"].opt()], outs=[ext["contribOwn"].opt()]))
                p.gps(grs)
                cc_rs = CC.n
                for g in range(16):
                    rp = g % 2

                    def scr_(sy, g=g, rp=rp, m=cm[rp], cc_rs=cc_rs):
                        sy.wait_ge(p.CC.h, cc_rs)
                        sy.wait_ge(VE.h, m)
                        LC[rp].inc(sy.dma_start(out=crt[:, rp, :],
                                                in_=ext["contribOwn"][g * 128:(g + 1) * 128, :]),
                                   16)
                    p.sync(scr_)
                    lc = LC[rp].n

                    def vcr(v, g=g, rp=rp, lc=lc):
                        v.wait_ge(LC[rp].h, lc)
                        VE.inc(v.tensor_tensor(out=res[:, g, :], in0=res[:, g, :],
                                               in1=crt[:, rp, :], op=ALU.add))
                    p.vec(vcr)
                    cm[rp] = VE.n

            for g in range(16):
                def so(sy, g=g, m=VE.n):
                    sy.wait_ge(VE.h, m)
                    SX.inc(sy.dma_start(out=x2_o[g * 128:(g + 1) * 128, :], in_=res[:, g, :]), 16)
                p.sync(so)

            def fin(sy):
                sy.wait_ge(SX.h, SX.n)
                DONE.inc(sy.nop())
            p.sync(fin)

    return _lower_ext(nc) if standalone else (nc, (DONE.h, DONE.n))



_CACHE = {}

def _get(name, builder):
    if name not in _CACHE:
        _CACHE[name] = builder()
    return _CACHE[name]


def kernel_multi(**inp):
    f32 = lambda a: np.ascontiguousarray(np.asarray(a, np.float32))
    x = f32(inp["L0_attn_x"])
    i = np.arange(D)
    bind_key = f32(inp["L1_attn_bind_key"])
    unbind_key = f32(inp["L1_attn_unbind_key"])
    Mb = np.ascontiguousarray(bind_key[(i[None, :] - i[:, None]) % D])
    Mu = np.ascontiguousarray(unbind_key[(i[:, None] - i[None, :]) % D])

    ncA = _get("A", build_A)
    mapsA = _prep_A_inputs(x, inp["L0_attn_Wq"], inp["L0_attn_Wk"], inp["L0_attn_Wv"],
                           inp["L0_ffn_ln_w"], inp["L0_ffn_ln_b"])
    rA = run_bass_kernel_spmd(ncA, mapsA, core_ids=list(range(NC))).results

    ncB = _get("B", build_B)
    mapsB = _prep_B_inputs([rA[c]["nm_own"] for c in range(NC)],
                           inp["L0_ffn_router_w"], inp["L0_ffn_gate_w"], inp["L0_ffn_up_w"],
                           inp["L0_ffn_down_w"], inp["L0_ffn_sh_gate"], inp["L0_ffn_sh_up"],
                           inp["L0_ffn_sh_down"])
    rB = run_bass_kernel_spmd(ncB, mapsB, core_ids=list(range(NC))).results

    ncC = _get("C", build_C)
    gw1T = np.ascontiguousarray(f32(inp["L1_ffn_gate_w"]).T)
    lgam = f32(inp["L1_ffn_ln_gamma"])
    lbet = f32(inp["L1_ffn_ln_beta"])
    xf = x.reshape(NTOK, D)
    mapsC = []
    for c in range(NC):
        sl = slice(c * TOK, (c + 1) * TOK)
        mapsC.append({"x_own": np.ascontiguousarray(xf[sl]), "xn_own": rA[c]["xn_own"],
                      "att_own": rA[c]["att_own"], "moe_own": rB[c]["moe_own"],
                      "M_bind": Mb, "gw1T": gw1T, "lgam": lgam, "lbeta": lbet})
    rC = run_bass_kernel_spmd(ncC, mapsC, core_ids=list(range(NC))).results

    ncD = _get("D", build_D)
    decay = f32(inp["L1_attn_decay"]).reshape(1, 1)
    btf = [np.concatenate([rC[2 * b]["boundT_own"], rC[2 * b + 1]["boundT_own"]], axis=1)
           for b in range(4)]
    units = [(b, j) for b in range(4) for j in range(8)]
    mapsD = []
    for c in range(NC):
        us_ = units[c::NC]
        bu = np.stack([btf[b][j * 128:(j + 1) * 128, :] for (b, j) in us_])
        mapsD.append({"bound_units": np.ascontiguousarray(bu), "decay": decay})
    rD = run_bass_kernel_spmd(ncD, mapsD, core_ids=list(range(NC))).results

    ncE = _get("E", build_E)
    z1_all = np.ascontiguousarray(np.concatenate([rC[c]["z1_own"] for c in range(NC)], axis=0))
    sraw_all = np.ascontiguousarray(np.concatenate([rC[c]["sraw_own"] for c in range(NC)], axis=0))
    fc1s = f32(inp["L1_ffn_fc1s"]); b1s = f32(inp["L1_ffn_b1s"])
    fc2s = f32(inp["L1_ffn_fc2s"]); b2s = f32(inp["L1_ffn_b2s"])
    usmat = np.triu(np.ones((128, 128), np.float32), 1)
    mapsE = []
    for e in range(NC):
        mapsE.append({"z1_all": z1_all, "scol": np.ascontiguousarray(sraw_all[:, e:e + 1]),
                      "fc1T": np.ascontiguousarray(fc1s[e].T),
                      "b1": np.ascontiguousarray(b1s[e].reshape(HID1 // 128, 128).T),
                      "fc2T": np.ascontiguousarray(fc2s[e].T), "b2": b2s[e], "Ustrict": usmat})
    rE = run_bass_kernel_spmd(ncE, mapsE, core_ids=list(range(NC))).results

    crounds = [np.zeros((NRND, TOK, D), dtype=z1_all.dtype) for _ in range(NC)]
    occ = np.zeros(NTOK, np.int64)
    for e in range(NC):
        tsel = rE[e]["tsel"]
        tglob = np.zeros(K1, np.int64)
        for g in range(16):
            tglob[g * 128 + np.arange(128)] = tsel[:, g]
        rows = rE[e]["opack"]
        rr = occ[tglob].copy()
        occ[tglob] += 1
        owners = tglob // TOK
        tl = tglob % TOK
        for o in range(NC):
            m = owners == o
            crounds[o][rr[m], tl[m], :] = rows[m]

    ncF = _get("F", build_F)
    gate = f32(inp["L1_attn_gate"]).reshape(1, 1)
    hof = [np.zeros((D, S), np.float32) for _ in range(4)]
    for c in range(NC):
        us_ = units[c::NC]
        hu = rD[c]["holo_units"]
        for k, (b, j) in enumerate(us_):
            hof[b][j * 128:(j + 1) * 128, :] = hu[k]
    mapsF = []
    for c in range(NC):
        b, h = c // 2, c % 2
        mapsF.append({"x1_own": rC[c]["x1_own"], "xn1_own": rC[c]["xn1_own"],
                      "holoT_own": np.ascontiguousarray(hof[b][:, h * TOK:(h + 1) * TOK]),
                      "M_unbind": Mu, "gate": gate, "crounds": crounds[c]})
    rF = run_bass_kernel_spmd(ncF, mapsF, core_ids=list(range(NC))).results

    out = np.concatenate([rF[c]["x2_own"] for c in range(NC)], axis=0)
    return out.reshape(B, S, D)



_DEBUG_OUT = False
_SES_KEEP = []


def build_full():
    nc = bass.Bass("TRN2", target_bir_lowering=False, detect_race_conditions=False)
    ext = {}
    ses = ExitStack()
    ext["_P"] = P(nc, ses, None, pfx="G_")
    _SES_KEEP.append(ses)
    EI = lambda n, sh, dt: nc.dram_tensor(n, sh, dt, kind="ExternalInput").ap()
    _dbg = {"xn_own", "nm_own", "att_own", "moe_own", "x1_own", "xn1_own",
            "holoT_bf", "scolE"}
    IN = lambda n, sh, dt: (
        nc.dram_tensor(n, sh, dt, kind="ExternalOutput").ap()
        if (_DEBUG_OUT and n in _dbg) else nc.dram_tensor(n, sh, dt).ap())
    for n, sh, dt in [
        ("x_own", [TOK, D], F32), ("x_other", [TOK, D], F32),
        ("WqT", [D, D], F32), ("WkT", [D, DK], F32), ("WvT", [D, DK], F32),
        ("ln_w", [D], F32), ("ln_b", [D], F32),
        ("router_w", [D, E], F32), ("gate_w", [E, D, I0], F32), ("up_w", [E, D, I0], F32),
        ("down_w", [E, I0, D], F32), ("sh_gate", [D, I0], F32), ("sh_up", [D, I0], F32),
        ("sh_down", [I0, D], F32), ("Ustrict", [128, 128], F32), ("blkstrict", [128, 128], F32),
        ("ecrow", [1, 128], F32),
        ("M_bind", [D, D], F32), ("gw1T", [D, E], F32), ("lgam", [D], F32), ("lbeta", [D], F32),
        ("decay", [1, 1], F32),
        ("fc1T", [D, HID1], F32), ("b1", [128, HID1 // 128], F32), ("fc2T", [HID1, D], F32),
        ("b2", [D], F32), ("esel8", [128, 8], F32), ("esel_tiled", [128, 1024], F32),
        ("carry_sel", [8, 1], F32), ("M_unbind", [D, D], F32), ("gate", [1, 1], F32),
    ]:
        ext[n] = EI(n, sh, dt)
    ext["x2_own"] = nc.dram_tensor("x2_own", [TOK, D], F32, kind="ExternalOutput").ap()
    for n, sh, dt in [
        ("xn_own", [TOK, D], F32), ("nm_own", [TOK, D], F32), ("att_own", [TOK, D], F32),
        ("moe_own", [TOK, D], F32), ("x1_own", [TOK, D], F32), ("xn1_own", [TOK, D], F32),
        ("sraw_own", [TOK, E], F32), ("z1_own", [TOK, D], BF16),
        ("holoT_bf", [D, TOK], BF16), ("carr_my", [1, 1024], F32),
        ("contribZ", [NTOK, D], BF16), ("contribOwn", [TOK, D], BF16),
        ("scolE", [NTOK, 1], F32),
    ]:
        ext[n] = IN(n, sh, dt)
    for n, sh, dt in [("carrag", [8, 1024], F32), ("z1ag", [NTOK, D], BF16),
                      ("srawag", [NTOK, E], F32)]:
        ext[n] = nc.dram_tensor(n, sh, dt, addr_space="Shared").ap()
    ext["z1_all"] = ext["z1ag"]

    _, dA = build_A(nc, ext, entry=())
    _, dB = build_B(nc, ext, entry=[dA])
    _, dC = build_C(nc, ext, entry=[dA, dB], fused=True)
    _, dE = build_E(nc, ext, entry=[dC], fused=True)
    _, dF = build_F(nc, ext, entry=[dC, dE], fused=True)
    return _lower_ext(nc)


def kernel(**inp):
    f32 = lambda a: np.ascontiguousarray(np.asarray(a, np.float32))
    x = f32(inp["L0_attn_x"])
    i = np.arange(D)
    bind_key = f32(inp["L1_attn_bind_key"])
    unbind_key = f32(inp["L1_attn_unbind_key"])
    Mb = np.ascontiguousarray(bind_key[(i[None, :] - i[:, None]) % D])
    Mu = np.ascontiguousarray(unbind_key[(i[:, None] - i[None, :]) % D])
    WqT = np.ascontiguousarray(f32(inp["L0_attn_Wq"]).T)
    WkT = np.ascontiguousarray(f32(inp["L0_attn_Wk"]).T)
    WvT = np.ascontiguousarray(f32(inp["L0_attn_Wv"]).T)
    usmat = np.triu(np.ones((128, 128), np.float32), 1)
    gi, ei = np.divmod(np.arange(128), 8)
    bsmat = ((ei[:, None] == ei[None, :]) & (gi[:, None] < gi[None, :])).astype(np.float32)
    ecr = (ei * C0).astype(np.float32)[None, :]
    gw1T = np.ascontiguousarray(f32(inp["L1_ffn_gate_w"]).T)
    fc1s = f32(inp["L1_ffn_fc1s"]); b1s = f32(inp["L1_ffn_b1s"])
    fc2s = f32(inp["L1_ffn_fc2s"]); b2s = f32(inp["L1_ffn_b2s"])
    decay = f32(inp["L1_attn_decay"]).reshape(1, 1)
    gate = f32(inp["L1_attn_gate"]).reshape(1, 1)
    xf = x.reshape(NTOK, D)

    nc = _get("FULL", build_full)
    maps = []
    for c in range(NC):
        b, h = c // 2, c % 2
        oh = np.zeros(8, np.float32); oh[c] = 1.0
        csel = np.zeros((8, 1), np.float32)
        if h == 1:
            csel[c - 1, 0] = 1.0
        maps.append({
            "x_own": np.ascontiguousarray(x[b, h * TOK:(h + 1) * TOK, :]),
            "x_other": np.ascontiguousarray(x[b, (1 - h) * TOK:(2 - h) * TOK, :]),
            "WqT": WqT, "WkT": WkT, "WvT": WvT,
            "ln_w": f32(inp["L0_ffn_ln_w"]), "ln_b": f32(inp["L0_ffn_ln_b"]),
            "router_w": f32(inp["L0_ffn_router_w"]), "gate_w": f32(inp["L0_ffn_gate_w"]),
            "up_w": f32(inp["L0_ffn_up_w"]), "down_w": f32(inp["L0_ffn_down_w"]),
            "sh_gate": f32(inp["L0_ffn_sh_gate"]), "sh_up": f32(inp["L0_ffn_sh_up"]),
            "sh_down": f32(inp["L0_ffn_sh_down"]), "Ustrict": usmat, "blkstrict": bsmat,
            "ecrow": ecr,
            "M_bind": Mb, "gw1T": gw1T, "lgam": f32(inp["L1_ffn_ln_gamma"]),
            "lbeta": f32(inp["L1_ffn_ln_beta"]), "decay": decay,
            "fc1T": np.ascontiguousarray(fc1s[c].T),
            "b1": np.ascontiguousarray(b1s[c].reshape(HID1 // 128, 128).T),
            "fc2T": np.ascontiguousarray(fc2s[c].T), "b2": b2s[c],
            "esel8": np.ascontiguousarray(np.tile(oh, (128, 1))),
            "esel_tiled": np.ascontiguousarray(np.tile(oh, (128, 128))),
            "carry_sel": csel, "M_unbind": Mu, "gate": gate,
        })
    res = run_bass_kernel_spmd(nc, maps, core_ids=list(range(NC))).results
    out = np.concatenate([res[c]["x2_own"] for c in range(NC)], axis=0)
    return out.reshape(B, S, D)
